# version 04
# baseline (speedup 1.0000x reference)
import numpy as np
import concourse.bass as bass
import concourse.mybir as mybir
from concourse.bass_utils import run_bass_kernel_spmd

F32 = mybir.dt.float32
BF16 = mybir.dt.bfloat16
I32 = mybir.dt.int32
AF = mybir.ActivationFunctionType
ALU = mybir.AluOpType
AX = mybir.AxisListType

NCORES = 8
T = 16384
TC = T // NCORES
D = 1024
NDC = 8
PI = float(np.pi)
TWO_PI = float(2 * np.pi)
HALF_PI = float(np.pi / 2)
LN_EPS = 1e-5
BIG_NEG = -1.0e30

ALL_PHASES = ("P1", "P2", "P3", "P4", "P5")
P4_STOP = 99


class Prog:
    QUEUES = ("qs", "qg", "qa")

    def __init__(self):
        self.steps = []
        self.region_w = {}
        self.region_r = {}

    def step(self, eng, fn, reads=(), writes=(), dma=None, cc=False, deps=()):
        dep = set(d for d in deps if d is not None)
        for rg in reads:
            w = self.region_w.get(rg)
            if w is not None:
                dep.add(w)
        for rg in writes:
            w = self.region_w.get(rg)
            if w is not None:
                dep.add(w)
            dep.update(self.region_r.get(rg, ()))
        sid = len(self.steps)
        counter = "cc" if cc else (dma if dma is not None else eng)
        self.steps.append([eng, fn, counter, sorted(dep)])
        for rg in reads:
            self.region_r.setdefault(rg, set()).add(sid)
        for rg in writes:
            self.region_w[rg] = sid
            self.region_r[rg] = set()
        return sid

    def fence(self, regions):
        out = set()
        for rg in regions:
            w = self.region_w.get(rg)
            if w is not None:
                out.add(w)
            out.update(self.region_r.get(rg, ()))
        return sorted(out)

    def emit(self, nc, block, sems):
        steps = self.steps
        needed = set()
        for st in steps:
            needed.update(st[3])
        ordinal = [None] * len(steps)
        counts = {}
        for sid, st in enumerate(steps):
            c = st[2]
            if sid in needed or c == "cc" or c in self.QUEUES:
                ordinal[sid] = counts.get(c, 0)
                counts[c] = ordinal[sid] + 1
        totals = {c: k * (16 if c in self.QUEUES else 1) for c, k in counts.items()}

        def wait_of(sid):
            c = steps[sid][2]
            v = ordinal[sid] + 1
            return c, (v * 16 if c in self.QUEUES else v)

        def emit_engine(eng_name, eng_obj):
            last = {}
            for sid, st in enumerate(steps):
                issuer, fn, counter, deps = st
                if issuer != eng_name:
                    continue
                for d in deps:
                    dc_, dv = wait_of(d)
                    if (eng_name == "tensor" and dc_ == "tensor"
                            and steps[d][2] == "tensor"):
                        continue
                    if last.get(dc_, 0) < dv:
                        eng_obj.wait_ge(sems[dc_], dv)
                        last[dc_] = dv
                inst = fn(eng_obj)
                if ordinal[sid] is not None:
                    inst.then_inc(sems[counter], 16 if counter in self.QUEUES else 1)

        @block.sync
        def _(sync):
            emit_engine("sync", sync)
            for c in sorted(totals):
                if c != "sync" and totals[c]:
                    sync.wait_ge(sems[c], totals[c])

        @block.scalar
        def _(scalar):
            emit_engine("scalar", scalar)

        @block.vector
        def _(vector):
            emit_engine("vector", vector)

        @block.tensor
        def _(tensor):
            emit_engine("tensor", tensor)

        @block.gpsimd
        def _(gpsimd):
            emit_engine("gpsimd", gpsimd)


def build_nc(phases=ALL_PHASES, dbg=False):
    import contextlib
    nc = bass.Bass("TRN2", target_bir_lowering=False)
    dram = {}

    def din(name, shape, dtype=F32):
        dram[name] = nc.dram_tensor(name, shape, dtype, kind="ExternalInput").ap()
        return dram[name]

    xT_h = din("xT", [D, TC])
    xhead_h = din("xhead", [1, D])
    wel_h = din("wel", [NDC, 128, 24])
    bel_h = din("bel", [24, 1])
    wle_h = din("wle", [24, D])
    ble_h = din("ble", [128, 8])
    ht_h = din("htm", [24, 12])
    wkb_h = din("wkb", [NDC, 128, D])
    wkab_h = din("wkab", [NDC, 128, D])
    wvb_h = din("wvb", [NDC, 128, D])
    battn_h = din("battn", [128, 40])
    rwb_h = din("rwb", [NDC, 128, 8])
    b1_h = din("b1v", [128, 8 * 32])
    b2l_h = din("b2l", [8, D])
    sel_h = din("sel", [8, 8 * 128])
    id128_h = din("id128", [128, 128])
    w1_h = din("w1b", [8, 32, 128, D])
    w2_h = din("w2b", [8, 32, 128, D])
    mexc_h = din("mexc", [128, 8])
    lx_h = din("lxc", [1, 8])

    yT_h = nc.dram_tensor("yT", [D, TC], F32, kind="ExternalOutput").ap()

    w1s_h = nc.dram_tensor("w1s", [8, 32, 128, 2048], BF16).ap()
    w2s_h = nc.dram_tensor("w2s", [8, 32, 128, 2048], BF16).ap()
    ew_h = nc.dram_tensor("ew_spill", [NDC, 128, TC], F32,
                          kind=("ExternalOutput" if dbg else "Internal")).ap()
    cc1i_h = nc.dram_tensor("cc1i", [1, 8], F32).ap()
    cc1o_h = nc.dram_tensor("cc1o", [8, 8], F32).ap()
    cc2i_h = nc.dram_tensor("cc2i", [128, 8], F32).ap()
    cc2o_h = nc.dram_tensor("cc2o", [8, 128, 8], F32).ap()
    cc3i_h = nc.dram_tensor("cc3i", [128, 8], F32).ap()
    cc3o_h = nc.dram_tensor("cc3o", [8, 128, 8], F32).ap()

    es = contextlib.ExitStack()

    def sb(name, shape, dtype=F32):
        return es.enter_context(nc.sbuf_tensor(name, shape, dtype))

    x_sb = sb("x_sb", [128, NDC * TC])
    bigA = sb("bigA", [128, 16384])
    bigB = sb("bigB", [128, 13824])
    wk_sb = sb("wk_sb", [128, 2048])
    wel_sb = sb("wel_sb", [128, NDC * 24])
    ht_sb = sb("ht_sb", [24, 12])
    b1_sb = sb("b1_sb", [128, 8 * 32])
    b2l_sb = sb("b2l_sb", [8, D])
    sel_sb = sb("sel_sb", [8, 8 * 128])
    id128_sb = sb("id128_sb", [128, 128])
    rl_sb = sb("rl_sb", [8, 512])
    gates_sb = sb("gates_sb", [8, 512])
    gtm_sb = sb("gtm_sb", [128, 64])
    battn_sb = sb("battn_sb", [128, 48])
    bias_sb = sb("bias_sb", [128, 48])
    rwb_sb = sb("rwb_sb", [128, NDC * 8])
    rws_sb = sb("rws_sb", [128, 64])
    mexc_sb = sb("mexc_sb", [128, 8])
    tot_sb = sb("tot_sb", [128, 40])
    mm_sb = sb("mm_sb", [128, 8])
    ones128 = sb("ones128", [128, 1])
    onesr = sb("onesr", [1, 128])
    ones12 = sb("ones12", [12, 1])
    kid_i = sb("kid_i", [128, 8], I32)
    gath_sb = sb("gath_sb", [128, 64])
    g1_sb = sb("g1_sb", [1, 64])

    def psmk(name, shape):
        return es.enter_context(nc.psum_tensor(name, shape, F32))

    psA = psmk("psA", [128, 512])
    psB = psmk("psB", [128, 512])
    psU0 = psmk("psU0", [128, 512])
    psU1 = psmk("psU1", [128, 512])
    psD0 = psmk("psD0", [128, 512])
    psD1 = psmk("psD1", [128, 512])
    psC = psmk("psC", [128, 1024])

    sems = {}
    for cname in ("sync", "scalar", "vector", "tensor", "gpsimd", "qs", "qg", "qa", "cc"):
        sems[cname] = es.enter_context(nc.semaphore("sem_" + cname))

    block = es.enter_context(nc.Block())
    p = Prog()

    def xc(dc, t0, tn):
        return x_sb[:, dc * TC + t0: dc * TC + t0 + tn]

    def dmas(fn, reads=(), writes=(), deps=()):
        return p.step("sync", fn, reads=reads, writes=writes, dma="qs", deps=deps)

    def dmag(fn, reads=(), writes=(), deps=()):
        return p.step("gpsimd", fn, reads=reads, writes=writes, dma="qg", deps=deps)

    def dmaa(fn, reads=(), writes=(), deps=()):
        return p.step("scalar", fn, reads=reads, writes=writes, dma="qa", deps=deps)

    def V(fn, reads=(), writes=(), deps=()):
        return p.step("vector", fn, reads=reads, writes=writes, deps=deps)

    def S(fn, reads=(), writes=(), deps=()):
        return p.step("scalar", fn, reads=reads, writes=writes, deps=deps)

    def TT(fn, reads=(), writes=(), deps=()):
        return p.step("tensor", fn, reads=reads, writes=writes, deps=deps)

    def G(fn, reads=(), writes=(), deps=()):
        return p.step("gpsimd", fn, reads=reads, writes=writes, deps=deps)

    dmas(lambda s: s.dma_start(
        out=x_sb[:].rearrange("p (c t) -> p c t", c=NDC),
        in_=xT_h.rearrange("(c p) t -> c p t", p=128).transpose([1, 0, 2])),
        writes=[("x", dc) for dc in range(NDC)])
    dmas(lambda s: s.dma_start(
        out=wel_sb[:].rearrange("p (c m) -> p c m", c=NDC),
        in_=wel_h.transpose([1, 0, 2])), writes=["wel"])
    dmas(lambda s: s.dma_start(out=bigB[0:24, 8192:9216], in_=wle_h), writes=["wle"])
    dmas(lambda s: s.dma_start(out=ht_sb[:], in_=ht_h), writes=["htm"])
    dmas(lambda s: s.dma_start(out=battn_sb[:, 0:40], in_=battn_h), writes=["battn"])
    dmas(lambda s: s.dma_start(out=b1_sb[:], in_=b1_h), writes=["b1"])
    dmas(lambda s: s.dma_start(out=b2l_sb[:], in_=b2l_h), writes=["b2l"])
    dmas(lambda s: s.dma_start(out=sel_sb[:], in_=sel_h), writes=["sel"])
    dmas(lambda s: s.dma_start(out=id128_sb[:], in_=id128_h), writes=["id128"])
    dmas(lambda s: s.dma_start(
        out=rwb_sb[:].rearrange("p (c e) -> p c e", c=NDC),
        in_=rwb_h.transpose([1, 0, 2])), writes=["rwb"])
    dmas(lambda s: s.dma_start(out=mexc_sb[:], in_=mexc_h), writes=["mexc"])
    dmas(lambda s: s.dma_start(out=bias_sb[:, 0:8], in_=ble_h), writes=["ble"])
    dmas(lambda s: s.dma_start(out=g1_sb[0:1, 32:40], in_=lx_h), writes=["lxrow"])
    dmas(lambda s: s.dma_start(out=bias_sb[0:24, 32:33], in_=bel_h), writes=["bel"])

    V(lambda v: v.memset(ones128[:], 1.0), writes=["ones128"])
    V(lambda v: v.memset(onesr[:], 1.0), writes=["onesr"])
    V(lambda v: v.memset(ones12[:], 1.0), writes=["ones12"])
    V(lambda v: v.memset(bias_sb[:, 24:25], HALF_PI), writes=["hpi"])
    V(lambda v: v.tensor_scalar(bias_sb[0:24, 33:34], bias_sb[0:24, 32:33], -1.0, None, ALU.mult),
      reads=["bel"], writes=["negbel"])
    V(lambda v: v.tensor_scalar(battn_sb[:, 40:48], battn_sb[:, 8:16], float(1.0 / 32.0), None, ALU.mult),
      reads=["battn"], writes=["bka32"])
    G(lambda g: g.iota(kid_i[:, :], [[128, 8]], channel_multiplier=1), writes=["kid"])
    rws_bf = rws_sb[:].bitcast(BF16)
    V(lambda v: v.tensor_copy(rws_bf[:, 0:64], rwb_sb[:]), reads=["rwb"], writes=["rws"])
    V(lambda v: v.tensor_sub(rws_bf[:, 64:128], rwb_sb[:], rws_bf[:, 0:64]),
      reads=["rwb", "rws"], writes=["rws"])

    def emit_ln(tn, src_ap, dst_ap, sq_ap, rowb, src_regions, dst_regions,
                sq_region, row_region, post_chunk=None):
        for dcx in range(NDC):
            TT(lambda t, dcx=dcx: t.matmul(psA[0:1, 0:tn], ones128[:], src_ap(dcx),
                                           start=(dcx == 0), stop=(dcx == NDC - 1)),
               reads=[src_regions[dcx], "ones128"], writes=["psA"])
        sqv = sq_ap[:, 0:tn]
        for dcx in range(NDC):
            S(lambda s, dcx=dcx: s.activation(sqv, src_ap(dcx), AF.Square),
              reads=[src_regions[dcx]], writes=[sq_region])
            TT(lambda t, dcx=dcx: t.matmul(psB[0:1, 0:tn], ones128[:], sqv,
                                           start=(dcx == 0), stop=(dcx == NDC - 1)),
               reads=[sq_region, "ones128"], writes=["psB"])
        V(lambda v: v.tensor_scalar(rowb(0), psA[0:1, 0:tn], float(1.0 / D), None, ALU.mult),
          reads=["psA"], writes=[row_region])
        V(lambda v: v.tensor_scalar(rowb(1), psB[0:1, 0:tn], float(1.0 / D), None, ALU.mult),
          reads=["psB", row_region], writes=[row_region])
        V(lambda v: v.tensor_mul(rowb(2), rowb(0), rowb(0)),
          reads=[row_region], writes=[row_region])
        V(lambda v: v.tensor_sub(rowb(1), rowb(1), rowb(2)),
          reads=[row_region], writes=[row_region])
        V(lambda v: v.tensor_scalar(rowb(1), rowb(1), LN_EPS, None, ALU.add),
          reads=[row_region], writes=[row_region])
        V(lambda v: v.reciprocal(rowb(2), rowb(1)),
          reads=[row_region], writes=[row_region])
        S(lambda s: s.activation(rowb(1), rowb(2), AF.Sqrt),
          reads=[row_region], writes=[row_region])
        TT(lambda t: t.matmul(psD0[:, 0:tn], onesr[:], rowb(0), start=True, stop=True),
           reads=[row_region, "onesr"], writes=["psD0"])
        TT(lambda t: t.matmul(psD1[:, 0:tn], onesr[:], rowb(1), start=True, stop=True),
           reads=[row_region, "onesr"], writes=["psD1"])
        for dcx in range(NDC):
            V(lambda v, dcx=dcx: v.tensor_sub(dst_ap(dcx), src_ap(dcx), psD0[:, 0:tn]),
              reads=[src_regions[dcx], "psD0"], writes=[dst_regions[dcx]])
            V(lambda v, dcx=dcx: v.tensor_mul(dst_ap(dcx), dst_ap(dcx), psD1[:, 0:tn]),
              reads=[dst_regions[dcx], "psD1"], writes=[dst_regions[dcx]])
            if post_chunk is not None:
                post_chunk(dcx)

    NT512 = TC // 512
    rowslot = lambda i: bigA[0:1, 10240 + i * 512: 10240 + (i + 1) * 512]
    rowk_f = bigA[0:1, 12288:13312]
    rowi_i = bigA[0:1, 13312:14336].bitcast(I32)
    arow = lambda a, b: bigA[0:1, 14336 + a: 14336 + b]
    asc = lambda i: bigA[0:1, 15360 + i: 15360 + i + 1]
    fx1_row = bigA[0:1, 15488:16000]
    ubuf = lambda dcx: bigB[:, dcx * 512:(dcx + 1) * 512]
    wle_sb = bigB[0:24, 8192:9216]
    bits_sb = bigB[0:24, 9216:9728]
    syn_sb = bigB[0:12, 9728:10240]
    sqbuf = bigB[:, 4096:4608]
    dcmp = bigB[0:1, 4608:5632]
    fx2_row = bigB[0:1, 5632:6144]
    onetwob = bigB[0:12, 6144:6656]
    aoutb = bigB[:, 6656:7168]
    dsq = bigB[0:1, 7168:8192]

    if "P1" in phases:
        G(lambda g: g.iota(rowi_i, [[1, 1024]], channel_multiplier=0), writes=["rowi"])
        V(lambda v: v.tensor_copy(rowk_f, rowi_i), reads=["rowi"], writes=["rowk"])
        V(lambda v: v.memset(mm_sb[:, 0:2], BIG_NEG), writes=["mmacc"])
        for it in range(NT512):
            t0 = it * 512
            emit_ln(512, lambda dcx, t0=t0: xc(dcx, t0, 512), ubuf, sqbuf, rowslot,
                    [("x", dcx) for dcx in range(NDC)],
                    [("u", dcx) for dcx in range(NDC)], "sq", "lnrow")
            for dcx in range(NDC):
                V(lambda v, dcx=dcx: v.reduce_max(mm_sb[:, 2:3], ubuf(dcx), axis=AX.X),
                  reads=[("u", dcx)], writes=["mmtmp"])
                V(lambda v: v.tensor_max(mm_sb[:, 0:1], mm_sb[:, 0:1], mm_sb[:, 2:3]),
                  reads=["mmtmp", "mmacc"], writes=["mmacc"])
                V(lambda v, dcx=dcx: v.tensor_reduce(mm_sb[:, 2:3], ubuf(dcx), AX.X, ALU.min, negate=True),
                  reads=[("u", dcx)], writes=["mmtmp"])
                V(lambda v: v.tensor_max(mm_sb[:, 1:2], mm_sb[:, 1:2], mm_sb[:, 2:3]),
                  reads=["mmtmp", "mmacc"], writes=["mmacc"])
        TT(lambda t: t.transpose(psC[0:2, 0:128], mm_sb[:, 0:2], id128_sb[:]),
           reads=["mmacc", "id128"], writes=["psC"])
        V(lambda v: v.reduce_max(mm_sb[0:2, 4:5], psC[0:2, 0:128], axis=AX.X),
          reads=["psC"], writes=["mmred"])
        dmas(lambda s: s.dma_start(out=g1_sb[0:1, 0:2], in_=mm_sb[0:2, 4:5]),
             reads=["mmred"], writes=["e1row"])
        V(lambda v: v.memset(g1_sb[0:1, 2:8], 0.0), writes=["e1pad"])
        dmag(lambda g: g.dma_start(out=cc1i_h, in_=g1_sb[0:1, 0:8]),
             reads=["e1row", "e1pad"], writes=["cc1i"])
        p.step("gpsimd", lambda g: g.collective_compute(
            "AllGather", ALU.bypass, replica_groups=[list(range(NCORES))],
            ins=[cc1i_h.opt()], outs=[cc1o_h.opt()]),
            reads=["cc1i"], writes=["cc1o"], cc=True)
        dmag(lambda g: g.dma_start(out=gath_sb[0:2, 0:8], in_=cc1o_h.transpose([1, 0])[0:2], allow_slow_non_contiguous=True),
             reads=["cc1o"], writes=["e1g"])
        V(lambda v: v.reduce_max(gath_sb[0:2, 8:9], gath_sb[0:2, 0:8], axis=AX.X),
          reads=["e1g"], writes=["e1red"])
        dmas(lambda s: s.dma_start(out=g1_sb[0:1, 8:10], in_=gath_sb[0:2, 8:9]),
             reads=["e1red"], writes=["gmm"])

        dmas(lambda s: s.dma_start(out=arow(0, 1024), in_=xhead_h), writes=["xh"])
        V(lambda v: v.reduce_sum(asc(0), arow(0, 1024), axis=AX.X),
          reads=["xh"], writes=["dsc"])
        V(lambda v: v.tensor_scalar(asc(0), asc(0), float(1.0 / D), None, ALU.mult),
          reads=["dsc"], writes=["dsc"])
        S(lambda s: s.activation(dsq, arow(0, 1024), AF.Square),
          reads=["xh"], writes=["dsqr"])
        V(lambda v: v.reduce_sum(asc(1), dsq, axis=AX.X),
          reads=["dsqr", "dsc"], writes=["dsc"])
        V(lambda v: v.tensor_scalar(asc(1), asc(1), float(1.0 / D), None, ALU.mult),
          reads=["dsc"], writes=["dsc"])
        V(lambda v: v.tensor_mul(asc(2), asc(0), asc(0)), reads=["dsc"], writes=["dsc"])
        V(lambda v: v.tensor_sub(asc(1), asc(1), asc(2)), reads=["dsc"], writes=["dsc"])
        V(lambda v: v.tensor_scalar(asc(1), asc(1), LN_EPS, None, ALU.add),
          reads=["dsc"], writes=["dsc"])
        V(lambda v: v.reciprocal(asc(2), asc(1)), reads=["dsc"], writes=["dsc"])
        S(lambda s: s.activation(asc(3), asc(2), AF.Sqrt), reads=["dsc"], writes=["dsc"])
        V(lambda v: v.tensor_scalar(arow(0, 1024), arow(0, 1024), asc(0), None, ALU.subtract),
          reads=["xh", "dsc"], writes=["xh"])
        V(lambda v: v.tensor_scalar(arow(0, 1024), arow(0, 1024), asc(3), None, ALU.mult),
          reads=["xh", "dsc"], writes=["xh"])
        V(lambda v: v.tensor_add(asc(4), g1_sb[0:1, 8:9], g1_sb[0:1, 9:10]),
          reads=["gmm"], writes=["dsc"])
        V(lambda v: v.tensor_scalar(asc(4), asc(4), 1e-10, None, ALU.add),
          reads=["dsc"], writes=["dsc"])
        V(lambda v: v.reciprocal(asc(5), asc(4)), reads=["dsc"], writes=["dsc"])
        V(lambda v: v.tensor_scalar(arow(0, 1024), arow(0, 1024), g1_sb[0:1, 9:10], None, ALU.add),
          reads=["xh", "gmm"], writes=["xh"])
        V(lambda v: v.tensor_scalar(arow(0, 1024), arow(0, 1024), asc(5), None, ALU.mult),
          reads=["xh", "dsc"], writes=["xh"])
        for i, sc2 in enumerate((4, 16, 64, 256, 1024)):
            V(lambda v, sc2=sc2: v.reduce_sum(asc(6), arow(0, sc2), axis=AX.X),
              reads=["xh"], writes=["dsc"])
            V(lambda v, sc2=sc2: v.tensor_scalar(asc(6), asc(6), float(1.0 / sc2), None, ALU.mult),
              reads=["dsc"], writes=["dsc"])
            V(lambda v, sc2=sc2: v.tensor_scalar(dcmp[0:1, 0:sc2], arow(0, sc2), asc(6), None, ALU.is_gt),
              reads=["xh", "dsc"], writes=["dcmpr"])
            V(lambda v, sc2=sc2, i=i: v.reduce_sum(asc(16 + i), dcmp[0:1, 0:sc2], axis=AX.X),
              reads=["dcmpr"], writes=["dcnt"])
            V(lambda v, i=i: v.tensor_scalar(asc(16 + i), asc(16 + i), 1.0, None, ALU.max),
              reads=["dcnt"], writes=["dcnt"])
        S(lambda s: s.activation(bigA[0:1, 15376:15381], bigA[0:1, 15376:15381], AF.Ln),
          reads=["dcnt"], writes=["dly"])
        S(lambda s: s.activation(bigA[0:1, 15384:15389], g1_sb[0:1, 32:37], AF.Ln),
          reads=["lxrow"], writes=["dlx"])
        ly = bigA[0:1, 15376:15381]
        lx = bigA[0:1, 15384:15389]
        tmp5 = bigA[0:1, 15392:15397]
        V(lambda v: v.tensor_mul(tmp5, lx, ly), reads=["dly", "dlx"], writes=["d5"])
        V(lambda v: v.reduce_sum(asc(40), tmp5, axis=AX.X), reads=["d5"], writes=["dsxy"])
        V(lambda v: v.reduce_sum(asc(41), lx, axis=AX.X), reads=["dlx"], writes=["dsx"])
        V(lambda v: v.reduce_sum(asc(42), ly, axis=AX.X), reads=["dly"], writes=["dsy"])
        V(lambda v: v.tensor_mul(tmp5, lx, lx), reads=["dlx", "d5"], writes=["d5"])
        V(lambda v: v.reduce_sum(asc(43), tmp5, axis=AX.X), reads=["d5"], writes=["dsxx"])
        V(lambda v: v.tensor_scalar(asc(44), asc(40), 5.0, None, ALU.mult),
          reads=["dsxy"], writes=["dnum"])
        V(lambda v: v.tensor_mul(asc(45), asc(41), asc(42)),
          reads=["dsx", "dsy"], writes=["dt"])
        V(lambda v: v.tensor_sub(asc(44), asc(44), asc(45)),
          reads=["dnum", "dt"], writes=["dnum"])
        V(lambda v: v.tensor_scalar(asc(46), asc(43), 5.0, None, ALU.mult),
          reads=["dsxx"], writes=["dden"])
        V(lambda v: v.tensor_mul(asc(45), asc(41), asc(41)),
          reads=["dsx"], writes=["dt"])
        V(lambda v: v.tensor_sub(asc(46), asc(46), asc(45)),
          reads=["dden", "dt"], writes=["dden"])
        V(lambda v: v.reciprocal(asc(45), asc(46)), reads=["dden"], writes=["dt"])
        V(lambda v: v.tensor_mul(asc(47), asc(44), asc(45)),
          reads=["dnum", "dt"], writes=["dq"])
        V(lambda v: v.tensor_scalar(asc(47), asc(47), -1.0, 1.0, ALU.mult, ALU.max),
          reads=["dq"], writes=["dq"])
        V(lambda v: v.tensor_scalar(asc(48), asc(47), 2.0, None, ALU.min),
          reads=["dq"], writes=["dD"])
        V(lambda v: v.tensor_scalar(asc(49), asc(48), 0.2, 1.0, ALU.mult, ALU.add),
          reads=["dD"], writes=["dom"])
        V(lambda v: v.tensor_mul(asc(49), asc(49), asc(48)),
          reads=["dom", "dD"], writes=["dom"])
        V(lambda v: v.tensor_scalar(asc(50), asc(48), 0.5, 1.0, ALU.mult, ALU.add),
          reads=["dD"], writes=["da"])
        V(lambda v: v.tensor_scalar(asc(51), asc(48), 0.1, None, ALU.mult),
          reads=["dD"], writes=["db"])
        TT(lambda t: t.matmul(psB[:, 0:3], onesr[:], bigA[0:1, 15409:15412], start=True, stop=True),
           reads=["dom", "da", "db", "onesr"], writes=["psB"])
        V(lambda v: v.tensor_copy(bias_sb[:, 25:28], psB[:, 0:3]),
          reads=["psB"], writes=["dbc"])

        V(lambda v: v.tensor_copy(gtm_sb[:, 0:8], kid_i[:, :]), reads=["kid"], writes=["gt0"])
        V(lambda v: v.tensor_scalar(gtm_sb[:, 8:16], gtm_sb[:, 0:8], -1.0, 1024.0, ALU.mult, ALU.add),
          reads=["gt0"], writes=["gt1"])
        V(lambda v: v.tensor_tensor(gtm_sb[:, 0:8], gtm_sb[:, 0:8], gtm_sb[:, 8:16], ALU.min),
          reads=["gt0", "gt1"], writes=["gt0"])
        V(lambda v: v.tensor_scalar(gtm_sb[:, 0:8], gtm_sb[:, 0:8], float(1.0 / 1024.0), 1e-8, ALU.mult, ALU.add),
          reads=["gt0"], writes=["gt0"])
        V(lambda v: v.tensor_scalar(gtm_sb[:, 8:16], gtm_sb[:, 0:8], bias_sb[:, 26:27], None, ALU.mult),
          reads=["gt0", "dbc"], writes=["gt1"])
        V(lambda v: v.tensor_scalar(gtm_sb[:, 8:16], gtm_sb[:, 8:16], bias_sb[:, 25:26], None, ALU.add),
          reads=["gt1", "dbc"], writes=["gt1"])
        V(lambda v: v.tensor_scalar(gtm_sb[:, 16:24], gtm_sb[:, 8:16], PI, None, ALU.is_gt),
          reads=["gt1"], writes=["gt2"])
        V(lambda v: v.scalar_tensor_tensor(gtm_sb[:, 8:16], gtm_sb[:, 16:24], -TWO_PI,
                                           gtm_sb[:, 8:16], ALU.mult, ALU.add),
          reads=["gt2", "gt1"], writes=["gt1"])
        S(lambda s: s.activation(gtm_sb[:, 16:24], gtm_sb[:, 8:16], AF.Sin),
          reads=["gt1"], writes=["gt2"])
        S(lambda s: s.activation(gtm_sb[:, 24:32], gtm_sb[:, 0:8], AF.Square),
          reads=["gt0"], writes=["gt3"])
        V(lambda v: v.tensor_scalar(gtm_sb[:, 24:32], gtm_sb[:, 24:32], bias_sb[:, 27:28], None, ALU.mult),
          reads=["gt3", "dbc"], writes=["gt3"])
        V(lambda v: v.tensor_scalar(gtm_sb[:, 8:16], gtm_sb[:, 0:8], -1.0, None, ALU.mult),
          reads=["gt0"], writes=["gt1"])
        V(lambda v: v.tensor_scalar(gtm_sb[:, 8:16], gtm_sb[:, 8:16], bias_sb[:, 25:26], None, ALU.add),
          reads=["gt1", "dbc"], writes=["gt1"])
        V(lambda v: v.tensor_add(gtm_sb[:, 8:16], gtm_sb[:, 8:16], gtm_sb[:, 24:32]),
          reads=["gt1", "gt3"], writes=["gt1"])
        S(lambda s: s.activation(gtm_sb[:, 24:32], gtm_sb[:, 8:16], AF.Sin),
          reads=["gt1"], writes=["gt3"])
        V(lambda v: v.tensor_mul(bias_sb[:, 16:24], gtm_sb[:, 16:24], gtm_sb[:, 24:32]),
          reads=["gt2", "gt3"], writes=["fim"])
        S(lambda s: s.activation(gtm_sb[:, 24:32], gtm_sb[:, 8:16], AF.Sin,
                                 bias=bias_sb[:, 24:25], scale=-1.0),
          reads=["gt1", "hpi"], writes=["gt3"])
        V(lambda v: v.tensor_mul(bias_sb[:, 8:16], gtm_sb[:, 16:24], gtm_sb[:, 24:32]),
          reads=["gt2", "gt3"], writes=["fre"])
        V(lambda v: v.tensor_scalar(bias_sb[:, 8:16], bias_sb[:, 8:16], float(1.0 / 1024.0), None, ALU.mult),
          reads=["fre"], writes=["fre"])
        V(lambda v: v.tensor_scalar(bias_sb[:, 16:24], bias_sb[:, 16:24], float(1.0 / 1024.0), None, ALU.mult),
          reads=["fim"], writes=["fim"])
        V(lambda v: v.tensor_scalar(bias_sb[:, 36:44], bias_sb[:, 16:24], -1.0, None, ALU.mult),
          reads=["fim"], writes=["fimn"])

        CT = bigA[:, 0:1024]
        STb = bigA[:, 1024:2048]
        T2b = bigA[:, 2048:3072]
        T3b = bigA[:, 3072:4096]
        CTm = bigA[:, 4096:4224]
        STm = bigA[:, 4224:4352]
        for kc in range(NDC):
            krow = rowk_f[0:1, 128 * kc: 128 * kc + 128]
            for h in range(2):
                pu = psU0 if h == 0 else psU1
                pur = "psU%d" % h
                TT(lambda t, h=h, krow=krow, pu=pu: t.matmul(
                    pu[:, :], krow,
                    rowk_f[0:1, h * 512:(h + 1) * 512], start=True, stop=True),
                   reads=["rowk"], writes=[pur])
                V(lambda v, h=h, pu=pu: v.tensor_copy(
                    T3b[:, h * 512:(h + 1) * 512].bitcast(I32), pu[:, :]),
                  reads=[pur], writes=["t3"])
                V(lambda v, h=h: v.tensor_scalar(
                    T2b[:, h * 512:(h + 1) * 512].bitcast(I32),
                    T3b[:, h * 512:(h + 1) * 512].bitcast(I32), 1023, None, ALU.bitwise_and),
                  reads=["t3"], writes=["t2"])
                V(lambda v, h=h: v.tensor_copy(
                    CT[:, h * 512:(h + 1) * 512], T2b[:, h * 512:(h + 1) * 512].bitcast(I32)),
                  reads=["t2"], writes=["ct"])
            V(lambda v: v.tensor_scalar(T2b, CT, 512.0, None, ALU.is_ge),
              reads=["ct"], writes=["t2"])
            V(lambda v: v.tensor_scalar(T3b, CT, float(TWO_PI / 1024.0), None, ALU.mult),
              reads=["ct"], writes=["t3"])
            V(lambda v: v.scalar_tensor_tensor(STb, T2b, -TWO_PI, T3b, ALU.mult, ALU.add),
              reads=["t2", "t3"], writes=["st"])
            S(lambda s: s.activation(T2b, STb, AF.Abs), reads=["st"], writes=["t2"])
            S(lambda s: s.activation(CT, T2b, AF.Sin, bias=bias_sb[:, 24:25], scale=-1.0),
              reads=["t2", "hpi"], writes=["ct"])
            S(lambda s: s.activation(STb, STb, AF.Sin), reads=["st"], writes=["st"])
            V(lambda v: v.tensor_copy(CTm, CT[:, 0:128]), reads=["ct"], writes=["ctm"])
            V(lambda v: v.tensor_copy(STm, STb[:, 0:128]), reads=["st"], writes=["stm"])
            fre_c = bias_sb[:, 8 + kc: 9 + kc]
            fim_c = bias_sb[:, 16 + kc: 17 + kc]
            fimn_c = bias_sb[:, 36 + kc: 37 + kc]
            V(lambda v, a=fre_c: v.tensor_scalar(T3b, STb, a, None, ALU.mult),
              reads=["st", "fre"], writes=["t3"])
            V(lambda v, a=fim_c: v.scalar_tensor_tensor(T2b, CT, a, T3b, ALU.mult, ALU.add),
              reads=["ct", "t3", "fim"], writes=["t2"])
            V(lambda v, a=fre_c: v.tensor_scalar(T3b, CT, a, None, ALU.mult),
              reads=["ct", "fre"], writes=["t3"])
            V(lambda v, a=fimn_c: v.scalar_tensor_tensor(CT, STb, a, T3b, ALU.mult, ALU.add),
              reads=["st", "t3", "fimn"], writes=["ct"])
            for h in range(2):
                TT(lambda t, h=h, kc=kc: t.matmul(
                    psC[:, h * 512:(h + 1) * 512], CTm, CT[:, h * 512:(h + 1) * 512],
                    start=(kc == 0), stop=False),
                   reads=["ctm", "ct"], writes=["psC"])
                TT(lambda t, h=h, kc=kc: t.matmul(
                    psC[:, h * 512:(h + 1) * 512], STm, T2b[:, h * 512:(h + 1) * 512],
                    start=False, stop=(kc == NDC - 1)),
                   reads=["stm", "t2"], writes=["psC"])
        for h in range(2):
            V(lambda v, h=h: v.tensor_copy(bigA[:, 5120 + h * 512: 5120 + (h + 1) * 512],
                                           psC[:, h * 512:(h + 1) * 512]),
              reads=["psC"], writes=["strip0"])

    if "P2" in phases:
        strip = lambda j: bigA[:, 5120 + j * 128: 5120 + (j + 1) * 128]
        X1 = lambda ncc: bigA[:, 6144 + ncc * 512: 6144 + (ncc + 1) * 512]
        for it in range(NT512):
            t0 = it * 512
            emit_ln(512, lambda dcx, t0=t0: xc(dcx, t0, 512), ubuf, sqbuf, rowslot,
                    [("x", dcx) for dcx in range(NDC)],
                    [("u", dcx) for dcx in range(NDC)], "sq", "lnrow")
            for ncc in range(NDC):
                pu = psU0 if ncc % 2 == 0 else psU1
                pur = "psU%d" % (ncc % 2)
                for mc in range(NDC):
                    TT(lambda t, ncc=ncc, mc=mc, pu=pu: t.matmul(
                        pu[:, :], strip((ncc - mc) % 8), ubuf(mc),
                        start=(mc == 0), stop=(mc == NDC - 1)),
                       reads=["strip0", ("u", mc)], writes=[pur])
                V(lambda v, ncc=ncc, pu=pu: v.tensor_copy(X1(ncc), pu[:, :]),
                  reads=[pur], writes=[("x1", ncc)])
            for dcx in range(NDC):
                TT(lambda t, dcx=dcx: t.matmul(
                    psA[0:24, :], wel_sb[:, dcx * 24:(dcx + 1) * 24], X1(dcx),
                    start=(dcx == 0), stop=(dcx == NDC - 1)),
                   reads=["wel", ("x1", dcx)], writes=["psA"])
            S(lambda s: s.activation(bits_sb, psA[0:24, :], AF.Identity,
                                     bias=bias_sb[0:24, 32:33], scale=1.0),
              reads=["psA", "bel"], writes=["bits"])
            V(lambda v: v.tensor_scalar(bits_sb, bits_sb, 0.0, None, ALU.is_gt),
              reads=["bits"], writes=["bits"])
            TT(lambda t: t.matmul(psB[0:12, :], ht_sb[:, :], bits_sb,
                                  start=True, stop=True),
               reads=["htm", "bits"], writes=["psB"])
            V(lambda v: v.tensor_scalar(syn_sb, psB[0:12, :], 0.5, None, ALU.is_gt),
              reads=["psB"], writes=["syn"])
            TT(lambda t: t.matmul(psA[0:1, :], ones12[:], syn_sb, start=True, stop=True),
               reads=["ones12", "syn"], writes=["psA"])
            V(lambda v: v.tensor_scalar(fx1_row, psA[0:1, :], 0.5, None, ALU.is_gt),
              reads=["psA"], writes=["fx1"])
            V(lambda v: v.tensor_scalar(fx2_row, psA[0:1, :], 3.5, None, ALU.is_le),
              reads=["psA"], writes=["fx2"])
            V(lambda v: v.tensor_mul(fx1_row, fx1_row, fx2_row),
              reads=["fx1", "fx2"], writes=["fx1"])
            TT(lambda t: t.matmul(psB[0:12, :], onesr[0:1, 0:12], fx1_row,
                                  start=True, stop=True),
               reads=["onesr", "fx1"], writes=["psB"])
            V(lambda v: v.tensor_tensor(syn_sb, syn_sb, psB[0:12, :], ALU.mult),
              reads=["syn", "psB"], writes=["syn"])
            V(lambda v: v.tensor_scalar(onetwob, bits_sb[0:12, 0:512], -2.0, 1.0, ALU.mult, ALU.add),
              reads=["bits"], writes=["otb"])
            V(lambda v: v.tensor_mul(syn_sb, syn_sb, onetwob),
              reads=["syn", "otb"], writes=["syn"])
            V(lambda v: v.tensor_add(bits_sb[0:12, 0:512], bits_sb[0:12, 0:512], syn_sb),
              reads=["bits", "syn"], writes=["bits"])
            for dcx in range(NDC):
                TT(lambda t, dcx=dcx: t.matmul(
                    psB[:, :], wle_sb[:, dcx * 128:(dcx + 1) * 128], bits_sb,
                    start=True, stop=True),
                   reads=["wle", "bits"], writes=["psB"])
                S(lambda s, dcx=dcx: s.activation(aoutb, psB[:, :], AF.Identity,
                                                  bias=bias_sb[:, dcx:dcx + 1], scale=1.0),
                  reads=["psB", "ble"], writes=["aout"])
                V(lambda v, dcx=dcx: v.tensor_add(aoutb, aoutb, X1(dcx)),
                  reads=["aout", ("x1", dcx)], writes=["aout"])
                V(lambda v, dcx=dcx, t0=t0: v.tensor_add(xc(dcx, t0, 512), xc(dcx, t0, 512), aoutb),
                  reads=["aout", ("x", dcx)], writes=[("x", dcx)])

    def emit_ffn(call, fence=(), do_split=False):
        SPLIN = wk_sb[:, 0:1024]
        SPLBF = wk_sb[:, 1024:2048].bitcast(BF16)

        def emit_split_group(gi2):
            if not do_split or gi2 >= 64:
                return
            e2, g2 = gi2 // 8, gi2 % 8
            for mi, (wsrc, wdst, wtag) in enumerate(
                    ((w1_h, w1s_h, "w1s"), (w2_h, w2s_h, "w2s"))):
                for fc in range(4 * g2, 4 * g2 + 4):
                    dmaa(lambda s, e2=e2, fc=fc, wsrc=wsrc: s.dma_start(
                        out=SPLIN, in_=wsrc[e2, fc]), writes=["spin"])
                    V(lambda v: v.tensor_copy(SPLBF[:, 0:1024], SPLIN),
                      reads=["spin"], writes=["spbf"])
                    V(lambda v: v.tensor_sub(SPLBF[:, 1024:2048], SPLIN, SPLBF[:, 0:1024]),
                      reads=["spin", "spbf"], writes=["spbf"])
                    dmaa(lambda s, e2=e2, fc=fc, wdst=wdst: s.dma_start(
                        out=wdst[e2, fc], in_=SPLBF), reads=["spbf"],
                        writes=[(wtag, e2, fc)])
        UHb = lambda dcx: bigB[:, 0:2048].bitcast(BF16)[:, dcx * 512:(dcx + 1) * 512]
        ULb = lambda dcx: bigB[:, 2048:4096].bitcast(BF16)[:, dcx * 512:(dcx + 1) * 512]
        ACC = lambda dcx: bigB[:, 4096 + dcx * 512: 4096 + (dcx + 1) * 512]
        GGF = bigB[:, 8192:8704]
        GGH = lambda i: bigB[:, 8704 + i * 256: 8704 + (i + 1) * 256].bitcast(BF16)
        GGL = lambda i: bigB[:, 9728 + i * 256: 9728 + (i + 1) * 256].bitcast(BF16)
        GBC = bigB[:, 10752:11264]
        SQF = bigB[:, 11264:11776]
        UTMP = bigB[:, 11776:12288]
        frow = lambda i: bigB[0:1, 12288 + i * 512: 12288 + (i + 1) * 512]
        bigAb = bigA[:].bitcast(BF16)

        def W1t(s_, i, hilo, dcx):
            base = (s_ * 4096 + i * 1024) * 2 + hilo * 1024
            return bigAb[:, base + dcx * 128: base + (dcx + 1) * 128]

        def W2t(s_, i, hilo, dcx):
            base = (s_ * 4096 + 2048 + i * 1024) * 2 + hilo * 1024
            return bigAb[:, base + dcx * 128: base + (dcx + 1) * 128]

        groups = [(e, g) for e in range(8) for g in range(16)]
        for tq in range(4):
            t0 = tq * 512

            def usplit(dcx):
                V(lambda v, dcx=dcx: v.tensor_copy(UHb(dcx), UTMP),
                  reads=["utmp"], writes=[("uh", dcx)])
                V(lambda v, dcx=dcx: v.tensor_sub(ULb(dcx), UTMP, UHb(dcx)),
                  reads=["utmp", ("uh", dcx)], writes=[("ul", dcx)])

            emit_ln(512, lambda dcx, t0=t0: xc(dcx, t0, 512), lambda dcx: UTMP,
                    SQF, frow,
                    [("x", dcx) for dcx in range(NDC)],
                    ["utmp"] * NDC, "fsq", "frow", post_chunk=usplit)
            for dcx in range(NDC):
                TT(lambda t, dcx=dcx: t.matmul(psA[0:8, :], rws_bf[:, dcx * 8:(dcx + 1) * 8],
                                               UHb(dcx), start=(dcx == 0), stop=False),
                   reads=["rws", ("uh", dcx)], writes=["psA"])
                TT(lambda t, dcx=dcx: t.matmul(psA[0:8, :], rws_bf[:, dcx * 8:(dcx + 1) * 8],
                                               ULb(dcx), start=False, stop=False),
                   reads=["rws", ("ul", dcx)], writes=["psA"])
                TT(lambda t, dcx=dcx: t.matmul(psA[0:8, :], rws_bf[:, 64 + dcx * 8: 64 + (dcx + 1) * 8],
                                               UHb(dcx), start=False, stop=(dcx == NDC - 1)),
                   reads=["rws", ("uh", dcx)], writes=["psA"])
            V(lambda v: v.tensor_copy(rl_sb[:, :], psA[0:8, :]), reads=["psA"], writes=["rl"])
            for b in range(4):
                TM = gtm_sb[:, b * 16: b * 16 + 8]
                m1c = gtm_sb[:, b * 16 + 8: b * 16 + 9]
                n1c = gtm_sb[:, b * 16 + 9: b * 16 + 10]
                m2c = gtm_sb[:, b * 16 + 10: b * 16 + 11]
                n2c = gtm_sb[:, b * 16 + 11: b * 16 + 12]
                pc = gtm_sb[:, b * 16 + 12: b * 16 + 13]
                ac = gtm_sb[:, b * 16 + 13: b * 16 + 14]
                g2c = gtm_sb[:, b * 16 + 14: b * 16 + 15]
                C1 = bigB[:, 11264 + b * 32 + 0: 11264 + b * 32 + 8]
                C2 = bigB[:, 11264 + b * 32 + 8: 11264 + b * 32 + 16]
                TMX = bigB[:, 11264 + b * 32 + 16: 11264 + b * 32 + 24]
                rg = "g%d" % b
                TT(lambda t, b=b: t.transpose(psB[:, 0:8], rl_sb[:, b * 128:(b + 1) * 128],
                                              id128_sb[0:8, 0:8]),
                   reads=["rl", "id128"], writes=["psB"])
                V(lambda v, TM=TM: v.tensor_copy(TM, psB[:, 0:8]), reads=["psB"], writes=[rg])
                V(lambda v, TM=TM, m1c=m1c: v.reduce_max(m1c, TM, axis=AX.X), reads=[rg], writes=[rg])
                V(lambda v, TM=TM, m1c=m1c, C1=C1: v.tensor_scalar(C1, TM, m1c, None, ALU.is_equal),
                  reads=[rg, "fsq"], writes=[rg, "fsq"])
                V(lambda v, C1=C1, n1c=n1c: v.reduce_sum(n1c, C1, axis=AX.X), reads=[rg, "fsq"], writes=[rg])
                V(lambda v, C1=C1, TM=TM, TMX=TMX: v.scalar_tensor_tensor(TMX, C1, BIG_NEG, TM, ALU.mult, ALU.add),
                  reads=[rg, "fsq"], writes=[rg, "fsq"])
                V(lambda v, TMX=TMX, m2c=m2c: v.reduce_max(m2c, TMX, axis=AX.X), reads=[rg, "fsq"], writes=[rg])
                V(lambda v, TMX=TMX, m2c=m2c, C2=C2: v.tensor_scalar(C2, TMX, m2c, None, ALU.is_equal),
                  reads=[rg, "fsq"], writes=[rg, "fsq"])
                V(lambda v, C2=C2, n2c=n2c: v.reduce_sum(n2c, C2, axis=AX.X), reads=[rg, "fsq"], writes=[rg])
                V(lambda v, n1c=n1c, pc=pc: v.tensor_scalar(pc, n1c, -1.0, 1.0, ALU.add, ALU.min),
                  reads=[rg], writes=[rg])
                V(lambda v, m1c=m1c, m2c=m2c, n1c=n1c: v.tensor_sub(n1c, m1c, m2c),
                  reads=[rg], writes=[rg])
                V(lambda v, n1c=n1c, pc=pc: v.tensor_scalar(n1c, n1c, pc, None, ALU.mult),
                  reads=[rg], writes=[rg])
                V(lambda v, n1c=n1c, m2c=m2c: v.tensor_add(n1c, n1c, m2c),
                  reads=[rg], writes=[rg])
                V(lambda v, m1c=m1c, n1c=n1c: v.tensor_sub(n1c, m1c, n1c),
                  reads=[rg], writes=[rg])
                S(lambda s, n1c=n1c, ac=ac: s.activation(ac, n1c, AF.Sigmoid),
                  reads=[rg], writes=[rg + "a"])
                V(lambda v, ac=ac, m1c=m1c: v.tensor_scalar(m1c, ac, -1.0, 1.0, ALU.mult, ALU.add),
                  reads=[rg + "a"], writes=[rg])
                V(lambda v, pc=pc: v.tensor_scalar(pc, pc, -1.0, 1.0, ALU.mult, ALU.add),
                  reads=[rg], writes=[rg])
                V(lambda v, m1c=m1c, pc=pc: v.tensor_mul(m1c, m1c, pc),
                  reads=[rg], writes=[rg])
                V(lambda v, n2c=n2c, m2c=m2c: v.reciprocal(m2c, n2c),
                  reads=[rg], writes=[rg])
                V(lambda v, m1c=m1c, m2c=m2c, g2c=g2c: v.tensor_mul(g2c, m1c, m2c),
                  reads=[rg], writes=[rg])
                V(lambda v, C2=C2, g2c=g2c: v.tensor_scalar(C2, C2, g2c, None, ALU.mult),
                  reads=[rg, "fsq"], writes=[rg, "fsq"])
                V(lambda v, C1=C1, ac=ac, C2=C2, TM=TM: v.scalar_tensor_tensor(TM, C1, ac, C2, ALU.mult, ALU.add),
                  reads=[rg, rg + "a", "fsq"], writes=[rg])
                TT(lambda t, TM=TM: t.transpose(psB[0:8, 0:128], TM, id128_sb[:, :]),
                   reads=[rg, "id128"], writes=["psB"])
                V(lambda v, b=b: v.tensor_copy(gates_sb[:, b * 128:(b + 1) * 128], psB[0:8, 0:128]),
                  reads=["psB"], writes=["gates"])
            for dcx in range(NDC):
                TT(lambda t, dcx=dcx: t.matmul(psB[:, :], b2l_sb[:, dcx * 128:(dcx + 1) * 128],
                                               gates_sb[:, :], start=True, stop=True),
                   reads=["b2l", "gates"], writes=["psB"])
                V(lambda v, dcx=dcx: v.tensor_copy(ACC(dcx), psB[:, :]),
                  reads=["psB"], writes=[("acc", dcx)])
            pending_down = []
            if tq == 0:
                emit_split_group(0)
                emit_split_group(1)
            for gi, (e, g) in enumerate(groups):
                if tq == 0 and gi % 2 == 0:
                    emit_split_group(gi // 2 + 2)
                slot = gi % 2
                fdeps = list(fence) if (tq == 0 and gi < 2) else []
                dmas(lambda s, e=e, g=g, slot=slot: s.dma_start(
                    out=bigAb[:, slot * 8192: slot * 8192 + 4096].rearrange("p (i n) -> p i n", i=2),
                    in_=w1s_h[e, 2 * g: 2 * g + 2].transpose([1, 0, 2])),
                    writes=[("wst1", slot)], deps=fdeps,
                    reads=[("w1s", e, 2 * g + i) for i in range(2)])
                dmas(lambda s, e=e, g=g, slot=slot: s.dma_start(
                    out=bigAb[:, slot * 8192 + 4096: slot * 8192 + 8192].rearrange("p (i n) -> p i n", i=2),
                    in_=w2s_h[e, 2 * g: 2 * g + 2].transpose([1, 0, 2])),
                    writes=[("wst2", slot)],
                    deps=fdeps,
                    reads=[("w2s", e, 2 * g + i) for i in range(2)])
                if g == 0:
                    TT(lambda t, e=e: t.matmul(psB[:, :], sel_sb[:, e * 128:(e + 1) * 128],
                                               gates_sb[:, :], start=True, stop=True),
                       reads=["sel", "gates"], writes=["psB"])
                    V(lambda v: v.tensor_copy(GBC, psB[:, :]), reads=["psB"], writes=["gbc"])
                for i in range(2):
                    fc = 2 * g + i
                    gsl = 2 * (gi % 2) + i
                    pu = psU0 if (i % 2 == 0) else psU1
                    pur = "psU%d" % (i % 2)
                    for dcx in range(NDC):
                        TT(lambda t, i=i, dcx=dcx, slot=slot, pu=pu: t.matmul(
                            pu[:, :], W1t(slot, i, 0, dcx), UHb(dcx),
                            start=(dcx == 0), stop=False),
                           reads=[("wst1", slot), ("uh", dcx)], writes=[pur])
                        TT(lambda t, i=i, dcx=dcx, slot=slot, pu=pu: t.matmul(
                            pu[:, :], W1t(slot, i, 0, dcx), ULb(dcx),
                            start=False, stop=False),
                           reads=[("wst1", slot), ("ul", dcx)], writes=[pur])
                        TT(lambda t, i=i, dcx=dcx, slot=slot, pu=pu: t.matmul(
                            pu[:, :], W1t(slot, i, 1, dcx), UHb(dcx),
                            start=False, stop=(dcx == NDC - 1)),
                           reads=[("wst1", slot), ("uh", dcx)], writes=[pur])
                    S(lambda s, gsl=gsl, e=e, fc=fc, pu=pu: s.activation(
                        GGF, pu[:, :], AF.Gelu,
                        bias=b1_sb[:, e * 32 + fc: e * 32 + fc + 1], scale=1.0),
                      reads=[pur, "b1"], writes=["ggf"])
                    V(lambda v: v.tensor_mul(GGF, GGF, GBC),
                      reads=["ggf", "gbc"], writes=["ggf"])
                    S(lambda s, gsl=gsl: s.activation(GGH(gsl), GGF, AF.Identity),
                      reads=["ggf"], writes=[("ggh", gsl)])
                    V(lambda v, gsl=gsl: v.tensor_sub(GGL(gsl), GGF, GGH(gsl)),
                      reads=["ggf", ("ggh", gsl)], writes=[("ggl", gsl)])
                def emit_down(slot=slot, gp=gi % 2):
                    for dcx in range(NDC):
                        pd = psD0 if (dcx % 2 == 0) else psD1
                        pdr = "psD%d" % (dcx % 2)
                        for i in range(2):
                            gsl2 = 2 * gp + i
                            TT(lambda t, i=i, gsl2=gsl2, dcx=dcx, slot=slot, pd=pd: t.matmul(
                                pd[:, :], W2t(slot, i, 0, dcx), GGH(gsl2),
                                start=(i == 0), stop=False),
                               reads=[("wst2", slot), ("ggh", gsl2)], writes=[pdr])
                            TT(lambda t, i=i, gsl2=gsl2, dcx=dcx, slot=slot, pd=pd: t.matmul(
                                pd[:, :], W2t(slot, i, 0, dcx), GGL(gsl2),
                                start=False, stop=False),
                               reads=[("wst2", slot), ("ggl", gsl2)], writes=[pdr])
                            TT(lambda t, i=i, gsl2=gsl2, dcx=dcx, slot=slot, pd=pd: t.matmul(
                                pd[:, :], W2t(slot, i, 1, dcx), GGH(gsl2),
                                start=False, stop=(i == 1)),
                               reads=[("wst2", slot), ("ggh", gsl2)], writes=[pdr])
                        V(lambda v, dcx=dcx, pd=pd: v.tensor_add(ACC(dcx), ACC(dcx), pd[:, :]),
                          reads=[pdr, ("acc", dcx)], writes=[("acc", dcx)])
                pending_down.append(emit_down)
                if len(pending_down) > 1:
                    pending_down.pop(0)()
            while pending_down:
                pending_down.pop(0)()
            for dcx in range(NDC):
                V(lambda v, dcx=dcx, t0=t0: v.tensor_add(xc(dcx, t0, 512), xc(dcx, t0, 512), ACC(dcx)),
                  reads=[("acc", dcx), ("x", dcx)], writes=[("x", dcx)])

    if "P3" in phases:
        emit_ffn(0, do_split=True, fence=p.fence(["ct", "st", "t2", "t3", "ctm", "stm", "rowk", "rowi", "strip0",
                "lnrow", "xh", "dsc", "dly", "dlx", "d5", "dsxy", "dsx", "dsy",
                "dsxx", "dnum", "dden", "dt", "dq", "dD", "dom", "da", "db",
                "dcnt", "fx1"] + [("x1", i) for i in range(NDC)]))

    if "P4" in phases:
        CHB = bigB[:, 0:2048]
        HT2 = lambda dcx: bigB[:, 2048 + dcx * 256: 2048 + (dcx + 1) * 256]
        RQ = bigB[:, 8192:9216]
        T1Q = bigB[:, 9216:10240]
        SQ4 = bigB[:, 10240:10752]
        VT4 = bigB[:, 6144:6656]
        hrow1 = lambda i: bigB[0:1, 11264 + i * 256: 11264 + (i + 1) * 256]
        hrow2 = lambda i: bigB[0:1, 12288 + i * 256: 12288 + (i + 1) * 256]
        ONES2K = bigB[:, 4096:6144]
        V(lambda v: v.memset(ONES2K, 1.0), writes=["ones2k"])
        NT256 = TC // 256

        def emit_h_tile(it):
            t0 = it * 256
            emit_ln(256, lambda dcx, t0=t0: xc(dcx, t0, 256), HT2, SQ4, hrow1,
                    [("x", dcx) for dcx in range(NDC)],
                    [("h", dcx) for dcx in range(NDC)], "p4sq", "hrow1")
            emit_ln(256, HT2, HT2, SQ4, hrow2,
                    [("h", dcx) for dcx in range(NDC)],
                    [("h", dcx) for dcx in range(NDC)], "p4sq", "hrow2")
            for dcx in range(NDC):
                V(lambda v, dcx=dcx: v.tensor_scalar(
                    HT2(dcx), HT2(dcx), battn_sb[:, 24 + dcx:25 + dcx],
                    battn_sb[:, 32 + dcx:33 + dcx], ALU.mult, ALU.add),
                  reads=[("h", dcx), "battn"], writes=[("h", dcx)])

        def make_stream(src_h, tag, pos):
            state = {"k": 0}

            def prefetch():
                k = state["k"]
                if k < len(pos):
                    _, dco2 = pos[k]
                    slot = k % 2
                    dmas(lambda s, dco2=dco2, slot=slot: s.dma_start(
                        out=wk_sb[:, slot * 1024:(slot + 1) * 1024], in_=src_h[dco2]),
                        writes=[(tag, slot)])
                    state["k"] += 1
            return prefetch

        pos = [(it, dco) for it in range(NT256) for dco in range(NDC)]
        pf1 = make_stream(wkb_h, "wkS", pos)
        pf1(); pf1()
        for it in range(NT256):
            t0 = it * 256
            emit_h_tile(it)
            if dbg and it == 6 and not dram.get("dbgH"):
                dram["dbgH"] = nc.dram_tensor("dbgH", [128, 2048], F32, kind="ExternalOutput").ap()
                dmas(lambda s: s.dma_start(out=dram["dbgH"], in_=bigB[:, 2048:4096]),
                     reads=[("h", i) for i in range(NDC)], writes=["dbgH"])
            for dco in range(NDC):
                slot = (it * NDC + dco) % 2
                pu = psU0 if dco % 2 == 0 else psU1
                pur = "psU%d" % (dco % 2)
                for dcx in range(NDC):
                    TT(lambda t, dcx=dcx, slot=slot, pu=pu: t.matmul(
                        pu[:, 0:256], wk_sb[:, slot * 1024 + dcx * 128: slot * 1024 + (dcx + 1) * 128],
                        HT2(dcx), start=(dcx == 0), stop=(dcx == NDC - 1)),
                       reads=[("wkS", slot), ("h", dcx)], writes=[pur])
                S(lambda s, dco=dco, t0=t0, pu=pu: s.activation(
                    bigA[:, dco * TC + t0: dco * TC + t0 + 256], pu[:, 0:256],
                    AF.Identity, bias=battn_sb[:, dco:dco + 1], scale=1.0),
                  reads=[pur, "battn"], writes=[("keys", dco)])
                pf1()
        if dbg:
            dbgK_h = nc.dram_tensor("dbgK", [128, 16384], F32, kind="ExternalOutput").ap()
            dmas(lambda s: s.dma_start(out=dbgK_h, in_=bigA[:]),
                 reads=[("keys", i) for i in range(NDC)], writes=["dbgK"])
        pf2 = make_stream(wkab_h, "wkS", pos)
        pf2(); pf2()
        for it in range(NT256):
            t0 = it * 256
            for dco in range(NDC):
                slot = (it * NDC + dco) % 2
                pd = psD0 if dco % 2 == 0 else psD1
                pdr = "psD%d" % (dco % 2)
                for dcx in range(NDC):
                    TT(lambda t, dcx=dcx, t0=t0, slot=slot, pd=pd: t.matmul(
                        pd[:, 0:256], wk_sb[:, slot * 1024 + dcx * 128: slot * 1024 + (dcx + 1) * 128],
                        bigA[:, dcx * TC + t0: dcx * TC + t0 + 256],
                        start=(dcx == 0), stop=(dcx == NDC - 1)),
                       reads=[("wkS", slot), ("keys", dcx)], writes=[pdr])
                S(lambda s, dco=dco, pd=pd: s.activation(
                    CHB[:, dco * 256:(dco + 1) * 256], pd[:, 0:256], AF.Exp,
                    bias=battn_sb[:, 40 + dco:41 + dco], scale=float(1.0 / 32.0)),
                  reads=[pdr, "bka32"], writes=["chb"])
                pf2()
            dmas(lambda s, it=it: s.dma_start(
                out=ew_h[:, :, it * 256:(it + 1) * 256].transpose([1, 0, 2]),
                in_=CHB.rearrange("p (c t) -> p c t", c=NDC)),
                reads=["chb"], writes=["ewh%d" % it])
        for dcx in range(NDC):
            dmas(lambda s, dcx=dcx: s.dma_start(out=CHB, in_=ew_h[dcx]),
                 reads=["ewh%d" % it2 for it2 in range(NT256)], writes=["chb"])
            V(lambda v: v.tensor_tensor_scan(CHB, ONES2K, CHB, 0.0, ALU.mult, ALU.add),
              reads=["chb", "ones2k"], writes=["chb"])
            V(lambda v, dcx=dcx: v.tensor_copy(tot_sb[:, dcx:dcx + 1], CHB[:, 2047:2048]),
              reads=["chb"], writes=["tot1"])
            dmas(lambda s, dcx=dcx: s.dma_start(out=ew_h[dcx], in_=CHB),
                 reads=["chb"], writes=["csh%d" % dcx])
        dmag(lambda g: g.dma_start(out=cc2i_h, in_=tot_sb[:, 0:8]),
             reads=["tot1"], writes=["cc2i"])
        p.step("gpsimd", lambda g: g.collective_compute(
            "AllGather", ALU.bypass, replica_groups=[list(range(NCORES))],
            ins=[cc2i_h.opt()], outs=[cc2o_h.opt()]),
            reads=["cc2i"], writes=["cc2o"], cc=True)
        dmag(lambda g: g.dma_start(out=gath_sb[:].rearrange("p (r c) -> p r c", r=8),
                                   in_=cc2o_h.transpose([1, 0, 2])),
             reads=["cc2o"], writes=["gath"])
        V(lambda v: v.memset(tot_sb[:, 8:16], 0.0), writes=["carry1"])
        for r8 in range(8):
            V(lambda v, r8=r8: v.scalar_tensor_tensor(
                tot_sb[:, 32:40], gath_sb[:, r8 * 8:(r8 + 1) * 8], mexc_sb[:, r8:r8 + 1],
                tot_sb[:, 8:16], ALU.mult, ALU.add),
              reads=["gath", "mexc", "carry1"], writes=["ctmp"])
            V(lambda v: v.tensor_copy(tot_sb[:, 8:16], tot_sb[:, 32:40]),
              reads=["ctmp"], writes=["carry1"])
        for dcx in range(NDC):
            dmas(lambda s, dcx=dcx: s.dma_start(out=CHB, in_=ew_h[dcx]),
                 reads=["csh%d" % dcx, "chb"], writes=["chb"])
            V(lambda v, dcx=dcx: v.tensor_scalar(CHB, CHB, tot_sb[:, 8 + dcx:9 + dcx], None, ALU.add),
              reads=["chb", "carry1"], writes=["chb"])
            for q in range(2):
                q0 = q * 1024
                V(lambda v, q0=q0: v.reciprocal(RQ, CHB[:, q0:q0 + 1024]),
                  reads=["chb"], writes=["rq"])
                if q == 0:
                    V(lambda v, dcx=dcx: v.tensor_scalar(
                        T1Q[:, 0:1], RQ[:, 0:1], tot_sb[:, 8 + dcx:9 + dcx], None, ALU.mult),
                      reads=["rq", "carry1"], writes=["t1q"])
                    V(lambda v: v.tensor_mul(T1Q[:, 1:1024], CHB[:, 0:1023], RQ[:, 1:1024]),
                      reads=["chb", "rq"], writes=["t1q"])
                else:
                    V(lambda v, q0=q0: v.tensor_mul(T1Q, CHB[:, q0 - 1:q0 + 1023], RQ),
                      reads=["chb", "rq"], writes=["t1q"])
                V(lambda v: v.tensor_scalar(T1Q, T1Q, -1.0, 1.0, ALU.mult, ALU.add),
                  reads=["t1q"], writes=["t1q"])
                V(lambda v, dcx=dcx, q0=q0: v.tensor_mul(
                    bigA[:, dcx * TC + q0: dcx * TC + q0 + 1024],
                    bigA[:, dcx * TC + q0: dcx * TC + q0 + 1024], T1Q),
                  reads=["t1q", ("keys", dcx)], writes=[("keys", dcx)])
        for dcx in range(NDC):
            V(lambda v, dcx=dcx: v.tensor_tensor_scan(
                bigA[:, dcx * TC: dcx * TC + TC], ONES2K, bigA[:, dcx * TC: dcx * TC + TC],
                0.0, ALU.mult, ALU.add),
              reads=[("keys", dcx), "ones2k"], writes=[("keys", dcx)])
            V(lambda v, dcx=dcx: v.tensor_copy(tot_sb[:, 16 + dcx:17 + dcx],
                                               bigA[:, dcx * TC + TC - 1: dcx * TC + TC]),
              reads=[("keys", dcx)], writes=["tot2"])
        dmag(lambda g: g.dma_start(out=cc3i_h, in_=tot_sb[:, 16:24]),
             reads=["tot2"], writes=["cc3i"])
        p.step("gpsimd", lambda g: g.collective_compute(
            "AllGather", ALU.bypass, replica_groups=[list(range(NCORES))],
            ins=[cc3i_h.opt()], outs=[cc3o_h.opt()]),
            reads=["cc3i"], writes=["cc3o"], cc=True)
        dmag(lambda g: g.dma_start(out=gath_sb[:].rearrange("p (r c) -> p r c", r=8),
                                   in_=cc3o_h.transpose([1, 0, 2])),
             reads=["cc3o"], writes=["gath"])
        V(lambda v: v.memset(tot_sb[:, 24:32], 0.0), writes=["carry2"])
        for r8 in range(8):
            V(lambda v, r8=r8: v.scalar_tensor_tensor(
                tot_sb[:, 32:40], gath_sb[:, r8 * 8:(r8 + 1) * 8], mexc_sb[:, r8:r8 + 1],
                tot_sb[:, 24:32], ALU.mult, ALU.add),
              reads=["gath", "mexc", "carry2"], writes=["ctmp"])
            V(lambda v: v.tensor_copy(tot_sb[:, 24:32], tot_sb[:, 32:40]),
              reads=["ctmp"], writes=["carry2"])
        for dcx in range(NDC):
            V(lambda v, dcx=dcx: v.tensor_scalar(
                bigA[:, dcx * TC: dcx * TC + TC], bigA[:, dcx * TC: dcx * TC + TC],
                tot_sb[:, 24 + dcx:25 + dcx], None, ALU.add),
              reads=[("keys", dcx), "carry2"], writes=[("keys", dcx)])
        pf3 = make_stream(wvb_h, "wkS", pos)
        pf3(); pf3()
        for it in range(NT256):
            t0 = it * 256
            emit_h_tile(it)
            if dbg and it == 6 and not dram.get("dbgH"):
                dram["dbgH"] = nc.dram_tensor("dbgH", [128, 2048], F32, kind="ExternalOutput").ap()
                dmas(lambda s: s.dma_start(out=dram["dbgH"], in_=bigB[:, 2048:4096]),
                     reads=[("h", i) for i in range(NDC)], writes=["dbgH"])
            for dco in range(NDC):
                slot = (it * NDC + dco) % 2
                pu = psU0 if dco % 2 == 0 else psU1
                pur = "psU%d" % (dco % 2)
                for dcx in range(NDC):
                    TT(lambda t, dcx=dcx, slot=slot, pu=pu: t.matmul(
                        pu[:, 0:256], wk_sb[:, slot * 1024 + dcx * 128: slot * 1024 + (dcx + 1) * 128],
                        HT2(dcx), start=(dcx == 0), stop=(dcx == NDC - 1)),
                       reads=[("wkS", slot), ("h", dcx)], writes=[pur])
                S(lambda s, dco=dco, pu=pu: s.activation(
                    VT4[:, 0:256], pu[:, 0:256], AF.Identity,
                    bias=battn_sb[:, 16 + dco:17 + dco], scale=1.0),
                  reads=[pur, "battn"], writes=["vtmp"])
                V(lambda v, dco=dco, t0=t0: v.tensor_mul(
                    VT4[:, 0:256], VT4[:, 0:256],
                    bigA[:, dco * TC + t0: dco * TC + t0 + 256]),
                  reads=["vtmp", ("keys", dco)], writes=["vtmp"])
                V(lambda v, dco=dco, t0=t0: v.tensor_add(
                    xc(dco, t0, 256), xc(dco, t0, 256), VT4[:, 0:256]),
                  reads=["vtmp", ("x", dco)], writes=[("x", dco)])
                pf3()

    if "P5" in phases:
        emit_ffn(1, do_split=("P3" not in phases), fence=p.fence(["ct", "st", "t2", "t3", "ctm", "stm", "rowk", "rowi", "strip0",
                "lnrow", "xh", "dsc", "dly", "dlx", "d5", "dsxy", "dsx", "dsy",
                "dsxx", "dnum", "dden", "dt", "dq", "dD", "dom", "da", "db",
                "dcnt", "fx1"] + [("x1", i) for i in range(NDC)]
            + [("keys", i) for i in range(NDC)]))

    dmas(lambda s: s.dma_start(
        out=yT_h.rearrange("(c p) t -> c p t", p=128).transpose([1, 0, 2]),
        in_=x_sb[:].rearrange("p (c t) -> p c t", c=NDC)),
        reads=[("x", dcx) for dcx in range(NDC)], writes=["yout"])
    if dbg:
        dbgA_h = nc.dram_tensor("dbgA", [128, 16384], F32, kind="ExternalOutput").ap()
        dbgB_h = nc.dram_tensor("dbgB", [128, 13312], F32, kind="ExternalOutput").ap()
        dbgr_h = nc.dram_tensor("dbgr", [1, 64], F32, kind="ExternalOutput").ap()
        dbgc_h = nc.dram_tensor("dbgc", [128, 48], F32, kind="ExternalOutput").ap()
        allreg = list(p.region_w.keys())
        dmas(lambda s: s.dma_start(out=dbgA_h, in_=bigA[:]), reads=allreg, writes=["dbg1"])
        dmas(lambda s: s.dma_start(out=dbgB_h, in_=bigB[:]), reads=[], writes=["dbg2"])
        dmas(lambda s: s.dma_start(out=dbgr_h, in_=g1_sb[:]), reads=[], writes=["dbg3"])
        dmas(lambda s: s.dma_start(out=dbgc_h, in_=bias_sb[:]), reads=[], writes=["dbg4"])
        dbgt_h = nc.dram_tensor("dbgt", [128, 40], F32, kind="ExternalOutput").ap()
        dmas(lambda s: s.dma_start(out=dbgt_h, in_=tot_sb[:]), reads=[], writes=["dbg5"])

    p.emit(nc, block, sems)
    es.close()
    return nc


def _prep_inputs(inputs):
    f32 = np.float32
    x = np.asarray(inputs["L0_attn_a_x"], f32).reshape(T, D)
    Wel = np.asarray(inputs["L0_attn_a_W_el"], f32)
    bel = np.asarray(inputs["L0_attn_a_b_el"], f32).reshape(24, 1)
    Wle = np.asarray(inputs["L0_attn_a_W_le"], f32)
    ble = np.ascontiguousarray(np.asarray(inputs["L0_attn_a_b_le"], f32).reshape(8, 128).T)
    A = np.array([[1, 1, 1, 0, 1, 1, 1, 0, 0, 0, 1, 1], [1, 1, 0, 1, 1, 1, 0, 0, 0, 1, 1, 0],
                  [1, 0, 1, 1, 1, 0, 0, 0, 1, 1, 0, 1], [0, 1, 1, 1, 0, 0, 0, 1, 1, 0, 1, 1],
                  [0, 0, 0, 0, 1, 1, 1, 1, 1, 1, 1, 1], [1, 1, 1, 1, 0, 0, 0, 0, 0, 0, 0, 0],
                  [0, 0, 0, 0, 1, 1, 1, 1, 0, 0, 0, 0], [1, 1, 0, 0, 1, 1, 0, 0, 1, 1, 0, 0],
                  [1, 0, 1, 0, 1, 0, 1, 0, 1, 0, 1, 0], [0, 1, 0, 1, 0, 1, 0, 1, 0, 1, 0, 1],
                  [1, 0, 0, 1, 1, 0, 0, 1, 1, 0, 0, 1], [0, 1, 1, 0, 0, 1, 1, 0, 0, 1, 1, 0]],
                 dtype=f32)
    H = np.concatenate([A.T, np.eye(12, dtype=f32)], axis=1)
    HT = np.ascontiguousarray(H.T)

    def colblocks(W):
        W4 = W.reshape(8, 128, 8, 128)
        return np.ascontiguousarray(W4.transpose(0, 3, 2, 1))

    wkb = colblocks(np.asarray(inputs["L0_attn_b_Wk"], f32)).reshape(8, 128, 1024)
    wkab = colblocks(np.asarray(inputs["L0_attn_b_Wka"], f32)).reshape(8, 128, 1024)
    wvb = colblocks(np.asarray(inputs["L0_attn_b_Wv"], f32)).reshape(8, 128, 1024)

    def ppcols(v):
        return np.ascontiguousarray(np.asarray(v, f32).reshape(8, 128).T)

    battn = np.concatenate([ppcols(inputs["L0_attn_b_bk"]), ppcols(inputs["L0_attn_b_bka"]),
                            ppcols(inputs["L0_attn_b_bv"]), ppcols(inputs["L0_attn_b_gamma"]),
                            ppcols(inputs["L0_attn_b_beta"])], axis=1)
    wel_blk = np.ascontiguousarray(Wel.reshape(8, 128, 24))
    rwb = np.ascontiguousarray(np.asarray(inputs["L0_ffn_router_w"], f32).T.reshape(8, 128, 8))
    b1 = np.asarray(inputs["L0_ffn_b1"], f32)
    b1v = np.ascontiguousarray(b1.reshape(8, 32, 128).transpose(2, 0, 1).reshape(128, 256))
    b2l = np.ascontiguousarray(np.asarray(inputs["L0_ffn_b2"], f32))
    w1 = np.asarray(inputs["L0_ffn_w1"], f32)
    w2 = np.asarray(inputs["L0_ffn_w2"], f32)
    w1b = np.ascontiguousarray(
        w1.reshape(8, 32, 128, 8, 128).transpose(0, 1, 4, 3, 2).reshape(8, 32, 128, 1024))
    w2b = np.ascontiguousarray(
        w2.reshape(8, 8, 128, 32, 128).transpose(0, 3, 4, 1, 2).reshape(8, 32, 128, 1024))
    sel = np.zeros((8, 8 * 128), f32)
    for e in range(8):
        sel[e, e * 128:(e + 1) * 128] = 1.0
    id128 = np.eye(128, dtype=f32)
    lxc = np.array([[2, 4, 8, 16, 32, 1, 1, 1]], dtype=f32)
    xhead = np.ascontiguousarray(x[0:1, :])

    shared = dict(wel=wel_blk, bel=bel, wle=Wle, ble=ble, htm=HT,
                  wkb=wkb, wkab=wkab, wvb=wvb, battn=battn, rwb=rwb,
                  b1v=b1v, b2l=b2l, sel=sel, id128=id128,
                  w1b=w1b, w2b=w2b, lxc=lxc, xhead=xhead)
    in_maps = []
    for j in range(NCORES):
        sh = x[j * TC:(j + 1) * TC, :]
        m = dict(shared)
        m["xT"] = np.ascontiguousarray(sh.T)
        m["mexc"] = np.broadcast_to(
            (np.arange(8) < j).astype(f32), (128, 8)).copy()
        in_maps.append(m)
    return in_maps


def kernel(**inputs):
    phases = inputs.pop("_phases", ALL_PHASES)
    dbg = inputs.pop("_dbg", False)
    in_maps = _prep_inputs(inputs)
    nc = build_nc(phases=phases, dbg=dbg)
    res = run_bass_kernel_spmd(nc, in_maps, core_ids=list(range(NCORES)))
    out = np.empty((T, D), np.float32)
    for j in range(NCORES):
        out[j * TC:(j + 1) * TC, :] = res.results[j]["yT"].T
    out = out.reshape(1, T, D)
    if dbg:
        return out, res
    return out


# version 05
# speedup vs baseline: 1.0767x; 1.0767x over previous
import numpy as np
import concourse.bass as bass
import concourse.mybir as mybir
from concourse.bass_utils import run_bass_kernel_spmd

F32 = mybir.dt.float32
BF16 = mybir.dt.bfloat16
I32 = mybir.dt.int32
AF = mybir.ActivationFunctionType
ALU = mybir.AluOpType
AX = mybir.AxisListType

NCORES = 8
T = 16384
TC = T // NCORES
D = 1024
NDC = 8
PI = float(np.pi)
TWO_PI = float(2 * np.pi)
HALF_PI = float(np.pi / 2)
LN_EPS = 1e-5
BIG_NEG = -1.0e30

ALL_PHASES = ("P1", "P2", "P3", "P4", "P5")
P4_STOP = 99


class Prog:
    QUEUES = ("qs", "qg", "qa")

    def __init__(self):
        self.steps = []
        self.region_w = {}
        self.region_r = {}

    def step(self, eng, fn, reads=(), writes=(), dma=None, cc=False, deps=()):
        dep = set(d for d in deps if d is not None)
        for rg in reads:
            w = self.region_w.get(rg)
            if w is not None:
                dep.add(w)
        for rg in writes:
            w = self.region_w.get(rg)
            if w is not None:
                dep.add(w)
            dep.update(self.region_r.get(rg, ()))
        sid = len(self.steps)
        counter = "cc" if cc else (dma if dma is not None else eng)
        self.steps.append([eng, fn, counter, sorted(dep)])
        for rg in reads:
            self.region_r.setdefault(rg, set()).add(sid)
        for rg in writes:
            self.region_w[rg] = sid
            self.region_r[rg] = set()
        return sid

    def fence(self, regions):
        out = set()
        for rg in regions:
            w = self.region_w.get(rg)
            if w is not None:
                out.add(w)
            out.update(self.region_r.get(rg, ()))
        return sorted(out)

    def emit(self, nc, block, sems):
        steps = self.steps
        needed = set()
        for st in steps:
            needed.update(st[3])
        ordinal = [None] * len(steps)
        counts = {}
        for sid, st in enumerate(steps):
            c = st[2]
            if sid in needed or c == "cc" or c in self.QUEUES:
                ordinal[sid] = counts.get(c, 0)
                counts[c] = ordinal[sid] + 1
        totals = {c: k * (16 if c in self.QUEUES else 1) for c, k in counts.items()}

        def wait_of(sid):
            c = steps[sid][2]
            v = ordinal[sid] + 1
            return c, (v * 16 if c in self.QUEUES else v)

        def emit_engine(eng_name, eng_obj):
            last = {}
            for sid, st in enumerate(steps):
                issuer, fn, counter, deps = st
                if issuer != eng_name:
                    continue
                for d in deps:
                    dc_, dv = wait_of(d)
                    if (eng_name == "tensor" and dc_ == "tensor"
                            and steps[d][2] == "tensor"):
                        continue
                    if last.get(dc_, 0) < dv:
                        eng_obj.wait_ge(sems[dc_], dv)
                        last[dc_] = dv
                inst = fn(eng_obj)
                if ordinal[sid] is not None:
                    inst.then_inc(sems[counter], 16 if counter in self.QUEUES else 1)

        @block.sync
        def _(sync):
            emit_engine("sync", sync)
            for c in sorted(totals):
                if c != "sync" and totals[c]:
                    sync.wait_ge(sems[c], totals[c])

        @block.scalar
        def _(scalar):
            emit_engine("scalar", scalar)

        @block.vector
        def _(vector):
            emit_engine("vector", vector)

        @block.tensor
        def _(tensor):
            emit_engine("tensor", tensor)

        @block.gpsimd
        def _(gpsimd):
            emit_engine("gpsimd", gpsimd)


def build_nc(phases=ALL_PHASES, dbg=False):
    import contextlib
    nc = bass.Bass("TRN2", target_bir_lowering=False)
    dram = {}

    def din(name, shape, dtype=F32):
        dram[name] = nc.dram_tensor(name, shape, dtype, kind="ExternalInput").ap()
        return dram[name]

    xT_h = din("xT", [D, TC])
    xhead_h = din("xhead", [1, D])
    wel_h = din("wel", [NDC, 128, 24])
    bel_h = din("bel", [24, 1])
    wle_h = din("wle", [24, D])
    ble_h = din("ble", [128, 8])
    ht_h = din("htm", [24, 12])
    wkb_h = din("wkb", [NDC, 128, D])
    wkab_h = din("wkab", [NDC, 128, D])
    wvb_h = din("wvb", [NDC, 128, D])
    battn_h = din("battn", [128, 40])
    rwb_h = din("rwb", [NDC, 128, 8])
    b1_h = din("b1v", [128, 8 * 32])
    b2l_h = din("b2l", [8, D])
    sel_h = din("sel", [8, 8 * 128])
    id128_h = din("id128", [128, 128])
    w1_h = din("w1b", [8, 32, 128, D])
    w2_h = din("w2b", [8, 32, 128, D])
    mexc_h = din("mexc", [128, 8])
    lx_h = din("lxc", [1, 8])

    yT_h = nc.dram_tensor("yT", [D, TC], F32, kind="ExternalOutput").ap()

    w1s_h = nc.dram_tensor("w1s", [8, 32, 128, 2048], BF16).ap()
    w2s_h = nc.dram_tensor("w2s", [8, 32, 128, 2048], BF16).ap()
    ew_h = nc.dram_tensor("ew_spill", [NDC, 128, TC], F32,
                          kind=("ExternalOutput" if dbg else "Internal")).ap()
    cc1i_h = nc.dram_tensor("cc1i", [1, 8], F32).ap()
    cc1o_h = nc.dram_tensor("cc1o", [8, 8], F32).ap()
    cc2i_h = nc.dram_tensor("cc2i", [128, 8], F32).ap()
    cc2o_h = nc.dram_tensor("cc2o", [8, 128, 8], F32).ap()
    cc3i_h = nc.dram_tensor("cc3i", [128, 8], F32).ap()
    cc3o_h = nc.dram_tensor("cc3o", [8, 128, 8], F32).ap()

    es = contextlib.ExitStack()

    def sb(name, shape, dtype=F32):
        return es.enter_context(nc.sbuf_tensor(name, shape, dtype))

    x_sb = sb("x_sb", [128, NDC * TC])
    bigA = sb("bigA", [128, 16384])
    bigB = sb("bigB", [128, 13824])
    wk_sb = sb("wk_sb", [128, 2048])
    wel_sb = sb("wel_sb", [128, NDC * 24])
    ht_sb = sb("ht_sb", [24, 12])
    b1_sb = sb("b1_sb", [128, 8 * 32])
    b2l_sb = sb("b2l_sb", [8, D])
    sel_sb = sb("sel_sb", [8, 8 * 128])
    id128_sb = sb("id128_sb", [128, 128])
    rl_sb = sb("rl_sb", [8, 512])
    gates_sb = sb("gates_sb", [8, 512])
    gtm_sb = sb("gtm_sb", [128, 64])
    battn_sb = sb("battn_sb", [128, 48])
    bias_sb = sb("bias_sb", [128, 48])
    rwb_sb = sb("rwb_sb", [128, NDC * 8])
    rws_sb = sb("rws_sb", [128, 64])
    mexc_sb = sb("mexc_sb", [128, 8])
    tot_sb = sb("tot_sb", [128, 40])
    mm_sb = sb("mm_sb", [128, 8])
    ones128 = sb("ones128", [128, 1])
    onesr = sb("onesr", [1, 128])
    ones12 = sb("ones12", [12, 1])
    kid_i = sb("kid_i", [128, 8], I32)
    gath_sb = sb("gath_sb", [128, 64])
    g1_sb = sb("g1_sb", [1, 64])

    def psmk(name, shape):
        return es.enter_context(nc.psum_tensor(name, shape, F32))

    psA = psmk("psA", [128, 512])
    psB = psmk("psB", [128, 512])
    psU0 = psmk("psU0", [128, 512])
    psU1 = psmk("psU1", [128, 512])
    psD0 = psmk("psD0", [128, 512])
    psD1 = psmk("psD1", [128, 512])
    psC = psmk("psC", [128, 1024])

    sems = {}
    for cname in ("sync", "scalar", "vector", "tensor", "gpsimd", "qs", "qg", "qa", "cc"):
        sems[cname] = es.enter_context(nc.semaphore("sem_" + cname))

    block = es.enter_context(nc.Block())
    p = Prog()

    def xc(dc, t0, tn):
        return x_sb[:, dc * TC + t0: dc * TC + t0 + tn]

    def dmas(fn, reads=(), writes=(), deps=()):
        return p.step("sync", fn, reads=reads, writes=writes, dma="qs", deps=deps)

    def dmag(fn, reads=(), writes=(), deps=()):
        return p.step("gpsimd", fn, reads=reads, writes=writes, dma="qg", deps=deps)

    def dmaa(fn, reads=(), writes=(), deps=()):
        return p.step("scalar", fn, reads=reads, writes=writes, dma="qa", deps=deps)

    def V(fn, reads=(), writes=(), deps=()):
        return p.step("vector", fn, reads=reads, writes=writes, deps=deps)

    def S(fn, reads=(), writes=(), deps=()):
        return p.step("scalar", fn, reads=reads, writes=writes, deps=deps)

    def TT(fn, reads=(), writes=(), deps=()):
        return p.step("tensor", fn, reads=reads, writes=writes, deps=deps)

    def G(fn, reads=(), writes=(), deps=()):
        return p.step("gpsimd", fn, reads=reads, writes=writes, deps=deps)

    dmas(lambda s: s.dma_start(
        out=x_sb[:].rearrange("p (c t) -> p c t", c=NDC),
        in_=xT_h.rearrange("(c p) t -> c p t", p=128).transpose([1, 0, 2])),
        writes=[("x", dc) for dc in range(NDC)])
    dmas(lambda s: s.dma_start(
        out=wel_sb[:].rearrange("p (c m) -> p c m", c=NDC),
        in_=wel_h.transpose([1, 0, 2])), writes=["wel"])
    dmas(lambda s: s.dma_start(out=bigB[0:24, 8192:9216], in_=wle_h), writes=["wle"])
    dmas(lambda s: s.dma_start(out=ht_sb[:], in_=ht_h), writes=["htm"])
    dmas(lambda s: s.dma_start(out=battn_sb[:, 0:40], in_=battn_h), writes=["battn"])
    dmas(lambda s: s.dma_start(out=b1_sb[:], in_=b1_h), writes=["b1"])
    dmas(lambda s: s.dma_start(out=b2l_sb[:], in_=b2l_h), writes=["b2l"])
    dmas(lambda s: s.dma_start(out=sel_sb[:], in_=sel_h), writes=["sel"])
    dmas(lambda s: s.dma_start(out=id128_sb[:], in_=id128_h), writes=["id128"])
    dmas(lambda s: s.dma_start(
        out=rwb_sb[:].rearrange("p (c e) -> p c e", c=NDC),
        in_=rwb_h.transpose([1, 0, 2])), writes=["rwb"])
    dmas(lambda s: s.dma_start(out=mexc_sb[:], in_=mexc_h), writes=["mexc"])
    dmas(lambda s: s.dma_start(out=bias_sb[:, 0:8], in_=ble_h), writes=["ble"])
    dmas(lambda s: s.dma_start(out=g1_sb[0:1, 32:40], in_=lx_h), writes=["lxrow"])
    dmas(lambda s: s.dma_start(out=bias_sb[0:24, 32:33], in_=bel_h), writes=["bel"])

    V(lambda v: v.memset(ones128[:], 1.0), writes=["ones128"])
    V(lambda v: v.memset(onesr[:], 1.0), writes=["onesr"])
    V(lambda v: v.memset(ones12[:], 1.0), writes=["ones12"])
    V(lambda v: v.memset(bias_sb[:, 24:25], HALF_PI), writes=["hpi"])
    V(lambda v: v.tensor_scalar(bias_sb[0:24, 33:34], bias_sb[0:24, 32:33], -1.0, None, ALU.mult),
      reads=["bel"], writes=["negbel"])
    V(lambda v: v.tensor_scalar(battn_sb[:, 40:48], battn_sb[:, 8:16], float(1.0 / 32.0), None, ALU.mult),
      reads=["battn"], writes=["bka32"])
    G(lambda g: g.iota(kid_i[:, :], [[128, 8]], channel_multiplier=1), writes=["kid"])
    rws_bf = rws_sb[:].bitcast(BF16)
    V(lambda v: v.tensor_copy(rws_bf[:, 0:64], rwb_sb[:]), reads=["rwb"], writes=["rws"])
    V(lambda v: v.tensor_sub(rws_bf[:, 64:128], rwb_sb[:], rws_bf[:, 0:64]),
      reads=["rwb", "rws"], writes=["rws"])

    def emit_ln(tn, src_ap, dst_ap, sq_ap, rowb, src_regions, dst_regions,
                sq_region, row_region, post_chunk=None):
        for dcx in range(NDC):
            TT(lambda t, dcx=dcx: t.matmul(psA[0:1, 0:tn], ones128[:], src_ap(dcx),
                                           start=(dcx == 0), stop=(dcx == NDC - 1)),
               reads=[src_regions[dcx], "ones128"], writes=["psA"])
        sqv = sq_ap[:, 0:tn]
        for dcx in range(NDC):
            S(lambda s, dcx=dcx: s.activation(sqv, src_ap(dcx), AF.Square),
              reads=[src_regions[dcx]], writes=[sq_region])
            TT(lambda t, dcx=dcx: t.matmul(psB[0:1, 0:tn], ones128[:], sqv,
                                           start=(dcx == 0), stop=(dcx == NDC - 1)),
               reads=[sq_region, "ones128"], writes=["psB"])
        V(lambda v: v.tensor_scalar(rowb(0), psA[0:1, 0:tn], float(1.0 / D), None, ALU.mult),
          reads=["psA"], writes=[row_region])
        V(lambda v: v.tensor_scalar(rowb(1), psB[0:1, 0:tn], float(1.0 / D), None, ALU.mult),
          reads=["psB", row_region], writes=[row_region])
        V(lambda v: v.tensor_mul(rowb(2), rowb(0), rowb(0)),
          reads=[row_region], writes=[row_region])
        V(lambda v: v.tensor_sub(rowb(1), rowb(1), rowb(2)),
          reads=[row_region], writes=[row_region])
        V(lambda v: v.tensor_scalar(rowb(1), rowb(1), LN_EPS, None, ALU.add),
          reads=[row_region], writes=[row_region])
        V(lambda v: v.reciprocal(rowb(2), rowb(1)),
          reads=[row_region], writes=[row_region])
        S(lambda s: s.activation(rowb(1), rowb(2), AF.Sqrt),
          reads=[row_region], writes=[row_region])
        TT(lambda t: t.matmul(psD0[:, 0:tn], onesr[:], rowb(0), start=True, stop=True),
           reads=[row_region, "onesr"], writes=["psD0"])
        TT(lambda t: t.matmul(psD1[:, 0:tn], onesr[:], rowb(1), start=True, stop=True),
           reads=[row_region, "onesr"], writes=["psD1"])
        for dcx in range(NDC):
            V(lambda v, dcx=dcx: v.tensor_sub(dst_ap(dcx), src_ap(dcx), psD0[:, 0:tn]),
              reads=[src_regions[dcx], "psD0"], writes=[dst_regions[dcx]])
            V(lambda v, dcx=dcx: v.tensor_mul(dst_ap(dcx), dst_ap(dcx), psD1[:, 0:tn]),
              reads=[dst_regions[dcx], "psD1"], writes=[dst_regions[dcx]])
            if post_chunk is not None:
                post_chunk(dcx)

    NT512 = TC // 512
    rowslot = lambda i: bigA[0:1, 10240 + i * 512: 10240 + (i + 1) * 512]
    rowk_f = bigA[0:1, 12288:13312]
    rowi_i = bigA[0:1, 13312:14336].bitcast(I32)
    arow = lambda a, b: bigA[0:1, 14336 + a: 14336 + b]
    asc = lambda i: bigA[0:1, 15360 + i: 15360 + i + 1]
    fx1_row = bigA[0:1, 15488:16000]
    ubuf = lambda dcx: bigB[:, dcx * 512:(dcx + 1) * 512]
    wle_sb = bigB[0:24, 8192:9216]
    bits_sb = bigB[0:24, 9216:9728]
    syn_sb = bigB[0:12, 9728:10240]
    sqbuf = bigB[:, 4096:4608]
    dcmp = bigB[0:1, 4608:5632]
    fx2_row = bigB[0:1, 5632:6144]
    onetwob = bigB[0:12, 6144:6656]
    aoutb = bigB[:, 6656:7168]
    dsq = bigB[0:1, 7168:8192]

    if "P1" in phases:
        G(lambda g: g.iota(rowi_i, [[1, 1024]], channel_multiplier=0), writes=["rowi"])
        V(lambda v: v.tensor_copy(rowk_f, rowi_i), reads=["rowi"], writes=["rowk"])
        V(lambda v: v.memset(mm_sb[:, 0:2], BIG_NEG), writes=["mmacc"])
        for it in range(NT512):
            t0 = it * 512
            emit_ln(512, lambda dcx, t0=t0: xc(dcx, t0, 512), ubuf, sqbuf, rowslot,
                    [("x", dcx) for dcx in range(NDC)],
                    [("u", dcx) for dcx in range(NDC)], "sq", "lnrow")
            for dcx in range(NDC):
                V(lambda v, dcx=dcx: v.reduce_max(mm_sb[:, 2:3], ubuf(dcx), axis=AX.X),
                  reads=[("u", dcx)], writes=["mmtmp"])
                V(lambda v: v.tensor_max(mm_sb[:, 0:1], mm_sb[:, 0:1], mm_sb[:, 2:3]),
                  reads=["mmtmp", "mmacc"], writes=["mmacc"])
                V(lambda v, dcx=dcx: v.tensor_reduce(mm_sb[:, 2:3], ubuf(dcx), AX.X, ALU.min, negate=True),
                  reads=[("u", dcx)], writes=["mmtmp"])
                V(lambda v: v.tensor_max(mm_sb[:, 1:2], mm_sb[:, 1:2], mm_sb[:, 2:3]),
                  reads=["mmtmp", "mmacc"], writes=["mmacc"])
        TT(lambda t: t.transpose(psC[0:2, 0:128], mm_sb[:, 0:2], id128_sb[:]),
           reads=["mmacc", "id128"], writes=["psC"])
        V(lambda v: v.reduce_max(mm_sb[0:2, 4:5], psC[0:2, 0:128], axis=AX.X),
          reads=["psC"], writes=["mmred"])
        dmas(lambda s: s.dma_start(out=g1_sb[0:1, 0:2], in_=mm_sb[0:2, 4:5]),
             reads=["mmred"], writes=["e1row"])
        V(lambda v: v.memset(g1_sb[0:1, 2:8], 0.0), writes=["e1pad"])
        dmag(lambda g: g.dma_start(out=cc1i_h, in_=g1_sb[0:1, 0:8]),
             reads=["e1row", "e1pad"], writes=["cc1i"])
        p.step("gpsimd", lambda g: g.collective_compute(
            "AllGather", ALU.bypass, replica_groups=[list(range(NCORES))],
            ins=[cc1i_h.opt()], outs=[cc1o_h.opt()]),
            reads=["cc1i"], writes=["cc1o"], cc=True)
        dmag(lambda g: g.dma_start(out=gath_sb[0:2, 0:8], in_=cc1o_h.transpose([1, 0])[0:2], allow_slow_non_contiguous=True),
             reads=["cc1o"], writes=["e1g"])
        V(lambda v: v.reduce_max(gath_sb[0:2, 8:9], gath_sb[0:2, 0:8], axis=AX.X),
          reads=["e1g"], writes=["e1red"])
        dmas(lambda s: s.dma_start(out=g1_sb[0:1, 8:10], in_=gath_sb[0:2, 8:9]),
             reads=["e1red"], writes=["gmm"])

        dmas(lambda s: s.dma_start(out=arow(0, 1024), in_=xhead_h), writes=["xh"])
        V(lambda v: v.reduce_sum(asc(0), arow(0, 1024), axis=AX.X),
          reads=["xh"], writes=["dsc"])
        V(lambda v: v.tensor_scalar(asc(0), asc(0), float(1.0 / D), None, ALU.mult),
          reads=["dsc"], writes=["dsc"])
        S(lambda s: s.activation(dsq, arow(0, 1024), AF.Square),
          reads=["xh"], writes=["dsqr"])
        V(lambda v: v.reduce_sum(asc(1), dsq, axis=AX.X),
          reads=["dsqr", "dsc"], writes=["dsc"])
        V(lambda v: v.tensor_scalar(asc(1), asc(1), float(1.0 / D), None, ALU.mult),
          reads=["dsc"], writes=["dsc"])
        V(lambda v: v.tensor_mul(asc(2), asc(0), asc(0)), reads=["dsc"], writes=["dsc"])
        V(lambda v: v.tensor_sub(asc(1), asc(1), asc(2)), reads=["dsc"], writes=["dsc"])
        V(lambda v: v.tensor_scalar(asc(1), asc(1), LN_EPS, None, ALU.add),
          reads=["dsc"], writes=["dsc"])
        V(lambda v: v.reciprocal(asc(2), asc(1)), reads=["dsc"], writes=["dsc"])
        S(lambda s: s.activation(asc(3), asc(2), AF.Sqrt), reads=["dsc"], writes=["dsc"])
        V(lambda v: v.tensor_scalar(arow(0, 1024), arow(0, 1024), asc(0), None, ALU.subtract),
          reads=["xh", "dsc"], writes=["xh"])
        V(lambda v: v.tensor_scalar(arow(0, 1024), arow(0, 1024), asc(3), None, ALU.mult),
          reads=["xh", "dsc"], writes=["xh"])
        V(lambda v: v.tensor_add(asc(4), g1_sb[0:1, 8:9], g1_sb[0:1, 9:10]),
          reads=["gmm"], writes=["dsc"])
        V(lambda v: v.tensor_scalar(asc(4), asc(4), 1e-10, None, ALU.add),
          reads=["dsc"], writes=["dsc"])
        V(lambda v: v.reciprocal(asc(5), asc(4)), reads=["dsc"], writes=["dsc"])
        V(lambda v: v.tensor_scalar(arow(0, 1024), arow(0, 1024), g1_sb[0:1, 9:10], None, ALU.add),
          reads=["xh", "gmm"], writes=["xh"])
        V(lambda v: v.tensor_scalar(arow(0, 1024), arow(0, 1024), asc(5), None, ALU.mult),
          reads=["xh", "dsc"], writes=["xh"])
        for i, sc2 in enumerate((4, 16, 64, 256, 1024)):
            V(lambda v, sc2=sc2: v.reduce_sum(asc(6), arow(0, sc2), axis=AX.X),
              reads=["xh"], writes=["dsc"])
            V(lambda v, sc2=sc2: v.tensor_scalar(asc(6), asc(6), float(1.0 / sc2), None, ALU.mult),
              reads=["dsc"], writes=["dsc"])
            V(lambda v, sc2=sc2: v.tensor_scalar(dcmp[0:1, 0:sc2], arow(0, sc2), asc(6), None, ALU.is_gt),
              reads=["xh", "dsc"], writes=["dcmpr"])
            V(lambda v, sc2=sc2, i=i: v.reduce_sum(asc(16 + i), dcmp[0:1, 0:sc2], axis=AX.X),
              reads=["dcmpr"], writes=["dcnt"])
            V(lambda v, i=i: v.tensor_scalar(asc(16 + i), asc(16 + i), 1.0, None, ALU.max),
              reads=["dcnt"], writes=["dcnt"])
        S(lambda s: s.activation(bigA[0:1, 15376:15381], bigA[0:1, 15376:15381], AF.Ln),
          reads=["dcnt"], writes=["dly"])
        S(lambda s: s.activation(bigA[0:1, 15384:15389], g1_sb[0:1, 32:37], AF.Ln),
          reads=["lxrow"], writes=["dlx"])
        ly = bigA[0:1, 15376:15381]
        lx = bigA[0:1, 15384:15389]
        tmp5 = bigA[0:1, 15392:15397]
        V(lambda v: v.tensor_mul(tmp5, lx, ly), reads=["dly", "dlx"], writes=["d5"])
        V(lambda v: v.reduce_sum(asc(40), tmp5, axis=AX.X), reads=["d5"], writes=["dsxy"])
        V(lambda v: v.reduce_sum(asc(41), lx, axis=AX.X), reads=["dlx"], writes=["dsx"])
        V(lambda v: v.reduce_sum(asc(42), ly, axis=AX.X), reads=["dly"], writes=["dsy"])
        V(lambda v: v.tensor_mul(tmp5, lx, lx), reads=["dlx", "d5"], writes=["d5"])
        V(lambda v: v.reduce_sum(asc(43), tmp5, axis=AX.X), reads=["d5"], writes=["dsxx"])
        V(lambda v: v.tensor_scalar(asc(44), asc(40), 5.0, None, ALU.mult),
          reads=["dsxy"], writes=["dnum"])
        V(lambda v: v.tensor_mul(asc(45), asc(41), asc(42)),
          reads=["dsx", "dsy"], writes=["dt"])
        V(lambda v: v.tensor_sub(asc(44), asc(44), asc(45)),
          reads=["dnum", "dt"], writes=["dnum"])
        V(lambda v: v.tensor_scalar(asc(46), asc(43), 5.0, None, ALU.mult),
          reads=["dsxx"], writes=["dden"])
        V(lambda v: v.tensor_mul(asc(45), asc(41), asc(41)),
          reads=["dsx"], writes=["dt"])
        V(lambda v: v.tensor_sub(asc(46), asc(46), asc(45)),
          reads=["dden", "dt"], writes=["dden"])
        V(lambda v: v.reciprocal(asc(45), asc(46)), reads=["dden"], writes=["dt"])
        V(lambda v: v.tensor_mul(asc(47), asc(44), asc(45)),
          reads=["dnum", "dt"], writes=["dq"])
        V(lambda v: v.tensor_scalar(asc(47), asc(47), -1.0, 1.0, ALU.mult, ALU.max),
          reads=["dq"], writes=["dq"])
        V(lambda v: v.tensor_scalar(asc(48), asc(47), 2.0, None, ALU.min),
          reads=["dq"], writes=["dD"])
        V(lambda v: v.tensor_scalar(asc(49), asc(48), 0.2, 1.0, ALU.mult, ALU.add),
          reads=["dD"], writes=["dom"])
        V(lambda v: v.tensor_mul(asc(49), asc(49), asc(48)),
          reads=["dom", "dD"], writes=["dom"])
        V(lambda v: v.tensor_scalar(asc(50), asc(48), 0.5, 1.0, ALU.mult, ALU.add),
          reads=["dD"], writes=["da"])
        V(lambda v: v.tensor_scalar(asc(51), asc(48), 0.1, None, ALU.mult),
          reads=["dD"], writes=["db"])
        TT(lambda t: t.matmul(psB[:, 0:3], onesr[:], bigA[0:1, 15409:15412], start=True, stop=True),
           reads=["dom", "da", "db", "onesr"], writes=["psB"])
        V(lambda v: v.tensor_copy(bias_sb[:, 25:28], psB[:, 0:3]),
          reads=["psB"], writes=["dbc"])

        V(lambda v: v.tensor_copy(gtm_sb[:, 0:8], kid_i[:, :]), reads=["kid"], writes=["gt0"])
        V(lambda v: v.tensor_scalar(gtm_sb[:, 8:16], gtm_sb[:, 0:8], -1.0, 1024.0, ALU.mult, ALU.add),
          reads=["gt0"], writes=["gt1"])
        V(lambda v: v.tensor_tensor(gtm_sb[:, 0:8], gtm_sb[:, 0:8], gtm_sb[:, 8:16], ALU.min),
          reads=["gt0", "gt1"], writes=["gt0"])
        V(lambda v: v.tensor_scalar(gtm_sb[:, 0:8], gtm_sb[:, 0:8], float(1.0 / 1024.0), 1e-8, ALU.mult, ALU.add),
          reads=["gt0"], writes=["gt0"])
        V(lambda v: v.tensor_scalar(gtm_sb[:, 8:16], gtm_sb[:, 0:8], bias_sb[:, 26:27], None, ALU.mult),
          reads=["gt0", "dbc"], writes=["gt1"])
        V(lambda v: v.tensor_scalar(gtm_sb[:, 8:16], gtm_sb[:, 8:16], bias_sb[:, 25:26], None, ALU.add),
          reads=["gt1", "dbc"], writes=["gt1"])
        V(lambda v: v.tensor_scalar(gtm_sb[:, 16:24], gtm_sb[:, 8:16], PI, None, ALU.is_gt),
          reads=["gt1"], writes=["gt2"])
        V(lambda v: v.scalar_tensor_tensor(gtm_sb[:, 8:16], gtm_sb[:, 16:24], -TWO_PI,
                                           gtm_sb[:, 8:16], ALU.mult, ALU.add),
          reads=["gt2", "gt1"], writes=["gt1"])
        S(lambda s: s.activation(gtm_sb[:, 16:24], gtm_sb[:, 8:16], AF.Sin),
          reads=["gt1"], writes=["gt2"])
        S(lambda s: s.activation(gtm_sb[:, 24:32], gtm_sb[:, 0:8], AF.Square),
          reads=["gt0"], writes=["gt3"])
        V(lambda v: v.tensor_scalar(gtm_sb[:, 24:32], gtm_sb[:, 24:32], bias_sb[:, 27:28], None, ALU.mult),
          reads=["gt3", "dbc"], writes=["gt3"])
        V(lambda v: v.tensor_scalar(gtm_sb[:, 8:16], gtm_sb[:, 0:8], -1.0, None, ALU.mult),
          reads=["gt0"], writes=["gt1"])
        V(lambda v: v.tensor_scalar(gtm_sb[:, 8:16], gtm_sb[:, 8:16], bias_sb[:, 25:26], None, ALU.add),
          reads=["gt1", "dbc"], writes=["gt1"])
        V(lambda v: v.tensor_add(gtm_sb[:, 8:16], gtm_sb[:, 8:16], gtm_sb[:, 24:32]),
          reads=["gt1", "gt3"], writes=["gt1"])
        S(lambda s: s.activation(gtm_sb[:, 24:32], gtm_sb[:, 8:16], AF.Sin),
          reads=["gt1"], writes=["gt3"])
        V(lambda v: v.tensor_mul(bias_sb[:, 16:24], gtm_sb[:, 16:24], gtm_sb[:, 24:32]),
          reads=["gt2", "gt3"], writes=["fim"])
        S(lambda s: s.activation(gtm_sb[:, 24:32], gtm_sb[:, 8:16], AF.Sin,
                                 bias=bias_sb[:, 24:25], scale=-1.0),
          reads=["gt1", "hpi"], writes=["gt3"])
        V(lambda v: v.tensor_mul(bias_sb[:, 8:16], gtm_sb[:, 16:24], gtm_sb[:, 24:32]),
          reads=["gt2", "gt3"], writes=["fre"])
        V(lambda v: v.tensor_scalar(bias_sb[:, 8:16], bias_sb[:, 8:16], float(1.0 / 1024.0), None, ALU.mult),
          reads=["fre"], writes=["fre"])
        V(lambda v: v.tensor_scalar(bias_sb[:, 16:24], bias_sb[:, 16:24], float(1.0 / 1024.0), None, ALU.mult),
          reads=["fim"], writes=["fim"])
        V(lambda v: v.tensor_scalar(bias_sb[:, 36:44], bias_sb[:, 16:24], -1.0, None, ALU.mult),
          reads=["fim"], writes=["fimn"])

        CT = bigA[:, 0:1024]
        STb = bigA[:, 1024:2048]
        T2b = bigA[:, 2048:3072]
        T3b = bigA[:, 3072:4096]
        CTm = bigA[:, 4096:4224]
        STm = bigA[:, 4224:4352]
        for kc in range(NDC):
            krow = rowk_f[0:1, 128 * kc: 128 * kc + 128]
            for h in range(2):
                pu = psU0 if h == 0 else psU1
                pur = "psU%d" % h
                TT(lambda t, h=h, krow=krow, pu=pu: t.matmul(
                    pu[:, :], krow,
                    rowk_f[0:1, h * 512:(h + 1) * 512], start=True, stop=True),
                   reads=["rowk"], writes=[pur])
                V(lambda v, h=h, pu=pu: v.tensor_copy(
                    T3b[:, h * 512:(h + 1) * 512].bitcast(I32), pu[:, :]),
                  reads=[pur], writes=["t3"])
                V(lambda v, h=h: v.tensor_scalar(
                    T2b[:, h * 512:(h + 1) * 512].bitcast(I32),
                    T3b[:, h * 512:(h + 1) * 512].bitcast(I32), 1023, None, ALU.bitwise_and),
                  reads=["t3"], writes=["t2"])
                V(lambda v, h=h: v.tensor_copy(
                    CT[:, h * 512:(h + 1) * 512], T2b[:, h * 512:(h + 1) * 512].bitcast(I32)),
                  reads=["t2"], writes=["ct"])
            V(lambda v: v.tensor_scalar(T2b, CT, 512.0, None, ALU.is_ge),
              reads=["ct"], writes=["t2"])
            V(lambda v: v.tensor_scalar(T3b, CT, float(TWO_PI / 1024.0), None, ALU.mult),
              reads=["ct"], writes=["t3"])
            V(lambda v: v.scalar_tensor_tensor(STb, T2b, -TWO_PI, T3b, ALU.mult, ALU.add),
              reads=["t2", "t3"], writes=["st"])
            S(lambda s: s.activation(T2b, STb, AF.Abs), reads=["st"], writes=["t2"])
            S(lambda s: s.activation(CT, T2b, AF.Sin, bias=bias_sb[:, 24:25], scale=-1.0),
              reads=["t2", "hpi"], writes=["ct"])
            S(lambda s: s.activation(STb, STb, AF.Sin), reads=["st"], writes=["st"])
            V(lambda v: v.tensor_copy(CTm, CT[:, 0:128]), reads=["ct"], writes=["ctm"])
            V(lambda v: v.tensor_copy(STm, STb[:, 0:128]), reads=["st"], writes=["stm"])
            fre_c = bias_sb[:, 8 + kc: 9 + kc]
            fim_c = bias_sb[:, 16 + kc: 17 + kc]
            fimn_c = bias_sb[:, 36 + kc: 37 + kc]
            V(lambda v, a=fre_c: v.tensor_scalar(T3b, STb, a, None, ALU.mult),
              reads=["st", "fre"], writes=["t3"])
            V(lambda v, a=fim_c: v.scalar_tensor_tensor(T2b, CT, a, T3b, ALU.mult, ALU.add),
              reads=["ct", "t3", "fim"], writes=["t2"])
            V(lambda v, a=fre_c: v.tensor_scalar(T3b, CT, a, None, ALU.mult),
              reads=["ct", "fre"], writes=["t3"])
            V(lambda v, a=fimn_c: v.scalar_tensor_tensor(CT, STb, a, T3b, ALU.mult, ALU.add),
              reads=["st", "t3", "fimn"], writes=["ct"])
            for h in range(2):
                TT(lambda t, h=h, kc=kc: t.matmul(
                    psC[:, h * 512:(h + 1) * 512], CTm, CT[:, h * 512:(h + 1) * 512],
                    start=(kc == 0), stop=False),
                   reads=["ctm", "ct"], writes=["psC"])
                TT(lambda t, h=h, kc=kc: t.matmul(
                    psC[:, h * 512:(h + 1) * 512], STm, T2b[:, h * 512:(h + 1) * 512],
                    start=False, stop=(kc == NDC - 1)),
                   reads=["stm", "t2"], writes=["psC"])
        for h in range(2):
            V(lambda v, h=h: v.tensor_copy(bigA[:, 5120 + h * 512: 5120 + (h + 1) * 512],
                                           psC[:, h * 512:(h + 1) * 512]),
              reads=["psC"], writes=["strip0"])

    if "P2" in phases:
        strip = lambda j: bigA[:, 5120 + j * 128: 5120 + (j + 1) * 128]
        X1 = lambda ncc: bigA[:, 6144 + ncc * 512: 6144 + (ncc + 1) * 512]
        for it in range(NT512):
            t0 = it * 512
            emit_ln(512, lambda dcx, t0=t0: xc(dcx, t0, 512), ubuf, sqbuf, rowslot,
                    [("x", dcx) for dcx in range(NDC)],
                    [("u", dcx) for dcx in range(NDC)], "sq", "lnrow")
            for ncc in range(NDC):
                pu = psU0 if ncc % 2 == 0 else psU1
                pur = "psU%d" % (ncc % 2)
                for mc in range(NDC):
                    TT(lambda t, ncc=ncc, mc=mc, pu=pu: t.matmul(
                        pu[:, :], strip((ncc - mc) % 8), ubuf(mc),
                        start=(mc == 0), stop=(mc == NDC - 1)),
                       reads=["strip0", ("u", mc)], writes=[pur])
                V(lambda v, ncc=ncc, pu=pu: v.tensor_copy(X1(ncc), pu[:, :]),
                  reads=[pur], writes=[("x1", ncc)])
            for dcx in range(NDC):
                TT(lambda t, dcx=dcx: t.matmul(
                    psA[0:24, :], wel_sb[:, dcx * 24:(dcx + 1) * 24], X1(dcx),
                    start=(dcx == 0), stop=(dcx == NDC - 1)),
                   reads=["wel", ("x1", dcx)], writes=["psA"])
            S(lambda s: s.activation(bits_sb, psA[0:24, :], AF.Identity,
                                     bias=bias_sb[0:24, 32:33], scale=1.0),
              reads=["psA", "bel"], writes=["bits"])
            V(lambda v: v.tensor_scalar(bits_sb, bits_sb, 0.0, None, ALU.is_gt),
              reads=["bits"], writes=["bits"])
            TT(lambda t: t.matmul(psB[0:12, :], ht_sb[:, :], bits_sb,
                                  start=True, stop=True),
               reads=["htm", "bits"], writes=["psB"])
            V(lambda v: v.tensor_scalar(syn_sb, psB[0:12, :], 0.5, None, ALU.is_gt),
              reads=["psB"], writes=["syn"])
            TT(lambda t: t.matmul(psA[0:1, :], ones12[:], syn_sb, start=True, stop=True),
               reads=["ones12", "syn"], writes=["psA"])
            V(lambda v: v.tensor_scalar(fx1_row, psA[0:1, :], 0.5, None, ALU.is_gt),
              reads=["psA"], writes=["fx1"])
            V(lambda v: v.tensor_scalar(fx2_row, psA[0:1, :], 3.5, None, ALU.is_le),
              reads=["psA"], writes=["fx2"])
            V(lambda v: v.tensor_mul(fx1_row, fx1_row, fx2_row),
              reads=["fx1", "fx2"], writes=["fx1"])
            TT(lambda t: t.matmul(psB[0:12, :], onesr[0:1, 0:12], fx1_row,
                                  start=True, stop=True),
               reads=["onesr", "fx1"], writes=["psB"])
            V(lambda v: v.tensor_tensor(syn_sb, syn_sb, psB[0:12, :], ALU.mult),
              reads=["syn", "psB"], writes=["syn"])
            V(lambda v: v.tensor_scalar(onetwob, bits_sb[0:12, 0:512], -2.0, 1.0, ALU.mult, ALU.add),
              reads=["bits"], writes=["otb"])
            V(lambda v: v.tensor_mul(syn_sb, syn_sb, onetwob),
              reads=["syn", "otb"], writes=["syn"])
            V(lambda v: v.tensor_add(bits_sb[0:12, 0:512], bits_sb[0:12, 0:512], syn_sb),
              reads=["bits", "syn"], writes=["bits"])
            for dcx in range(NDC):
                TT(lambda t, dcx=dcx: t.matmul(
                    psB[:, :], wle_sb[:, dcx * 128:(dcx + 1) * 128], bits_sb,
                    start=True, stop=True),
                   reads=["wle", "bits"], writes=["psB"])
                S(lambda s, dcx=dcx: s.activation(aoutb, psB[:, :], AF.Identity,
                                                  bias=bias_sb[:, dcx:dcx + 1], scale=1.0),
                  reads=["psB", "ble"], writes=["aout"])
                V(lambda v, dcx=dcx: v.tensor_add(aoutb, aoutb, X1(dcx)),
                  reads=["aout", ("x1", dcx)], writes=["aout"])
                V(lambda v, dcx=dcx, t0=t0: v.tensor_add(xc(dcx, t0, 512), xc(dcx, t0, 512), aoutb),
                  reads=["aout", ("x", dcx)], writes=[("x", dcx)])

    def emit_ffn(call, fence=(), do_split=False):
        bigAb_ = bigA[:].bitcast(BF16)

        def emit_split_for(gi2, fdeps=()):
            if not do_split or gi2 >= 128:
                return
            e2, g2 = gi2 // 16, gi2 % 16
            s2 = gi2 % 4
            for wi, (wsrc, wdst, wtag, cbase, stq) in enumerate((
                    (w1_h, w1s_h, "w1s", s2 * 8192, dmaa),
                    (w2_h, w2s_h, "w2s", s2 * 8192 + 4096, dmag))):
                wr = "wst%d" % (wi + 1)
                for i in range(2):
                    fc = 2 * g2 + i
                    sl = (2 * wi + i) % 2
                    hi = bigAb_[:, cbase + i * 2048: cbase + i * 2048 + 1024]
                    lo = bigAb_[:, cbase + i * 2048 + 1024: cbase + i * 2048 + 2048]
                    dmas(lambda s, wsrc=wsrc, e2=e2, fc=fc, sl=sl: s.dma_start(
                        out=wk_sb[:, sl * 1024:(sl + 1) * 1024], in_=wsrc[e2, fc]),
                        writes=[("spin", sl)])
                    S(lambda s, hi=hi, sl=sl: s.activation(
                        hi, wk_sb[:, sl * 1024:(sl + 1) * 1024], AF.Identity),
                      reads=[("spin", sl)], writes=[(wr, s2)], deps=list(fdeps))
                    V(lambda v, hi=hi, lo=lo, sl=sl: v.tensor_sub(
                        lo, wk_sb[:, sl * 1024:(sl + 1) * 1024], hi),
                      reads=[("spin", sl), (wr, s2)], writes=[(wr, s2)])
                stq(lambda s, wdst=wdst, e2=e2, g2=g2, cbase=cbase: s.dma_start(
                    out=wdst[e2, 2 * g2: 2 * g2 + 2].transpose([1, 0, 2]),
                    in_=bigAb_[:, cbase: cbase + 4096].rearrange("p (i n) -> p i n", i=2)),
                    reads=[(wr, s2)],
                    writes=[(wtag, e2, 2 * g2), (wtag, e2, 2 * g2 + 1)])
        UHb = lambda dcx: bigB[:, 0:2048].bitcast(BF16)[:, dcx * 512:(dcx + 1) * 512]
        ULb = lambda dcx: bigB[:, 2048:4096].bitcast(BF16)[:, dcx * 512:(dcx + 1) * 512]
        ACC = lambda dcx: bigB[:, 4096 + dcx * 512: 4096 + (dcx + 1) * 512]
        GGF = bigB[:, 8192:8704]
        GGH = lambda i: bigB[:, 8704 + i * 256: 8704 + (i + 1) * 256].bitcast(BF16)
        GGL = lambda i: bigB[:, 9728 + i * 256: 9728 + (i + 1) * 256].bitcast(BF16)
        GBC = bigB[:, 10752:11264]
        SQF = bigB[:, 11264:11776]
        UTMP = bigB[:, 11776:12288]
        frow = lambda i: bigB[0:1, 12288 + i * 512: 12288 + (i + 1) * 512]
        bigAb = bigA[:].bitcast(BF16)

        def W1t(s_, i, hilo, dcx):
            base = (s_ * 4096 + i * 1024) * 2 + hilo * 1024
            return bigAb[:, base + dcx * 128: base + (dcx + 1) * 128]

        def W2t(s_, i, hilo, dcx):
            base = (s_ * 4096 + 2048 + i * 1024) * 2 + hilo * 1024
            return bigAb[:, base + dcx * 128: base + (dcx + 1) * 128]

        groups = [(e, g) for e in range(8) for g in range(16)]
        for tq in range(4):
            t0 = tq * 512

            def usplit(dcx):
                V(lambda v, dcx=dcx: v.tensor_copy(UHb(dcx), UTMP),
                  reads=["utmp"], writes=[("uh", dcx)])
                V(lambda v, dcx=dcx: v.tensor_sub(ULb(dcx), UTMP, UHb(dcx)),
                  reads=["utmp", ("uh", dcx)], writes=[("ul", dcx)])

            emit_ln(512, lambda dcx, t0=t0: xc(dcx, t0, 512), lambda dcx: UTMP,
                    SQF, frow,
                    [("x", dcx) for dcx in range(NDC)],
                    ["utmp"] * NDC, "fsq", "frow", post_chunk=usplit)
            for dcx in range(NDC):
                TT(lambda t, dcx=dcx: t.matmul(psA[0:8, :], rws_bf[:, dcx * 8:(dcx + 1) * 8],
                                               UHb(dcx), start=(dcx == 0), stop=False),
                   reads=["rws", ("uh", dcx)], writes=["psA"])
                TT(lambda t, dcx=dcx: t.matmul(psA[0:8, :], rws_bf[:, dcx * 8:(dcx + 1) * 8],
                                               ULb(dcx), start=False, stop=False),
                   reads=["rws", ("ul", dcx)], writes=["psA"])
                TT(lambda t, dcx=dcx: t.matmul(psA[0:8, :], rws_bf[:, 64 + dcx * 8: 64 + (dcx + 1) * 8],
                                               UHb(dcx), start=False, stop=(dcx == NDC - 1)),
                   reads=["rws", ("uh", dcx)], writes=["psA"])
            V(lambda v: v.tensor_copy(rl_sb[:, :], psA[0:8, :]), reads=["psA"], writes=["rl"])
            for b in range(4):
                TM = gtm_sb[:, b * 16: b * 16 + 8]
                m1c = gtm_sb[:, b * 16 + 8: b * 16 + 9]
                n1c = gtm_sb[:, b * 16 + 9: b * 16 + 10]
                m2c = gtm_sb[:, b * 16 + 10: b * 16 + 11]
                n2c = gtm_sb[:, b * 16 + 11: b * 16 + 12]
                pc = gtm_sb[:, b * 16 + 12: b * 16 + 13]
                ac = gtm_sb[:, b * 16 + 13: b * 16 + 14]
                g2c = gtm_sb[:, b * 16 + 14: b * 16 + 15]
                C1 = bigB[:, 11264 + b * 32 + 0: 11264 + b * 32 + 8]
                C2 = bigB[:, 11264 + b * 32 + 8: 11264 + b * 32 + 16]
                TMX = bigB[:, 11264 + b * 32 + 16: 11264 + b * 32 + 24]
                rg = "g%d" % b
                TT(lambda t, b=b: t.transpose(psB[:, 0:8], rl_sb[:, b * 128:(b + 1) * 128],
                                              id128_sb[0:8, 0:8]),
                   reads=["rl", "id128"], writes=["psB"])
                V(lambda v, TM=TM: v.tensor_copy(TM, psB[:, 0:8]), reads=["psB"], writes=[rg])
                V(lambda v, TM=TM, m1c=m1c: v.reduce_max(m1c, TM, axis=AX.X), reads=[rg], writes=[rg])
                V(lambda v, TM=TM, m1c=m1c, C1=C1: v.tensor_scalar(C1, TM, m1c, None, ALU.is_equal),
                  reads=[rg, "fsq"], writes=[rg, "fsq"])
                V(lambda v, C1=C1, n1c=n1c: v.reduce_sum(n1c, C1, axis=AX.X), reads=[rg, "fsq"], writes=[rg])
                V(lambda v, C1=C1, TM=TM, TMX=TMX: v.scalar_tensor_tensor(TMX, C1, BIG_NEG, TM, ALU.mult, ALU.add),
                  reads=[rg, "fsq"], writes=[rg, "fsq"])
                V(lambda v, TMX=TMX, m2c=m2c: v.reduce_max(m2c, TMX, axis=AX.X), reads=[rg, "fsq"], writes=[rg])
                V(lambda v, TMX=TMX, m2c=m2c, C2=C2: v.tensor_scalar(C2, TMX, m2c, None, ALU.is_equal),
                  reads=[rg, "fsq"], writes=[rg, "fsq"])
                V(lambda v, C2=C2, n2c=n2c: v.reduce_sum(n2c, C2, axis=AX.X), reads=[rg, "fsq"], writes=[rg])
                V(lambda v, n1c=n1c, pc=pc: v.tensor_scalar(pc, n1c, -1.0, 1.0, ALU.add, ALU.min),
                  reads=[rg], writes=[rg])
                V(lambda v, m1c=m1c, m2c=m2c, n1c=n1c: v.tensor_sub(n1c, m1c, m2c),
                  reads=[rg], writes=[rg])
                V(lambda v, n1c=n1c, pc=pc: v.tensor_scalar(n1c, n1c, pc, None, ALU.mult),
                  reads=[rg], writes=[rg])
                V(lambda v, n1c=n1c, m2c=m2c: v.tensor_add(n1c, n1c, m2c),
                  reads=[rg], writes=[rg])
                V(lambda v, m1c=m1c, n1c=n1c: v.tensor_sub(n1c, m1c, n1c),
                  reads=[rg], writes=[rg])
                S(lambda s, n1c=n1c, ac=ac: s.activation(ac, n1c, AF.Sigmoid),
                  reads=[rg], writes=[rg + "a"])
                V(lambda v, ac=ac, m1c=m1c: v.tensor_scalar(m1c, ac, -1.0, 1.0, ALU.mult, ALU.add),
                  reads=[rg + "a"], writes=[rg])
                V(lambda v, pc=pc: v.tensor_scalar(pc, pc, -1.0, 1.0, ALU.mult, ALU.add),
                  reads=[rg], writes=[rg])
                V(lambda v, m1c=m1c, pc=pc: v.tensor_mul(m1c, m1c, pc),
                  reads=[rg], writes=[rg])
                V(lambda v, n2c=n2c, m2c=m2c: v.reciprocal(m2c, n2c),
                  reads=[rg], writes=[rg])
                V(lambda v, m1c=m1c, m2c=m2c, g2c=g2c: v.tensor_mul(g2c, m1c, m2c),
                  reads=[rg], writes=[rg])
                V(lambda v, C2=C2, g2c=g2c: v.tensor_scalar(C2, C2, g2c, None, ALU.mult),
                  reads=[rg, "fsq"], writes=[rg, "fsq"])
                V(lambda v, C1=C1, ac=ac, C2=C2, TM=TM: v.scalar_tensor_tensor(TM, C1, ac, C2, ALU.mult, ALU.add),
                  reads=[rg, rg + "a", "fsq"], writes=[rg])
                TT(lambda t, TM=TM: t.transpose(psB[0:8, 0:128], TM, id128_sb[:, :]),
                   reads=[rg, "id128"], writes=["psB"])
                V(lambda v, b=b: v.tensor_copy(gates_sb[:, b * 128:(b + 1) * 128], psB[0:8, 0:128]),
                  reads=["psB"], writes=["gates"])
            for dcx in range(NDC):
                TT(lambda t, dcx=dcx: t.matmul(psB[:, :], b2l_sb[:, dcx * 128:(dcx + 1) * 128],
                                               gates_sb[:, :], start=True, stop=True),
                   reads=["b2l", "gates"], writes=["psB"])
                V(lambda v, dcx=dcx: v.tensor_copy(ACC(dcx), psB[:, :]),
                  reads=["psB"], writes=[("acc", dcx)])
            pending_down = []
            if tq == 0 and do_split:
                emit_split_for(0, fdeps=fence)
                emit_split_for(1, fdeps=fence)
            for gi, (e, g) in enumerate(groups):
                if tq == 0 and do_split:
                    emit_split_for(gi + 2, fdeps=fence if gi < 2 else ())
                slot = gi % 4
                fdeps = list(fence) if (tq == 0 and gi < 4 and not do_split) else []
                if not (tq == 0 and do_split):
                    dmas(lambda s, e=e, g=g, slot=slot: s.dma_start(
                        out=bigAb[:, slot * 8192: slot * 8192 + 4096].rearrange("p (i n) -> p i n", i=2),
                        in_=w1s_h[e, 2 * g: 2 * g + 2].transpose([1, 0, 2])),
                        writes=[("wst1", slot)], deps=fdeps,
                        reads=[("w1s", e, 2 * g + i) for i in range(2)])
                    dmas(lambda s, e=e, g=g, slot=slot: s.dma_start(
                        out=bigAb[:, slot * 8192 + 4096: slot * 8192 + 8192].rearrange("p (i n) -> p i n", i=2),
                        in_=w2s_h[e, 2 * g: 2 * g + 2].transpose([1, 0, 2])),
                        writes=[("wst2", slot)],
                        deps=fdeps,
                        reads=[("w2s", e, 2 * g + i) for i in range(2)])
                if g == 0:
                    TT(lambda t, e=e: t.matmul(psB[:, :], sel_sb[:, e * 128:(e + 1) * 128],
                                               gates_sb[:, :], start=True, stop=True),
                       reads=["sel", "gates"], writes=["psB"])
                    V(lambda v: v.tensor_copy(GBC, psB[:, :]), reads=["psB"], writes=["gbc"])
                for i in range(2):
                    fc = 2 * g + i
                    gsl = 2 * (gi % 2) + i
                    pu = psU0 if (i % 2 == 0) else psU1
                    pur = "psU%d" % (i % 2)
                    for dcx in range(NDC):
                        TT(lambda t, i=i, dcx=dcx, slot=slot, pu=pu: t.matmul(
                            pu[:, :], W1t(slot, i, 0, dcx), UHb(dcx),
                            start=(dcx == 0), stop=False),
                           reads=[("wst1", slot), ("uh", dcx)], writes=[pur])
                        TT(lambda t, i=i, dcx=dcx, slot=slot, pu=pu: t.matmul(
                            pu[:, :], W1t(slot, i, 0, dcx), ULb(dcx),
                            start=False, stop=False),
                           reads=[("wst1", slot), ("ul", dcx)], writes=[pur])
                        TT(lambda t, i=i, dcx=dcx, slot=slot, pu=pu: t.matmul(
                            pu[:, :], W1t(slot, i, 1, dcx), UHb(dcx),
                            start=False, stop=(dcx == NDC - 1)),
                           reads=[("wst1", slot), ("uh", dcx)], writes=[pur])
                    S(lambda s, gsl=gsl, e=e, fc=fc, pu=pu: s.activation(
                        GGF, pu[:, :], AF.Gelu,
                        bias=b1_sb[:, e * 32 + fc: e * 32 + fc + 1], scale=1.0),
                      reads=[pur, "b1"], writes=["ggf"])
                    V(lambda v: v.tensor_mul(GGF, GGF, GBC),
                      reads=["ggf", "gbc"], writes=["ggf"])
                    S(lambda s, gsl=gsl: s.activation(GGH(gsl), GGF, AF.Identity),
                      reads=["ggf"], writes=[("ggh", gsl)])
                    V(lambda v, gsl=gsl: v.tensor_sub(GGL(gsl), GGF, GGH(gsl)),
                      reads=["ggf", ("ggh", gsl)], writes=[("ggl", gsl)])
                def emit_down(slot=slot, gp=gi % 2):
                    for dcx in range(NDC):
                        pd = psD0 if (dcx % 2 == 0) else psD1
                        pdr = "psD%d" % (dcx % 2)
                        for i in range(2):
                            gsl2 = 2 * gp + i
                            TT(lambda t, i=i, gsl2=gsl2, dcx=dcx, slot=slot, pd=pd: t.matmul(
                                pd[:, :], W2t(slot, i, 0, dcx), GGH(gsl2),
                                start=(i == 0), stop=False),
                               reads=[("wst2", slot), ("ggh", gsl2)], writes=[pdr])
                            TT(lambda t, i=i, gsl2=gsl2, dcx=dcx, slot=slot, pd=pd: t.matmul(
                                pd[:, :], W2t(slot, i, 0, dcx), GGL(gsl2),
                                start=False, stop=False),
                               reads=[("wst2", slot), ("ggl", gsl2)], writes=[pdr])
                            TT(lambda t, i=i, gsl2=gsl2, dcx=dcx, slot=slot, pd=pd: t.matmul(
                                pd[:, :], W2t(slot, i, 1, dcx), GGH(gsl2),
                                start=False, stop=(i == 1)),
                               reads=[("wst2", slot), ("ggh", gsl2)], writes=[pdr])
                        V(lambda v, dcx=dcx, pd=pd: v.tensor_add(ACC(dcx), ACC(dcx), pd[:, :]),
                          reads=[pdr, ("acc", dcx)], writes=[("acc", dcx)])
                pending_down.append(emit_down)
                if len(pending_down) > 1:
                    pending_down.pop(0)()
            while pending_down:
                pending_down.pop(0)()
            for dcx in range(NDC):
                V(lambda v, dcx=dcx, t0=t0: v.tensor_add(xc(dcx, t0, 512), xc(dcx, t0, 512), ACC(dcx)),
                  reads=[("acc", dcx), ("x", dcx)], writes=[("x", dcx)])

    if "P3" in phases:
        emit_ffn(0, do_split=True, fence=p.fence(["ct", "st", "t2", "t3", "ctm", "stm", "rowk", "rowi", "strip0",
                "lnrow", "xh", "dsc", "dly", "dlx", "d5", "dsxy", "dsx", "dsy",
                "dsxx", "dnum", "dden", "dt", "dq", "dD", "dom", "da", "db",
                "dcnt", "fx1"] + [("x1", i) for i in range(NDC)]))

    if "P4" in phases:
        CHB = bigB[:, 0:2048]
        HT2 = lambda dcx: bigB[:, 2048 + dcx * 256: 2048 + (dcx + 1) * 256]
        RQ = bigB[:, 8192:9216]
        T1Q = bigB[:, 9216:10240]
        SQ4 = bigB[:, 10240:10752]
        VT4 = bigB[:, 6144:6656]
        hrow1 = lambda i: bigB[0:1, 11264 + i * 256: 11264 + (i + 1) * 256]
        hrow2 = lambda i: bigB[0:1, 12288 + i * 256: 12288 + (i + 1) * 256]
        ONES2K = bigB[:, 4096:6144]
        V(lambda v: v.memset(ONES2K, 1.0), writes=["ones2k"])
        NT256 = TC // 256

        def emit_h_tile(it):
            t0 = it * 256
            emit_ln(256, lambda dcx, t0=t0: xc(dcx, t0, 256), HT2, SQ4, hrow1,
                    [("x", dcx) for dcx in range(NDC)],
                    [("h", dcx) for dcx in range(NDC)], "p4sq", "hrow1")
            emit_ln(256, HT2, HT2, SQ4, hrow2,
                    [("h", dcx) for dcx in range(NDC)],
                    [("h", dcx) for dcx in range(NDC)], "p4sq", "hrow2")
            for dcx in range(NDC):
                V(lambda v, dcx=dcx: v.tensor_scalar(
                    HT2(dcx), HT2(dcx), battn_sb[:, 24 + dcx:25 + dcx],
                    battn_sb[:, 32 + dcx:33 + dcx], ALU.mult, ALU.add),
                  reads=[("h", dcx), "battn"], writes=[("h", dcx)])

        def make_stream(src_h, tag, pos):
            state = {"k": 0}

            def prefetch():
                k = state["k"]
                if k < len(pos):
                    _, dco2 = pos[k]
                    slot = k % 2
                    dmas(lambda s, dco2=dco2, slot=slot: s.dma_start(
                        out=wk_sb[:, slot * 1024:(slot + 1) * 1024], in_=src_h[dco2]),
                        writes=[(tag, slot)])
                    state["k"] += 1
            return prefetch

        pos = [(it, dco) for it in range(NT256) for dco in range(NDC)]
        pf1 = make_stream(wkb_h, "wkS", pos)
        pf1(); pf1()
        for it in range(NT256):
            t0 = it * 256
            emit_h_tile(it)
            if dbg and it == 6 and not dram.get("dbgH"):
                dram["dbgH"] = nc.dram_tensor("dbgH", [128, 2048], F32, kind="ExternalOutput").ap()
                dmas(lambda s: s.dma_start(out=dram["dbgH"], in_=bigB[:, 2048:4096]),
                     reads=[("h", i) for i in range(NDC)], writes=["dbgH"])
            for dco in range(NDC):
                slot = (it * NDC + dco) % 2
                pu = psU0 if dco % 2 == 0 else psU1
                pur = "psU%d" % (dco % 2)
                for dcx in range(NDC):
                    TT(lambda t, dcx=dcx, slot=slot, pu=pu: t.matmul(
                        pu[:, 0:256], wk_sb[:, slot * 1024 + dcx * 128: slot * 1024 + (dcx + 1) * 128],
                        HT2(dcx), start=(dcx == 0), stop=(dcx == NDC - 1)),
                       reads=[("wkS", slot), ("h", dcx)], writes=[pur])
                S(lambda s, dco=dco, t0=t0, pu=pu: s.activation(
                    bigA[:, dco * TC + t0: dco * TC + t0 + 256], pu[:, 0:256],
                    AF.Identity, bias=battn_sb[:, dco:dco + 1], scale=1.0),
                  reads=[pur, "battn"], writes=[("keys", dco)])
                pf1()
        if dbg:
            dbgK_h = nc.dram_tensor("dbgK", [128, 16384], F32, kind="ExternalOutput").ap()
            dmas(lambda s: s.dma_start(out=dbgK_h, in_=bigA[:]),
                 reads=[("keys", i) for i in range(NDC)], writes=["dbgK"])
        pf2 = make_stream(wkab_h, "wkS", pos)
        pf2(); pf2()
        for it in range(NT256):
            t0 = it * 256
            for dco in range(NDC):
                slot = (it * NDC + dco) % 2
                pd = psD0 if dco % 2 == 0 else psD1
                pdr = "psD%d" % (dco % 2)
                for dcx in range(NDC):
                    TT(lambda t, dcx=dcx, t0=t0, slot=slot, pd=pd: t.matmul(
                        pd[:, 0:256], wk_sb[:, slot * 1024 + dcx * 128: slot * 1024 + (dcx + 1) * 128],
                        bigA[:, dcx * TC + t0: dcx * TC + t0 + 256],
                        start=(dcx == 0), stop=(dcx == NDC - 1)),
                       reads=[("wkS", slot), ("keys", dcx)], writes=[pdr])
                S(lambda s, dco=dco, pd=pd: s.activation(
                    CHB[:, dco * 256:(dco + 1) * 256], pd[:, 0:256], AF.Exp,
                    bias=battn_sb[:, 40 + dco:41 + dco], scale=float(1.0 / 32.0)),
                  reads=[pdr, "bka32"], writes=["chb"])
                pf2()
            dmas(lambda s, it=it: s.dma_start(
                out=ew_h[:, :, it * 256:(it + 1) * 256].transpose([1, 0, 2]),
                in_=CHB.rearrange("p (c t) -> p c t", c=NDC)),
                reads=["chb"], writes=["ewh%d" % it])
        for dcx in range(NDC):
            dmas(lambda s, dcx=dcx: s.dma_start(out=CHB, in_=ew_h[dcx]),
                 reads=["ewh%d" % it2 for it2 in range(NT256)], writes=["chb"])
            V(lambda v: v.tensor_tensor_scan(CHB, ONES2K, CHB, 0.0, ALU.mult, ALU.add),
              reads=["chb", "ones2k"], writes=["chb"])
            V(lambda v, dcx=dcx: v.tensor_copy(tot_sb[:, dcx:dcx + 1], CHB[:, 2047:2048]),
              reads=["chb"], writes=["tot1"])
            dmas(lambda s, dcx=dcx: s.dma_start(out=ew_h[dcx], in_=CHB),
                 reads=["chb"], writes=["csh%d" % dcx])
        dmag(lambda g: g.dma_start(out=cc2i_h, in_=tot_sb[:, 0:8]),
             reads=["tot1"], writes=["cc2i"])
        p.step("gpsimd", lambda g: g.collective_compute(
            "AllGather", ALU.bypass, replica_groups=[list(range(NCORES))],
            ins=[cc2i_h.opt()], outs=[cc2o_h.opt()]),
            reads=["cc2i"], writes=["cc2o"], cc=True)
        dmag(lambda g: g.dma_start(out=gath_sb[:].rearrange("p (r c) -> p r c", r=8),
                                   in_=cc2o_h.transpose([1, 0, 2])),
             reads=["cc2o"], writes=["gath"])
        V(lambda v: v.memset(tot_sb[:, 8:16], 0.0), writes=["carry1"])
        for r8 in range(8):
            V(lambda v, r8=r8: v.scalar_tensor_tensor(
                tot_sb[:, 32:40], gath_sb[:, r8 * 8:(r8 + 1) * 8], mexc_sb[:, r8:r8 + 1],
                tot_sb[:, 8:16], ALU.mult, ALU.add),
              reads=["gath", "mexc", "carry1"], writes=["ctmp"])
            V(lambda v: v.tensor_copy(tot_sb[:, 8:16], tot_sb[:, 32:40]),
              reads=["ctmp"], writes=["carry1"])
        for dcx in range(NDC):
            dmas(lambda s, dcx=dcx: s.dma_start(out=CHB, in_=ew_h[dcx]),
                 reads=["csh%d" % dcx, "chb"], writes=["chb"])
            V(lambda v, dcx=dcx: v.tensor_scalar(CHB, CHB, tot_sb[:, 8 + dcx:9 + dcx], None, ALU.add),
              reads=["chb", "carry1"], writes=["chb"])
            for q in range(2):
                q0 = q * 1024
                V(lambda v, q0=q0: v.reciprocal(RQ, CHB[:, q0:q0 + 1024]),
                  reads=["chb"], writes=["rq"])
                if q == 0:
                    V(lambda v, dcx=dcx: v.tensor_scalar(
                        T1Q[:, 0:1], RQ[:, 0:1], tot_sb[:, 8 + dcx:9 + dcx], None, ALU.mult),
                      reads=["rq", "carry1"], writes=["t1q"])
                    V(lambda v: v.tensor_mul(T1Q[:, 1:1024], CHB[:, 0:1023], RQ[:, 1:1024]),
                      reads=["chb", "rq"], writes=["t1q"])
                else:
                    V(lambda v, q0=q0: v.tensor_mul(T1Q, CHB[:, q0 - 1:q0 + 1023], RQ),
                      reads=["chb", "rq"], writes=["t1q"])
                V(lambda v: v.tensor_scalar(T1Q, T1Q, -1.0, 1.0, ALU.mult, ALU.add),
                  reads=["t1q"], writes=["t1q"])
                V(lambda v, dcx=dcx, q0=q0: v.tensor_mul(
                    bigA[:, dcx * TC + q0: dcx * TC + q0 + 1024],
                    bigA[:, dcx * TC + q0: dcx * TC + q0 + 1024], T1Q),
                  reads=["t1q", ("keys", dcx)], writes=[("keys", dcx)])
        for dcx in range(NDC):
            V(lambda v, dcx=dcx: v.tensor_tensor_scan(
                bigA[:, dcx * TC: dcx * TC + TC], ONES2K, bigA[:, dcx * TC: dcx * TC + TC],
                0.0, ALU.mult, ALU.add),
              reads=[("keys", dcx), "ones2k"], writes=[("keys", dcx)])
            V(lambda v, dcx=dcx: v.tensor_copy(tot_sb[:, 16 + dcx:17 + dcx],
                                               bigA[:, dcx * TC + TC - 1: dcx * TC + TC]),
              reads=[("keys", dcx)], writes=["tot2"])
        dmag(lambda g: g.dma_start(out=cc3i_h, in_=tot_sb[:, 16:24]),
             reads=["tot2"], writes=["cc3i"])
        p.step("gpsimd", lambda g: g.collective_compute(
            "AllGather", ALU.bypass, replica_groups=[list(range(NCORES))],
            ins=[cc3i_h.opt()], outs=[cc3o_h.opt()]),
            reads=["cc3i"], writes=["cc3o"], cc=True)
        dmag(lambda g: g.dma_start(out=gath_sb[:].rearrange("p (r c) -> p r c", r=8),
                                   in_=cc3o_h.transpose([1, 0, 2])),
             reads=["cc3o"], writes=["gath"])
        V(lambda v: v.memset(tot_sb[:, 24:32], 0.0), writes=["carry2"])
        for r8 in range(8):
            V(lambda v, r8=r8: v.scalar_tensor_tensor(
                tot_sb[:, 32:40], gath_sb[:, r8 * 8:(r8 + 1) * 8], mexc_sb[:, r8:r8 + 1],
                tot_sb[:, 24:32], ALU.mult, ALU.add),
              reads=["gath", "mexc", "carry2"], writes=["ctmp"])
            V(lambda v: v.tensor_copy(tot_sb[:, 24:32], tot_sb[:, 32:40]),
              reads=["ctmp"], writes=["carry2"])
        for dcx in range(NDC):
            V(lambda v, dcx=dcx: v.tensor_scalar(
                bigA[:, dcx * TC: dcx * TC + TC], bigA[:, dcx * TC: dcx * TC + TC],
                tot_sb[:, 24 + dcx:25 + dcx], None, ALU.add),
              reads=[("keys", dcx), "carry2"], writes=[("keys", dcx)])
        pf3 = make_stream(wvb_h, "wkS", pos)
        pf3(); pf3()
        for it in range(NT256):
            t0 = it * 256
            emit_h_tile(it)
            if dbg and it == 6 and not dram.get("dbgH"):
                dram["dbgH"] = nc.dram_tensor("dbgH", [128, 2048], F32, kind="ExternalOutput").ap()
                dmas(lambda s: s.dma_start(out=dram["dbgH"], in_=bigB[:, 2048:4096]),
                     reads=[("h", i) for i in range(NDC)], writes=["dbgH"])
            for dco in range(NDC):
                slot = (it * NDC + dco) % 2
                pu = psU0 if dco % 2 == 0 else psU1
                pur = "psU%d" % (dco % 2)
                for dcx in range(NDC):
                    TT(lambda t, dcx=dcx, slot=slot, pu=pu: t.matmul(
                        pu[:, 0:256], wk_sb[:, slot * 1024 + dcx * 128: slot * 1024 + (dcx + 1) * 128],
                        HT2(dcx), start=(dcx == 0), stop=(dcx == NDC - 1)),
                       reads=[("wkS", slot), ("h", dcx)], writes=[pur])
                S(lambda s, dco=dco, pu=pu: s.activation(
                    VT4[:, 0:256], pu[:, 0:256], AF.Identity,
                    bias=battn_sb[:, 16 + dco:17 + dco], scale=1.0),
                  reads=[pur, "battn"], writes=["vtmp"])
                V(lambda v, dco=dco, t0=t0: v.tensor_mul(
                    VT4[:, 0:256], VT4[:, 0:256],
                    bigA[:, dco * TC + t0: dco * TC + t0 + 256]),
                  reads=["vtmp", ("keys", dco)], writes=["vtmp"])
                V(lambda v, dco=dco, t0=t0: v.tensor_add(
                    xc(dco, t0, 256), xc(dco, t0, 256), VT4[:, 0:256]),
                  reads=["vtmp", ("x", dco)], writes=[("x", dco)])
                pf3()

    if "P5" in phases:
        emit_ffn(1, do_split=("P3" not in phases), fence=p.fence(["ct", "st", "t2", "t3", "ctm", "stm", "rowk", "rowi", "strip0",
                "lnrow", "xh", "dsc", "dly", "dlx", "d5", "dsxy", "dsx", "dsy",
                "dsxx", "dnum", "dden", "dt", "dq", "dD", "dom", "da", "db",
                "dcnt", "fx1"] + [("x1", i) for i in range(NDC)]
            + [("keys", i) for i in range(NDC)]))

    dmas(lambda s: s.dma_start(
        out=yT_h.rearrange("(c p) t -> c p t", p=128).transpose([1, 0, 2]),
        in_=x_sb[:].rearrange("p (c t) -> p c t", c=NDC)),
        reads=[("x", dcx) for dcx in range(NDC)], writes=["yout"])
    if dbg:
        dbgA_h = nc.dram_tensor("dbgA", [128, 16384], F32, kind="ExternalOutput").ap()
        dbgB_h = nc.dram_tensor("dbgB", [128, 13312], F32, kind="ExternalOutput").ap()
        dbgr_h = nc.dram_tensor("dbgr", [1, 64], F32, kind="ExternalOutput").ap()
        dbgc_h = nc.dram_tensor("dbgc", [128, 48], F32, kind="ExternalOutput").ap()
        allreg = list(p.region_w.keys())
        dmas(lambda s: s.dma_start(out=dbgA_h, in_=bigA[:]), reads=allreg, writes=["dbg1"])
        dmas(lambda s: s.dma_start(out=dbgB_h, in_=bigB[:]), reads=[], writes=["dbg2"])
        dmas(lambda s: s.dma_start(out=dbgr_h, in_=g1_sb[:]), reads=[], writes=["dbg3"])
        dmas(lambda s: s.dma_start(out=dbgc_h, in_=bias_sb[:]), reads=[], writes=["dbg4"])
        dbgt_h = nc.dram_tensor("dbgt", [128, 40], F32, kind="ExternalOutput").ap()
        dmas(lambda s: s.dma_start(out=dbgt_h, in_=tot_sb[:]), reads=[], writes=["dbg5"])

    p.emit(nc, block, sems)
    es.close()
    return nc


def _prep_inputs(inputs):
    f32 = np.float32
    x = np.asarray(inputs["L0_attn_a_x"], f32).reshape(T, D)
    Wel = np.asarray(inputs["L0_attn_a_W_el"], f32)
    bel = np.asarray(inputs["L0_attn_a_b_el"], f32).reshape(24, 1)
    Wle = np.asarray(inputs["L0_attn_a_W_le"], f32)
    ble = np.ascontiguousarray(np.asarray(inputs["L0_attn_a_b_le"], f32).reshape(8, 128).T)
    A = np.array([[1, 1, 1, 0, 1, 1, 1, 0, 0, 0, 1, 1], [1, 1, 0, 1, 1, 1, 0, 0, 0, 1, 1, 0],
                  [1, 0, 1, 1, 1, 0, 0, 0, 1, 1, 0, 1], [0, 1, 1, 1, 0, 0, 0, 1, 1, 0, 1, 1],
                  [0, 0, 0, 0, 1, 1, 1, 1, 1, 1, 1, 1], [1, 1, 1, 1, 0, 0, 0, 0, 0, 0, 0, 0],
                  [0, 0, 0, 0, 1, 1, 1, 1, 0, 0, 0, 0], [1, 1, 0, 0, 1, 1, 0, 0, 1, 1, 0, 0],
                  [1, 0, 1, 0, 1, 0, 1, 0, 1, 0, 1, 0], [0, 1, 0, 1, 0, 1, 0, 1, 0, 1, 0, 1],
                  [1, 0, 0, 1, 1, 0, 0, 1, 1, 0, 0, 1], [0, 1, 1, 0, 0, 1, 1, 0, 0, 1, 1, 0]],
                 dtype=f32)
    H = np.concatenate([A.T, np.eye(12, dtype=f32)], axis=1)
    HT = np.ascontiguousarray(H.T)

    def colblocks(W):
        W4 = W.reshape(8, 128, 8, 128)
        return np.ascontiguousarray(W4.transpose(0, 3, 2, 1))

    wkb = colblocks(np.asarray(inputs["L0_attn_b_Wk"], f32)).reshape(8, 128, 1024)
    wkab = colblocks(np.asarray(inputs["L0_attn_b_Wka"], f32)).reshape(8, 128, 1024)
    wvb = colblocks(np.asarray(inputs["L0_attn_b_Wv"], f32)).reshape(8, 128, 1024)

    def ppcols(v):
        return np.ascontiguousarray(np.asarray(v, f32).reshape(8, 128).T)

    battn = np.concatenate([ppcols(inputs["L0_attn_b_bk"]), ppcols(inputs["L0_attn_b_bka"]),
                            ppcols(inputs["L0_attn_b_bv"]), ppcols(inputs["L0_attn_b_gamma"]),
                            ppcols(inputs["L0_attn_b_beta"])], axis=1)
    wel_blk = np.ascontiguousarray(Wel.reshape(8, 128, 24))
    rwb = np.ascontiguousarray(np.asarray(inputs["L0_ffn_router_w"], f32).T.reshape(8, 128, 8))
    b1 = np.asarray(inputs["L0_ffn_b1"], f32)
    b1v = np.ascontiguousarray(b1.reshape(8, 32, 128).transpose(2, 0, 1).reshape(128, 256))
    b2l = np.ascontiguousarray(np.asarray(inputs["L0_ffn_b2"], f32))
    w1 = np.asarray(inputs["L0_ffn_w1"], f32)
    w2 = np.asarray(inputs["L0_ffn_w2"], f32)
    w1b = np.ascontiguousarray(
        w1.reshape(8, 32, 128, 8, 128).transpose(0, 1, 4, 3, 2).reshape(8, 32, 128, 1024))
    w2b = np.ascontiguousarray(
        w2.reshape(8, 8, 128, 32, 128).transpose(0, 3, 4, 1, 2).reshape(8, 32, 128, 1024))
    sel = np.zeros((8, 8 * 128), f32)
    for e in range(8):
        sel[e, e * 128:(e + 1) * 128] = 1.0
    id128 = np.eye(128, dtype=f32)
    lxc = np.array([[2, 4, 8, 16, 32, 1, 1, 1]], dtype=f32)
    xhead = np.ascontiguousarray(x[0:1, :])

    shared = dict(wel=wel_blk, bel=bel, wle=Wle, ble=ble, htm=HT,
                  wkb=wkb, wkab=wkab, wvb=wvb, battn=battn, rwb=rwb,
                  b1v=b1v, b2l=b2l, sel=sel, id128=id128,
                  w1b=w1b, w2b=w2b, lxc=lxc, xhead=xhead)
    in_maps = []
    for j in range(NCORES):
        sh = x[j * TC:(j + 1) * TC, :]
        m = dict(shared)
        m["xT"] = np.ascontiguousarray(sh.T)
        m["mexc"] = np.broadcast_to(
            (np.arange(8) < j).astype(f32), (128, 8)).copy()
        in_maps.append(m)
    return in_maps


def kernel(**inputs):
    phases = inputs.pop("_phases", ALL_PHASES)
    dbg = inputs.pop("_dbg", False)
    in_maps = _prep_inputs(inputs)
    nc = build_nc(phases=phases, dbg=dbg)
    res = run_bass_kernel_spmd(nc, in_maps, core_ids=list(range(NCORES)))
    out = np.empty((T, D), np.float32)
    for j in range(NCORES):
        out[j * TC:(j + 1) * TC, :] = res.results[j]["yT"].T
    out = out.reshape(1, T, D)
    if dbg:
        return out, res
    return out


# version 06
# speedup vs baseline: 1.0933x; 1.0154x over previous
import numpy as np
import concourse.bass as bass
import concourse.mybir as mybir
from concourse.bass_utils import run_bass_kernel_spmd

F32 = mybir.dt.float32
BF16 = mybir.dt.bfloat16
I32 = mybir.dt.int32
AF = mybir.ActivationFunctionType
ALU = mybir.AluOpType
AX = mybir.AxisListType

NCORES = 8
T = 16384
TC = T // NCORES
D = 1024
NDC = 8
PI = float(np.pi)
TWO_PI = float(2 * np.pi)
HALF_PI = float(np.pi / 2)
LN_EPS = 1e-5
BIG_NEG = -1.0e30

ALL_PHASES = ("P1", "P2", "P3", "P4", "P5")
P4_STOP = 99


class Prog:
    QUEUES = ("qs", "qg", "qa")

    def __init__(self):
        self.steps = []
        self.region_w = {}
        self.region_r = {}

    def step(self, eng, fn, reads=(), writes=(), dma=None, cc=False, deps=()):
        dep = set(d for d in deps if d is not None)
        for rg in reads:
            w = self.region_w.get(rg)
            if w is not None:
                dep.add(w)
        for rg in writes:
            w = self.region_w.get(rg)
            if w is not None:
                dep.add(w)
            dep.update(self.region_r.get(rg, ()))
        sid = len(self.steps)
        counter = "cc" if cc else (dma if dma is not None else eng)
        self.steps.append([eng, fn, counter, sorted(dep)])
        for rg in reads:
            self.region_r.setdefault(rg, set()).add(sid)
        for rg in writes:
            self.region_w[rg] = sid
            self.region_r[rg] = set()
        return sid

    def fence(self, regions):
        out = set()
        for rg in regions:
            w = self.region_w.get(rg)
            if w is not None:
                out.add(w)
            out.update(self.region_r.get(rg, ()))
        return sorted(out)

    def emit(self, nc, block, sems):
        steps = self.steps
        needed = set()
        for st in steps:
            needed.update(st[3])
        ordinal = [None] * len(steps)
        counts = {}
        for sid, st in enumerate(steps):
            c = st[2]
            if sid in needed or c == "cc" or c in self.QUEUES:
                ordinal[sid] = counts.get(c, 0)
                counts[c] = ordinal[sid] + 1
        totals = {c: k * (16 if c in self.QUEUES else 1) for c, k in counts.items()}

        def wait_of(sid):
            c = steps[sid][2]
            v = ordinal[sid] + 1
            return c, (v * 16 if c in self.QUEUES else v)

        def emit_engine(eng_name, eng_obj):
            last = {}
            for sid, st in enumerate(steps):
                issuer, fn, counter, deps = st
                if issuer != eng_name:
                    continue
                for d in deps:
                    dc_, dv = wait_of(d)
                    if (eng_name == "tensor" and dc_ == "tensor"
                            and steps[d][2] == "tensor"):
                        continue
                    if last.get(dc_, 0) < dv:
                        eng_obj.wait_ge(sems[dc_], dv)
                        last[dc_] = dv
                inst = fn(eng_obj)
                if ordinal[sid] is not None:
                    inst.then_inc(sems[counter], 16 if counter in self.QUEUES else 1)

        @block.sync
        def _(sync):
            emit_engine("sync", sync)
            for c in sorted(totals):
                if c != "sync" and totals[c]:
                    sync.wait_ge(sems[c], totals[c])

        @block.scalar
        def _(scalar):
            emit_engine("scalar", scalar)

        @block.vector
        def _(vector):
            emit_engine("vector", vector)

        @block.tensor
        def _(tensor):
            emit_engine("tensor", tensor)

        @block.gpsimd
        def _(gpsimd):
            emit_engine("gpsimd", gpsimd)


def build_nc(phases=ALL_PHASES, dbg=False):
    import contextlib
    nc = bass.Bass("TRN2", target_bir_lowering=False)
    dram = {}

    def din(name, shape, dtype=F32):
        dram[name] = nc.dram_tensor(name, shape, dtype, kind="ExternalInput").ap()
        return dram[name]

    xT_h = din("xT", [D, TC])
    xhead_h = din("xhead", [1, D])
    wel_h = din("wel", [NDC, 128, 24])
    bel_h = din("bel", [24, 1])
    wle_h = din("wle", [24, D])
    ble_h = din("ble", [128, 8])
    ht_h = din("htm", [24, 12])
    wkb_h = din("wkb", [NDC, 128, D])
    wkab_h = din("wkab", [NDC, 128, D])
    wvb_h = din("wvb", [NDC, 128, D])
    battn_h = din("battn", [128, 40])
    rwb_h = din("rwb", [NDC, 128, 8])
    b1_h = din("b1v", [128, 8 * 32])
    b2l_h = din("b2l", [8, D])
    sel_h = din("sel", [8, 8 * 128])
    id128_h = din("id128", [128, 128])
    w1_h = din("w1b", [8, 32, 128, D])
    w2_h = din("w2b", [8, 32, 128, D])
    mexc_h = din("mexc", [128, 8])
    lx_h = din("lxc", [1, 8])

    yT_h = nc.dram_tensor("yT", [D, TC], F32, kind="ExternalOutput").ap()

    w1s_h = nc.dram_tensor("w1s", [8, 32, 128, 2048], BF16).ap()
    w2s_h = nc.dram_tensor("w2s", [8, 32, 128, 2048], BF16).ap()
    ew_h = nc.dram_tensor("ew_spill", [NDC, 128, TC], F32,
                          kind=("ExternalOutput" if dbg else "Internal")).ap()
    cc1i_h = nc.dram_tensor("cc1i", [1, 8], F32).ap()
    cc1o_h = nc.dram_tensor("cc1o", [8, 8], F32).ap()
    cc2i_h = nc.dram_tensor("cc2i", [128, 8], F32).ap()
    cc2o_h = nc.dram_tensor("cc2o", [8, 128, 8], F32).ap()
    cc3i_h = nc.dram_tensor("cc3i", [128, 8], F32).ap()
    cc3o_h = nc.dram_tensor("cc3o", [8, 128, 8], F32).ap()

    es = contextlib.ExitStack()

    def sb(name, shape, dtype=F32):
        return es.enter_context(nc.sbuf_tensor(name, shape, dtype))

    x_sb = sb("x_sb", [128, NDC * TC])
    bigA = sb("bigA", [128, 16384])
    bigB = sb("bigB", [128, 13824])
    wk_sb = sb("wk_sb", [128, 2048])
    wel_sb = sb("wel_sb", [128, NDC * 24])
    ht_sb = sb("ht_sb", [24, 12])
    b1_sb = sb("b1_sb", [128, 8 * 32])
    b2l_sb = sb("b2l_sb", [8, D])
    sel_sb = sb("sel_sb", [8, 8 * 128])
    id128_sb = sb("id128_sb", [128, 128])
    rl_sb = sb("rl_sb", [8, 512])
    gates_sb = sb("gates_sb", [8, 512])
    gtm_sb = sb("gtm_sb", [128, 64])
    battn_sb = sb("battn_sb", [128, 48])
    bias_sb = sb("bias_sb", [128, 48])
    rwb_sb = sb("rwb_sb", [128, NDC * 8])
    rws_sb = sb("rws_sb", [128, 64])
    mexc_sb = sb("mexc_sb", [128, 8])
    tot_sb = sb("tot_sb", [128, 40])
    mm_sb = sb("mm_sb", [128, 8])
    ones128 = sb("ones128", [128, 1])
    onesr = sb("onesr", [1, 128])
    ones12 = sb("ones12", [12, 1])
    kid_i = sb("kid_i", [128, 8], I32)
    gath_sb = sb("gath_sb", [128, 64])
    g1_sb = sb("g1_sb", [1, 64])

    def psmk(name, shape):
        return es.enter_context(nc.psum_tensor(name, shape, F32))

    psA = psmk("psA", [128, 512])
    psB = psmk("psB", [128, 512])
    psU0 = psmk("psU0", [128, 512])
    psU1 = psmk("psU1", [128, 512])
    psD0 = psmk("psD0", [128, 512])
    psD1 = psmk("psD1", [128, 512])
    psC = psmk("psC", [128, 1024])

    sems = {}
    for cname in ("sync", "scalar", "vector", "tensor", "gpsimd", "qs", "qg", "qa", "cc"):
        sems[cname] = es.enter_context(nc.semaphore("sem_" + cname))

    block = es.enter_context(nc.Block())
    p = Prog()

    def xc(dc, t0, tn):
        return x_sb[:, dc * TC + t0: dc * TC + t0 + tn]

    def dmas(fn, reads=(), writes=(), deps=()):
        return p.step("sync", fn, reads=reads, writes=writes, dma="qs", deps=deps)

    def dmag(fn, reads=(), writes=(), deps=()):
        return p.step("gpsimd", fn, reads=reads, writes=writes, dma="qg", deps=deps)

    def dmaa(fn, reads=(), writes=(), deps=()):
        return p.step("scalar", fn, reads=reads, writes=writes, dma="qa", deps=deps)

    def V(fn, reads=(), writes=(), deps=()):
        return p.step("vector", fn, reads=reads, writes=writes, deps=deps)

    def S(fn, reads=(), writes=(), deps=()):
        return p.step("scalar", fn, reads=reads, writes=writes, deps=deps)

    def TT(fn, reads=(), writes=(), deps=()):
        return p.step("tensor", fn, reads=reads, writes=writes, deps=deps)

    def G(fn, reads=(), writes=(), deps=()):
        return p.step("gpsimd", fn, reads=reads, writes=writes, deps=deps)

    dmas(lambda s: s.dma_start(
        out=x_sb[:].rearrange("p (c t) -> p c t", c=NDC),
        in_=xT_h.rearrange("(c p) t -> c p t", p=128).transpose([1, 0, 2])),
        writes=[("x", dc) for dc in range(NDC)])
    dmas(lambda s: s.dma_start(
        out=wel_sb[:].rearrange("p (c m) -> p c m", c=NDC),
        in_=wel_h.transpose([1, 0, 2])), writes=["wel"])
    dmas(lambda s: s.dma_start(out=bigB[0:24, 8192:9216], in_=wle_h), writes=["wle"])
    dmas(lambda s: s.dma_start(out=ht_sb[:], in_=ht_h), writes=["htm"])
    dmas(lambda s: s.dma_start(out=battn_sb[:, 0:40], in_=battn_h), writes=["battn"])
    dmas(lambda s: s.dma_start(out=b1_sb[:], in_=b1_h), writes=["b1"])
    dmas(lambda s: s.dma_start(out=b2l_sb[:], in_=b2l_h), writes=["b2l"])
    dmas(lambda s: s.dma_start(out=sel_sb[:], in_=sel_h), writes=["sel"])
    dmas(lambda s: s.dma_start(out=id128_sb[:], in_=id128_h), writes=["id128"])
    dmas(lambda s: s.dma_start(
        out=rwb_sb[:].rearrange("p (c e) -> p c e", c=NDC),
        in_=rwb_h.transpose([1, 0, 2])), writes=["rwb"])
    dmas(lambda s: s.dma_start(out=mexc_sb[:], in_=mexc_h), writes=["mexc"])
    dmas(lambda s: s.dma_start(out=bias_sb[:, 0:8], in_=ble_h), writes=["ble"])
    dmas(lambda s: s.dma_start(out=g1_sb[0:1, 32:40], in_=lx_h), writes=["lxrow"])
    dmas(lambda s: s.dma_start(out=bias_sb[0:24, 32:33], in_=bel_h), writes=["bel"])

    V(lambda v: v.memset(ones128[:], 1.0), writes=["ones128"])
    V(lambda v: v.memset(onesr[:], 1.0), writes=["onesr"])
    V(lambda v: v.memset(ones12[:], 1.0), writes=["ones12"])
    V(lambda v: v.memset(bias_sb[:, 24:25], HALF_PI), writes=["hpi"])
    V(lambda v: v.tensor_scalar(bias_sb[0:24, 33:34], bias_sb[0:24, 32:33], -1.0, None, ALU.mult),
      reads=["bel"], writes=["negbel"])
    V(lambda v: v.tensor_scalar(battn_sb[:, 40:48], battn_sb[:, 8:16], float(1.0 / 32.0), None, ALU.mult),
      reads=["battn"], writes=["bka32"])
    G(lambda g: g.iota(kid_i[:, :], [[128, 8]], channel_multiplier=1), writes=["kid"])
    rws_bf = rws_sb[:].bitcast(BF16)
    V(lambda v: v.tensor_copy(rws_bf[:, 0:64], rwb_sb[:]), reads=["rwb"], writes=["rws"])
    V(lambda v: v.tensor_sub(rws_bf[:, 64:128], rwb_sb[:], rws_bf[:, 0:64]),
      reads=["rwb", "rws"], writes=["rws"])

    def emit_ln(tn, src_ap, dst_ap, sq_ap, rowb, src_regions, dst_regions,
                sq_region, row_region, post_chunk=None):
        for dcx in range(NDC):
            TT(lambda t, dcx=dcx: t.matmul(psA[0:1, 0:tn], ones128[:], src_ap(dcx),
                                           start=(dcx == 0), stop=(dcx == NDC - 1)),
               reads=[src_regions[dcx], "ones128"], writes=["psA"])
        sqv = sq_ap[:, 0:tn]
        for dcx in range(NDC):
            S(lambda s, dcx=dcx: s.activation(sqv, src_ap(dcx), AF.Square),
              reads=[src_regions[dcx]], writes=[sq_region])
            TT(lambda t, dcx=dcx: t.matmul(psB[0:1, 0:tn], ones128[:], sqv,
                                           start=(dcx == 0), stop=(dcx == NDC - 1)),
               reads=[sq_region, "ones128"], writes=["psB"])
        V(lambda v: v.tensor_scalar(rowb(0), psA[0:1, 0:tn], float(1.0 / D), None, ALU.mult),
          reads=["psA"], writes=[row_region])
        V(lambda v: v.tensor_scalar(rowb(1), psB[0:1, 0:tn], float(1.0 / D), None, ALU.mult),
          reads=["psB", row_region], writes=[row_region])
        V(lambda v: v.tensor_mul(rowb(2), rowb(0), rowb(0)),
          reads=[row_region], writes=[row_region])
        V(lambda v: v.tensor_sub(rowb(1), rowb(1), rowb(2)),
          reads=[row_region], writes=[row_region])
        V(lambda v: v.tensor_scalar(rowb(1), rowb(1), LN_EPS, None, ALU.add),
          reads=[row_region], writes=[row_region])
        V(lambda v: v.reciprocal(rowb(2), rowb(1)),
          reads=[row_region], writes=[row_region])
        S(lambda s: s.activation(rowb(1), rowb(2), AF.Sqrt),
          reads=[row_region], writes=[row_region])
        TT(lambda t: t.matmul(psD0[:, 0:tn], onesr[:], rowb(0), start=True, stop=True),
           reads=[row_region, "onesr"], writes=["psD0"])
        TT(lambda t: t.matmul(psD1[:, 0:tn], onesr[:], rowb(1), start=True, stop=True),
           reads=[row_region, "onesr"], writes=["psD1"])
        for dcx in range(NDC):
            V(lambda v, dcx=dcx: v.tensor_sub(dst_ap(dcx), src_ap(dcx), psD0[:, 0:tn]),
              reads=[src_regions[dcx], "psD0"], writes=[dst_regions[dcx]])
            V(lambda v, dcx=dcx: v.tensor_mul(dst_ap(dcx), dst_ap(dcx), psD1[:, 0:tn]),
              reads=[dst_regions[dcx], "psD1"], writes=[dst_regions[dcx]])
            if post_chunk is not None:
                post_chunk(dcx)

    NT512 = TC // 512
    rowslot = lambda i: bigA[0:1, 10240 + i * 512: 10240 + (i + 1) * 512]
    rowk_f = bigA[0:1, 12288:13312]
    rowi_i = bigA[0:1, 13312:14336].bitcast(I32)
    arow = lambda a, b: bigA[0:1, 14336 + a: 14336 + b]
    asc = lambda i: bigA[0:1, 15360 + i: 15360 + i + 1]
    fx1_row = bigA[0:1, 15488:16000]
    ubuf = lambda dcx: bigB[:, dcx * 512:(dcx + 1) * 512]
    wle_sb = bigB[0:24, 8192:9216]
    bits_sb = bigB[0:24, 9216:9728]
    syn_sb = bigB[0:12, 9728:10240]
    sqbuf = bigB[:, 4096:4608]
    dcmp = bigB[0:1, 4608:5632]
    fx2_row = bigB[0:1, 5632:6144]
    onetwob = bigB[0:12, 6144:6656]
    aoutb = bigB[:, 6656:7168]
    dsq = bigB[0:1, 7168:8192]

    if "P1" in phases:
        G(lambda g: g.iota(rowi_i, [[1, 1024]], channel_multiplier=0), writes=["rowi"])
        V(lambda v: v.tensor_copy(rowk_f, rowi_i), reads=["rowi"], writes=["rowk"])
        V(lambda v: v.memset(mm_sb[:, 0:2], BIG_NEG), writes=["mmacc"])
        for it in range(NT512):
            t0 = it * 512
            emit_ln(512, lambda dcx, t0=t0: xc(dcx, t0, 512), ubuf, sqbuf, rowslot,
                    [("x", dcx) for dcx in range(NDC)],
                    [("u", dcx) for dcx in range(NDC)], "sq", "lnrow")
            for dcx in range(NDC):
                V(lambda v, dcx=dcx: v.reduce_max(mm_sb[:, 2:3], ubuf(dcx), axis=AX.X),
                  reads=[("u", dcx)], writes=["mmtmp"])
                V(lambda v: v.tensor_max(mm_sb[:, 0:1], mm_sb[:, 0:1], mm_sb[:, 2:3]),
                  reads=["mmtmp", "mmacc"], writes=["mmacc"])
                V(lambda v, dcx=dcx: v.tensor_reduce(mm_sb[:, 2:3], ubuf(dcx), AX.X, ALU.min, negate=True),
                  reads=[("u", dcx)], writes=["mmtmp"])
                V(lambda v: v.tensor_max(mm_sb[:, 1:2], mm_sb[:, 1:2], mm_sb[:, 2:3]),
                  reads=["mmtmp", "mmacc"], writes=["mmacc"])
        TT(lambda t: t.transpose(psC[0:2, 0:128], mm_sb[:, 0:2], id128_sb[:]),
           reads=["mmacc", "id128"], writes=["psC"])
        V(lambda v: v.reduce_max(mm_sb[0:2, 4:5], psC[0:2, 0:128], axis=AX.X),
          reads=["psC"], writes=["mmred"])
        dmas(lambda s: s.dma_start(out=g1_sb[0:1, 0:2], in_=mm_sb[0:2, 4:5]),
             reads=["mmred"], writes=["e1row"])
        V(lambda v: v.memset(g1_sb[0:1, 2:8], 0.0), writes=["e1pad"])
        dmag(lambda g: g.dma_start(out=cc1i_h, in_=g1_sb[0:1, 0:8]),
             reads=["e1row", "e1pad"], writes=["cc1i"])
        p.step("gpsimd", lambda g: g.collective_compute(
            "AllGather", ALU.bypass, replica_groups=[list(range(NCORES))],
            ins=[cc1i_h.opt()], outs=[cc1o_h.opt()]),
            reads=["cc1i"], writes=["cc1o"], cc=True)
        dmag(lambda g: g.dma_start(out=gath_sb[0:2, 0:8], in_=cc1o_h.transpose([1, 0])[0:2], allow_slow_non_contiguous=True),
             reads=["cc1o"], writes=["e1g"])
        V(lambda v: v.reduce_max(gath_sb[0:2, 8:9], gath_sb[0:2, 0:8], axis=AX.X),
          reads=["e1g"], writes=["e1red"])
        dmas(lambda s: s.dma_start(out=g1_sb[0:1, 8:10], in_=gath_sb[0:2, 8:9]),
             reads=["e1red"], writes=["gmm"])

        dmas(lambda s: s.dma_start(out=arow(0, 1024), in_=xhead_h), writes=["xh"])
        V(lambda v: v.reduce_sum(asc(0), arow(0, 1024), axis=AX.X),
          reads=["xh"], writes=["dsc"])
        V(lambda v: v.tensor_scalar(asc(0), asc(0), float(1.0 / D), None, ALU.mult),
          reads=["dsc"], writes=["dsc"])
        S(lambda s: s.activation(dsq, arow(0, 1024), AF.Square),
          reads=["xh"], writes=["dsqr"])
        V(lambda v: v.reduce_sum(asc(1), dsq, axis=AX.X),
          reads=["dsqr", "dsc"], writes=["dsc"])
        V(lambda v: v.tensor_scalar(asc(1), asc(1), float(1.0 / D), None, ALU.mult),
          reads=["dsc"], writes=["dsc"])
        V(lambda v: v.tensor_mul(asc(2), asc(0), asc(0)), reads=["dsc"], writes=["dsc"])
        V(lambda v: v.tensor_sub(asc(1), asc(1), asc(2)), reads=["dsc"], writes=["dsc"])
        V(lambda v: v.tensor_scalar(asc(1), asc(1), LN_EPS, None, ALU.add),
          reads=["dsc"], writes=["dsc"])
        V(lambda v: v.reciprocal(asc(2), asc(1)), reads=["dsc"], writes=["dsc"])
        S(lambda s: s.activation(asc(3), asc(2), AF.Sqrt), reads=["dsc"], writes=["dsc"])
        V(lambda v: v.tensor_scalar(arow(0, 1024), arow(0, 1024), asc(0), None, ALU.subtract),
          reads=["xh", "dsc"], writes=["xh"])
        V(lambda v: v.tensor_scalar(arow(0, 1024), arow(0, 1024), asc(3), None, ALU.mult),
          reads=["xh", "dsc"], writes=["xh"])
        V(lambda v: v.tensor_add(asc(4), g1_sb[0:1, 8:9], g1_sb[0:1, 9:10]),
          reads=["gmm"], writes=["dsc"])
        V(lambda v: v.tensor_scalar(asc(4), asc(4), 1e-10, None, ALU.add),
          reads=["dsc"], writes=["dsc"])
        V(lambda v: v.reciprocal(asc(5), asc(4)), reads=["dsc"], writes=["dsc"])
        V(lambda v: v.tensor_scalar(arow(0, 1024), arow(0, 1024), g1_sb[0:1, 9:10], None, ALU.add),
          reads=["xh", "gmm"], writes=["xh"])
        V(lambda v: v.tensor_scalar(arow(0, 1024), arow(0, 1024), asc(5), None, ALU.mult),
          reads=["xh", "dsc"], writes=["xh"])
        for i, sc2 in enumerate((4, 16, 64, 256, 1024)):
            V(lambda v, sc2=sc2: v.reduce_sum(asc(6), arow(0, sc2), axis=AX.X),
              reads=["xh"], writes=["dsc"])
            V(lambda v, sc2=sc2: v.tensor_scalar(asc(6), asc(6), float(1.0 / sc2), None, ALU.mult),
              reads=["dsc"], writes=["dsc"])
            V(lambda v, sc2=sc2: v.tensor_scalar(dcmp[0:1, 0:sc2], arow(0, sc2), asc(6), None, ALU.is_gt),
              reads=["xh", "dsc"], writes=["dcmpr"])
            V(lambda v, sc2=sc2, i=i: v.reduce_sum(asc(16 + i), dcmp[0:1, 0:sc2], axis=AX.X),
              reads=["dcmpr"], writes=["dcnt"])
            V(lambda v, i=i: v.tensor_scalar(asc(16 + i), asc(16 + i), 1.0, None, ALU.max),
              reads=["dcnt"], writes=["dcnt"])
        S(lambda s: s.activation(bigA[0:1, 15376:15381], bigA[0:1, 15376:15381], AF.Ln),
          reads=["dcnt"], writes=["dly"])
        S(lambda s: s.activation(bigA[0:1, 15384:15389], g1_sb[0:1, 32:37], AF.Ln),
          reads=["lxrow"], writes=["dlx"])
        ly = bigA[0:1, 15376:15381]
        lx = bigA[0:1, 15384:15389]
        tmp5 = bigA[0:1, 15392:15397]
        V(lambda v: v.tensor_mul(tmp5, lx, ly), reads=["dly", "dlx"], writes=["d5"])
        V(lambda v: v.reduce_sum(asc(40), tmp5, axis=AX.X), reads=["d5"], writes=["dsxy"])
        V(lambda v: v.reduce_sum(asc(41), lx, axis=AX.X), reads=["dlx"], writes=["dsx"])
        V(lambda v: v.reduce_sum(asc(42), ly, axis=AX.X), reads=["dly"], writes=["dsy"])
        V(lambda v: v.tensor_mul(tmp5, lx, lx), reads=["dlx", "d5"], writes=["d5"])
        V(lambda v: v.reduce_sum(asc(43), tmp5, axis=AX.X), reads=["d5"], writes=["dsxx"])
        V(lambda v: v.tensor_scalar(asc(44), asc(40), 5.0, None, ALU.mult),
          reads=["dsxy"], writes=["dnum"])
        V(lambda v: v.tensor_mul(asc(45), asc(41), asc(42)),
          reads=["dsx", "dsy"], writes=["dt"])
        V(lambda v: v.tensor_sub(asc(44), asc(44), asc(45)),
          reads=["dnum", "dt"], writes=["dnum"])
        V(lambda v: v.tensor_scalar(asc(46), asc(43), 5.0, None, ALU.mult),
          reads=["dsxx"], writes=["dden"])
        V(lambda v: v.tensor_mul(asc(45), asc(41), asc(41)),
          reads=["dsx"], writes=["dt"])
        V(lambda v: v.tensor_sub(asc(46), asc(46), asc(45)),
          reads=["dden", "dt"], writes=["dden"])
        V(lambda v: v.reciprocal(asc(45), asc(46)), reads=["dden"], writes=["dt"])
        V(lambda v: v.tensor_mul(asc(47), asc(44), asc(45)),
          reads=["dnum", "dt"], writes=["dq"])
        V(lambda v: v.tensor_scalar(asc(47), asc(47), -1.0, 1.0, ALU.mult, ALU.max),
          reads=["dq"], writes=["dq"])
        V(lambda v: v.tensor_scalar(asc(48), asc(47), 2.0, None, ALU.min),
          reads=["dq"], writes=["dD"])
        V(lambda v: v.tensor_scalar(asc(49), asc(48), 0.2, 1.0, ALU.mult, ALU.add),
          reads=["dD"], writes=["dom"])
        V(lambda v: v.tensor_mul(asc(49), asc(49), asc(48)),
          reads=["dom", "dD"], writes=["dom"])
        V(lambda v: v.tensor_scalar(asc(50), asc(48), 0.5, 1.0, ALU.mult, ALU.add),
          reads=["dD"], writes=["da"])
        V(lambda v: v.tensor_scalar(asc(51), asc(48), 0.1, None, ALU.mult),
          reads=["dD"], writes=["db"])
        TT(lambda t: t.matmul(psB[:, 0:3], onesr[:], bigA[0:1, 15409:15412], start=True, stop=True),
           reads=["dom", "da", "db", "onesr"], writes=["psB"])
        V(lambda v: v.tensor_copy(bias_sb[:, 25:28], psB[:, 0:3]),
          reads=["psB"], writes=["dbc"])

        V(lambda v: v.tensor_copy(gtm_sb[:, 0:8], kid_i[:, :]), reads=["kid"], writes=["gt0"])
        V(lambda v: v.tensor_scalar(gtm_sb[:, 8:16], gtm_sb[:, 0:8], -1.0, 1024.0, ALU.mult, ALU.add),
          reads=["gt0"], writes=["gt1"])
        V(lambda v: v.tensor_tensor(gtm_sb[:, 0:8], gtm_sb[:, 0:8], gtm_sb[:, 8:16], ALU.min),
          reads=["gt0", "gt1"], writes=["gt0"])
        V(lambda v: v.tensor_scalar(gtm_sb[:, 0:8], gtm_sb[:, 0:8], float(1.0 / 1024.0), 1e-8, ALU.mult, ALU.add),
          reads=["gt0"], writes=["gt0"])
        V(lambda v: v.tensor_scalar(gtm_sb[:, 8:16], gtm_sb[:, 0:8], bias_sb[:, 26:27], None, ALU.mult),
          reads=["gt0", "dbc"], writes=["gt1"])
        V(lambda v: v.tensor_scalar(gtm_sb[:, 8:16], gtm_sb[:, 8:16], bias_sb[:, 25:26], None, ALU.add),
          reads=["gt1", "dbc"], writes=["gt1"])
        V(lambda v: v.tensor_scalar(gtm_sb[:, 16:24], gtm_sb[:, 8:16], PI, None, ALU.is_gt),
          reads=["gt1"], writes=["gt2"])
        V(lambda v: v.scalar_tensor_tensor(gtm_sb[:, 8:16], gtm_sb[:, 16:24], -TWO_PI,
                                           gtm_sb[:, 8:16], ALU.mult, ALU.add),
          reads=["gt2", "gt1"], writes=["gt1"])
        S(lambda s: s.activation(gtm_sb[:, 16:24], gtm_sb[:, 8:16], AF.Sin),
          reads=["gt1"], writes=["gt2"])
        S(lambda s: s.activation(gtm_sb[:, 24:32], gtm_sb[:, 0:8], AF.Square),
          reads=["gt0"], writes=["gt3"])
        V(lambda v: v.tensor_scalar(gtm_sb[:, 24:32], gtm_sb[:, 24:32], bias_sb[:, 27:28], None, ALU.mult),
          reads=["gt3", "dbc"], writes=["gt3"])
        V(lambda v: v.tensor_scalar(gtm_sb[:, 8:16], gtm_sb[:, 0:8], -1.0, None, ALU.mult),
          reads=["gt0"], writes=["gt1"])
        V(lambda v: v.tensor_scalar(gtm_sb[:, 8:16], gtm_sb[:, 8:16], bias_sb[:, 25:26], None, ALU.add),
          reads=["gt1", "dbc"], writes=["gt1"])
        V(lambda v: v.tensor_add(gtm_sb[:, 8:16], gtm_sb[:, 8:16], gtm_sb[:, 24:32]),
          reads=["gt1", "gt3"], writes=["gt1"])
        S(lambda s: s.activation(gtm_sb[:, 24:32], gtm_sb[:, 8:16], AF.Sin),
          reads=["gt1"], writes=["gt3"])
        V(lambda v: v.tensor_mul(bias_sb[:, 16:24], gtm_sb[:, 16:24], gtm_sb[:, 24:32]),
          reads=["gt2", "gt3"], writes=["fim"])
        S(lambda s: s.activation(gtm_sb[:, 24:32], gtm_sb[:, 8:16], AF.Sin,
                                 bias=bias_sb[:, 24:25], scale=-1.0),
          reads=["gt1", "hpi"], writes=["gt3"])
        V(lambda v: v.tensor_mul(bias_sb[:, 8:16], gtm_sb[:, 16:24], gtm_sb[:, 24:32]),
          reads=["gt2", "gt3"], writes=["fre"])
        V(lambda v: v.tensor_scalar(bias_sb[:, 8:16], bias_sb[:, 8:16], float(1.0 / 1024.0), None, ALU.mult),
          reads=["fre"], writes=["fre"])
        V(lambda v: v.tensor_scalar(bias_sb[:, 16:24], bias_sb[:, 16:24], float(1.0 / 1024.0), None, ALU.mult),
          reads=["fim"], writes=["fim"])
        V(lambda v: v.tensor_scalar(bias_sb[:, 36:44], bias_sb[:, 16:24], -1.0, None, ALU.mult),
          reads=["fim"], writes=["fimn"])

        CT = bigA[:, 0:1024]
        STb = bigA[:, 1024:2048]
        T2b = bigA[:, 2048:3072]
        T3b = bigA[:, 3072:4096]
        CTm = bigA[:, 4096:4224]
        STm = bigA[:, 4224:4352]
        for kc in range(NDC):
            krow = rowk_f[0:1, 128 * kc: 128 * kc + 128]
            for h in range(2):
                pu = psU0 if h == 0 else psU1
                pur = "psU%d" % h
                TT(lambda t, h=h, krow=krow, pu=pu: t.matmul(
                    pu[:, :], krow,
                    rowk_f[0:1, h * 512:(h + 1) * 512], start=True, stop=True),
                   reads=["rowk"], writes=[pur])
                V(lambda v, h=h, pu=pu: v.tensor_copy(
                    T3b[:, h * 512:(h + 1) * 512].bitcast(I32), pu[:, :]),
                  reads=[pur], writes=["t3"])
                V(lambda v, h=h: v.tensor_scalar(
                    T2b[:, h * 512:(h + 1) * 512].bitcast(I32),
                    T3b[:, h * 512:(h + 1) * 512].bitcast(I32), 1023, None, ALU.bitwise_and),
                  reads=["t3"], writes=["t2"])
                V(lambda v, h=h: v.tensor_copy(
                    CT[:, h * 512:(h + 1) * 512], T2b[:, h * 512:(h + 1) * 512].bitcast(I32)),
                  reads=["t2"], writes=["ct"])
            V(lambda v: v.tensor_scalar(T2b, CT, 512.0, None, ALU.is_ge),
              reads=["ct"], writes=["t2"])
            V(lambda v: v.tensor_scalar(T3b, CT, float(TWO_PI / 1024.0), None, ALU.mult),
              reads=["ct"], writes=["t3"])
            V(lambda v: v.scalar_tensor_tensor(STb, T2b, -TWO_PI, T3b, ALU.mult, ALU.add),
              reads=["t2", "t3"], writes=["st"])
            S(lambda s: s.activation(T2b, STb, AF.Abs), reads=["st"], writes=["t2"])
            S(lambda s: s.activation(CT, T2b, AF.Sin, bias=bias_sb[:, 24:25], scale=-1.0),
              reads=["t2", "hpi"], writes=["ct"])
            S(lambda s: s.activation(STb, STb, AF.Sin), reads=["st"], writes=["st"])
            V(lambda v: v.tensor_copy(CTm, CT[:, 0:128]), reads=["ct"], writes=["ctm"])
            V(lambda v: v.tensor_copy(STm, STb[:, 0:128]), reads=["st"], writes=["stm"])
            fre_c = bias_sb[:, 8 + kc: 9 + kc]
            fim_c = bias_sb[:, 16 + kc: 17 + kc]
            fimn_c = bias_sb[:, 36 + kc: 37 + kc]
            V(lambda v, a=fre_c: v.tensor_scalar(T3b, STb, a, None, ALU.mult),
              reads=["st", "fre"], writes=["t3"])
            V(lambda v, a=fim_c: v.scalar_tensor_tensor(T2b, CT, a, T3b, ALU.mult, ALU.add),
              reads=["ct", "t3", "fim"], writes=["t2"])
            V(lambda v, a=fre_c: v.tensor_scalar(T3b, CT, a, None, ALU.mult),
              reads=["ct", "fre"], writes=["t3"])
            V(lambda v, a=fimn_c: v.scalar_tensor_tensor(CT, STb, a, T3b, ALU.mult, ALU.add),
              reads=["st", "t3", "fimn"], writes=["ct"])
            for h in range(2):
                TT(lambda t, h=h, kc=kc: t.matmul(
                    psC[:, h * 512:(h + 1) * 512], CTm, CT[:, h * 512:(h + 1) * 512],
                    start=(kc == 0), stop=False),
                   reads=["ctm", "ct"], writes=["psC"])
                TT(lambda t, h=h, kc=kc: t.matmul(
                    psC[:, h * 512:(h + 1) * 512], STm, T2b[:, h * 512:(h + 1) * 512],
                    start=False, stop=(kc == NDC - 1)),
                   reads=["stm", "t2"], writes=["psC"])
        for h in range(2):
            V(lambda v, h=h: v.tensor_copy(bigA[:, 5120 + h * 512: 5120 + (h + 1) * 512],
                                           psC[:, h * 512:(h + 1) * 512]),
              reads=["psC"], writes=["strip0"])

    if "P2" in phases:
        strip = lambda j: bigA[:, 5120 + j * 128: 5120 + (j + 1) * 128]
        X1 = lambda ncc: bigA[:, 6144 + ncc * 512: 6144 + (ncc + 1) * 512]
        for it in range(NT512):
            t0 = it * 512
            emit_ln(512, lambda dcx, t0=t0: xc(dcx, t0, 512), ubuf, sqbuf, rowslot,
                    [("x", dcx) for dcx in range(NDC)],
                    [("u", dcx) for dcx in range(NDC)], "sq", "lnrow")
            for ncc in range(NDC):
                pu = psU0 if ncc % 2 == 0 else psU1
                pur = "psU%d" % (ncc % 2)
                for mc in range(NDC):
                    TT(lambda t, ncc=ncc, mc=mc, pu=pu: t.matmul(
                        pu[:, :], strip((ncc - mc) % 8), ubuf(mc),
                        start=(mc == 0), stop=(mc == NDC - 1)),
                       reads=["strip0", ("u", mc)], writes=[pur])
                V(lambda v, ncc=ncc, pu=pu: v.tensor_copy(X1(ncc), pu[:, :]),
                  reads=[pur], writes=[("x1", ncc)])
            for dcx in range(NDC):
                TT(lambda t, dcx=dcx: t.matmul(
                    psA[0:24, :], wel_sb[:, dcx * 24:(dcx + 1) * 24], X1(dcx),
                    start=(dcx == 0), stop=(dcx == NDC - 1)),
                   reads=["wel", ("x1", dcx)], writes=["psA"])
            S(lambda s: s.activation(bits_sb, psA[0:24, :], AF.Identity,
                                     bias=bias_sb[0:24, 32:33], scale=1.0),
              reads=["psA", "bel"], writes=["bits"])
            V(lambda v: v.tensor_scalar(bits_sb, bits_sb, 0.0, None, ALU.is_gt),
              reads=["bits"], writes=["bits"])
            TT(lambda t: t.matmul(psB[0:12, :], ht_sb[:, :], bits_sb,
                                  start=True, stop=True),
               reads=["htm", "bits"], writes=["psB"])
            V(lambda v: v.tensor_scalar(syn_sb, psB[0:12, :], 0.5, None, ALU.is_gt),
              reads=["psB"], writes=["syn"])
            TT(lambda t: t.matmul(psA[0:1, :], ones12[:], syn_sb, start=True, stop=True),
               reads=["ones12", "syn"], writes=["psA"])
            V(lambda v: v.tensor_scalar(fx1_row, psA[0:1, :], 0.5, None, ALU.is_gt),
              reads=["psA"], writes=["fx1"])
            V(lambda v: v.tensor_scalar(fx2_row, psA[0:1, :], 3.5, None, ALU.is_le),
              reads=["psA"], writes=["fx2"])
            V(lambda v: v.tensor_mul(fx1_row, fx1_row, fx2_row),
              reads=["fx1", "fx2"], writes=["fx1"])
            TT(lambda t: t.matmul(psB[0:12, :], onesr[0:1, 0:12], fx1_row,
                                  start=True, stop=True),
               reads=["onesr", "fx1"], writes=["psB"])
            V(lambda v: v.tensor_tensor(syn_sb, syn_sb, psB[0:12, :], ALU.mult),
              reads=["syn", "psB"], writes=["syn"])
            V(lambda v: v.tensor_scalar(onetwob, bits_sb[0:12, 0:512], -2.0, 1.0, ALU.mult, ALU.add),
              reads=["bits"], writes=["otb"])
            V(lambda v: v.tensor_mul(syn_sb, syn_sb, onetwob),
              reads=["syn", "otb"], writes=["syn"])
            V(lambda v: v.tensor_add(bits_sb[0:12, 0:512], bits_sb[0:12, 0:512], syn_sb),
              reads=["bits", "syn"], writes=["bits"])
            for dcx in range(NDC):
                TT(lambda t, dcx=dcx: t.matmul(
                    psB[:, :], wle_sb[:, dcx * 128:(dcx + 1) * 128], bits_sb,
                    start=True, stop=True),
                   reads=["wle", "bits"], writes=["psB"])
                S(lambda s, dcx=dcx: s.activation(aoutb, psB[:, :], AF.Identity,
                                                  bias=bias_sb[:, dcx:dcx + 1], scale=1.0),
                  reads=["psB", "ble"], writes=["aout"])
                V(lambda v, dcx=dcx: v.tensor_add(aoutb, aoutb, X1(dcx)),
                  reads=["aout", ("x1", dcx)], writes=["aout"])
                V(lambda v, dcx=dcx, t0=t0: v.tensor_add(xc(dcx, t0, 512), xc(dcx, t0, 512), aoutb),
                  reads=["aout", ("x", dcx)], writes=[("x", dcx)])

    def emit_ffn(call, fence=(), do_split=False):
        bigAb_ = bigA[:].bitcast(BF16)

        def emit_split_for(gi2, fdeps=()):
            if not do_split or gi2 >= 128:
                return
            e2, g2 = gi2 // 16, gi2 % 16
            s2 = gi2 % 4
            for wi, (wsrc, wdst, wtag, cbase, stq) in enumerate((
                    (w1_h, w1s_h, "w1s", s2 * 8192, dmaa),
                    (w2_h, w2s_h, "w2s", s2 * 8192 + 4096, dmag))):
                wr = "wst%d" % (wi + 1)
                for i in range(2):
                    fc = 2 * g2 + i
                    sl = (2 * wi + i) % 2
                    hi = bigAb_[:, cbase + i * 2048: cbase + i * 2048 + 1024]
                    lo = bigAb_[:, cbase + i * 2048 + 1024: cbase + i * 2048 + 2048]
                    dmas(lambda s, wsrc=wsrc, e2=e2, fc=fc, sl=sl: s.dma_start(
                        out=wk_sb[:, sl * 1024:(sl + 1) * 1024], in_=wsrc[e2, fc]),
                        writes=[("spin", sl)])
                    S(lambda s, hi=hi, sl=sl: s.activation(
                        hi, wk_sb[:, sl * 1024:(sl + 1) * 1024], AF.Identity),
                      reads=[("spin", sl)], writes=[(wr, s2)], deps=list(fdeps))
                    V(lambda v, hi=hi, lo=lo, sl=sl: v.tensor_sub(
                        lo, wk_sb[:, sl * 1024:(sl + 1) * 1024], hi),
                      reads=[("spin", sl), (wr, s2)], writes=[(wr, s2)])
                stq(lambda s, wdst=wdst, e2=e2, g2=g2, cbase=cbase: s.dma_start(
                    out=wdst[e2, 2 * g2: 2 * g2 + 2].transpose([1, 0, 2]),
                    in_=bigAb_[:, cbase: cbase + 4096].rearrange("p (i n) -> p i n", i=2)),
                    reads=[(wr, s2)],
                    writes=[(wtag, e2, 2 * g2), (wtag, e2, 2 * g2 + 1)])
        UHb = lambda dcx: bigB[:, 0:2048].bitcast(BF16)[:, dcx * 512:(dcx + 1) * 512]
        ULb = lambda dcx: bigB[:, 2048:4096].bitcast(BF16)[:, dcx * 512:(dcx + 1) * 512]
        ACC = lambda dcx: bigB[:, 4096 + dcx * 512: 4096 + (dcx + 1) * 512]
        GGF = bigB[:, 8192:8704]
        GGH = lambda i: bigB[:, 8704 + i * 256: 8704 + (i + 1) * 256].bitcast(BF16)
        GGL = lambda i: bigB[:, 9728 + i * 256: 9728 + (i + 1) * 256].bitcast(BF16)
        GBC = bigB[:, 10752:11264]
        SQF = bigB[:, 11264:11776]
        UTMP = bigB[:, 11776:12288]
        frow = lambda i: bigB[0:1, 12288 + i * 512: 12288 + (i + 1) * 512]
        bigAb = bigA[:].bitcast(BF16)

        def W1t(s_, i, hilo, dcx):
            base = (s_ * 4096 + i * 1024) * 2 + hilo * 1024
            return bigAb[:, base + dcx * 128: base + (dcx + 1) * 128]

        def W2t(s_, i, hilo, dcx):
            base = (s_ * 4096 + 2048 + i * 1024) * 2 + hilo * 1024
            return bigAb[:, base + dcx * 128: base + (dcx + 1) * 128]

        groups = [(e, g) for e in range(8) for g in range(16)]
        for tq in range(4):
            t0 = tq * 512

            def usplit(dcx):
                V(lambda v, dcx=dcx: v.tensor_copy(UHb(dcx), UTMP),
                  reads=["utmp"], writes=[("uh", dcx)])
                V(lambda v, dcx=dcx: v.tensor_sub(ULb(dcx), UTMP, UHb(dcx)),
                  reads=["utmp", ("uh", dcx)], writes=[("ul", dcx)])

            emit_ln(512, lambda dcx, t0=t0: xc(dcx, t0, 512), lambda dcx: UTMP,
                    SQF, frow,
                    [("x", dcx) for dcx in range(NDC)],
                    ["utmp"] * NDC, "fsq", "frow", post_chunk=usplit)
            for dcx in range(NDC):
                TT(lambda t, dcx=dcx: t.matmul(psA[0:8, :], rws_bf[:, dcx * 8:(dcx + 1) * 8],
                                               UHb(dcx), start=(dcx == 0), stop=False),
                   reads=["rws", ("uh", dcx)], writes=["psA"])
                TT(lambda t, dcx=dcx: t.matmul(psA[0:8, :], rws_bf[:, dcx * 8:(dcx + 1) * 8],
                                               ULb(dcx), start=False, stop=False),
                   reads=["rws", ("ul", dcx)], writes=["psA"])
                TT(lambda t, dcx=dcx: t.matmul(psA[0:8, :], rws_bf[:, 64 + dcx * 8: 64 + (dcx + 1) * 8],
                                               UHb(dcx), start=False, stop=(dcx == NDC - 1)),
                   reads=["rws", ("uh", dcx)], writes=["psA"])
            V(lambda v: v.tensor_copy(rl_sb[:, :], psA[0:8, :]), reads=["psA"], writes=["rl"])
            for b in range(4):
                TM = gtm_sb[:, b * 16: b * 16 + 8]
                m1c = gtm_sb[:, b * 16 + 8: b * 16 + 9]
                n1c = gtm_sb[:, b * 16 + 9: b * 16 + 10]
                m2c = gtm_sb[:, b * 16 + 10: b * 16 + 11]
                n2c = gtm_sb[:, b * 16 + 11: b * 16 + 12]
                pc = gtm_sb[:, b * 16 + 12: b * 16 + 13]
                ac = gtm_sb[:, b * 16 + 13: b * 16 + 14]
                g2c = gtm_sb[:, b * 16 + 14: b * 16 + 15]
                C1 = bigB[:, 11264 + b * 32 + 0: 11264 + b * 32 + 8]
                C2 = bigB[:, 11264 + b * 32 + 8: 11264 + b * 32 + 16]
                TMX = bigB[:, 11264 + b * 32 + 16: 11264 + b * 32 + 24]
                rg = "g%d" % b
                TT(lambda t, b=b: t.transpose(psB[:, 0:8], rl_sb[:, b * 128:(b + 1) * 128],
                                              id128_sb[0:8, 0:8]),
                   reads=["rl", "id128"], writes=["psB"])
                V(lambda v, TM=TM: v.tensor_copy(TM, psB[:, 0:8]), reads=["psB"], writes=[rg])
                V(lambda v, TM=TM, m1c=m1c: v.reduce_max(m1c, TM, axis=AX.X), reads=[rg], writes=[rg])
                V(lambda v, TM=TM, m1c=m1c, C1=C1: v.tensor_scalar(C1, TM, m1c, None, ALU.is_equal),
                  reads=[rg, "fsq"], writes=[rg, "fsq"])
                V(lambda v, C1=C1, n1c=n1c: v.reduce_sum(n1c, C1, axis=AX.X), reads=[rg, "fsq"], writes=[rg])
                V(lambda v, C1=C1, TM=TM, TMX=TMX: v.scalar_tensor_tensor(TMX, C1, BIG_NEG, TM, ALU.mult, ALU.add),
                  reads=[rg, "fsq"], writes=[rg, "fsq"])
                V(lambda v, TMX=TMX, m2c=m2c: v.reduce_max(m2c, TMX, axis=AX.X), reads=[rg, "fsq"], writes=[rg])
                V(lambda v, TMX=TMX, m2c=m2c, C2=C2: v.tensor_scalar(C2, TMX, m2c, None, ALU.is_equal),
                  reads=[rg, "fsq"], writes=[rg, "fsq"])
                V(lambda v, C2=C2, n2c=n2c: v.reduce_sum(n2c, C2, axis=AX.X), reads=[rg, "fsq"], writes=[rg])
                V(lambda v, n1c=n1c, pc=pc: v.tensor_scalar(pc, n1c, -1.0, 1.0, ALU.add, ALU.min),
                  reads=[rg], writes=[rg])
                V(lambda v, m1c=m1c, m2c=m2c, n1c=n1c: v.tensor_sub(n1c, m1c, m2c),
                  reads=[rg], writes=[rg])
                V(lambda v, n1c=n1c, pc=pc: v.tensor_scalar(n1c, n1c, pc, None, ALU.mult),
                  reads=[rg], writes=[rg])
                V(lambda v, n1c=n1c, m2c=m2c: v.tensor_add(n1c, n1c, m2c),
                  reads=[rg], writes=[rg])
                V(lambda v, m1c=m1c, n1c=n1c: v.tensor_sub(n1c, m1c, n1c),
                  reads=[rg], writes=[rg])
                S(lambda s, n1c=n1c, ac=ac: s.activation(ac, n1c, AF.Sigmoid),
                  reads=[rg], writes=[rg + "a"])
                V(lambda v, ac=ac, m1c=m1c: v.tensor_scalar(m1c, ac, -1.0, 1.0, ALU.mult, ALU.add),
                  reads=[rg + "a"], writes=[rg])
                V(lambda v, pc=pc: v.tensor_scalar(pc, pc, -1.0, 1.0, ALU.mult, ALU.add),
                  reads=[rg], writes=[rg])
                V(lambda v, m1c=m1c, pc=pc: v.tensor_mul(m1c, m1c, pc),
                  reads=[rg], writes=[rg])
                V(lambda v, n2c=n2c, m2c=m2c: v.reciprocal(m2c, n2c),
                  reads=[rg], writes=[rg])
                V(lambda v, m1c=m1c, m2c=m2c, g2c=g2c: v.tensor_mul(g2c, m1c, m2c),
                  reads=[rg], writes=[rg])
                V(lambda v, C2=C2, g2c=g2c: v.tensor_scalar(C2, C2, g2c, None, ALU.mult),
                  reads=[rg, "fsq"], writes=[rg, "fsq"])
                V(lambda v, C1=C1, ac=ac, C2=C2, TM=TM: v.scalar_tensor_tensor(TM, C1, ac, C2, ALU.mult, ALU.add),
                  reads=[rg, rg + "a", "fsq"], writes=[rg])
                TT(lambda t, TM=TM: t.transpose(psB[0:8, 0:128], TM, id128_sb[:, :]),
                   reads=[rg, "id128"], writes=["psB"])
                V(lambda v, b=b: v.tensor_copy(gates_sb[:, b * 128:(b + 1) * 128], psB[0:8, 0:128]),
                  reads=["psB"], writes=["gates"])
            for dcx in range(NDC):
                TT(lambda t, dcx=dcx: t.matmul(psB[:, :], b2l_sb[:, dcx * 128:(dcx + 1) * 128],
                                               gates_sb[:, :], start=True, stop=True),
                   reads=["b2l", "gates"], writes=["psB"])
                V(lambda v, dcx=dcx: v.tensor_copy(ACC(dcx), psB[:, :]),
                  reads=["psB"], writes=[("acc", dcx)])
            pending_down = []
            if tq == 0 and do_split:
                emit_split_for(0, fdeps=fence)
                emit_split_for(1, fdeps=fence)
            for gi, (e, g) in enumerate(groups):
                slot = gi % 4
                fdeps = list(fence) if (tq == 0 and gi < 4 and not do_split) else []
                if not (tq == 0 and do_split):
                    dmas(lambda s, e=e, g=g, slot=slot: s.dma_start(
                        out=bigAb[:, slot * 8192: slot * 8192 + 4096].rearrange("p (i n) -> p i n", i=2),
                        in_=w1s_h[e, 2 * g: 2 * g + 2].transpose([1, 0, 2])),
                        writes=[("wst1", slot)], deps=fdeps,
                        reads=[("w1s", e, 2 * g + i) for i in range(2)])
                    dmas(lambda s, e=e, g=g, slot=slot: s.dma_start(
                        out=bigAb[:, slot * 8192 + 4096: slot * 8192 + 8192].rearrange("p (i n) -> p i n", i=2),
                        in_=w2s_h[e, 2 * g: 2 * g + 2].transpose([1, 0, 2])),
                        writes=[("wst2", slot)],
                        deps=fdeps,
                        reads=[("w2s", e, 2 * g + i) for i in range(2)])
                if g == 0:
                    TT(lambda t, e=e: t.matmul(psB[:, :], sel_sb[:, e * 128:(e + 1) * 128],
                                               gates_sb[:, :], start=True, stop=True),
                       reads=["sel", "gates"], writes=["psB"])
                    V(lambda v: v.tensor_copy(GBC, psB[:, :]), reads=["psB"], writes=["gbc"])
                for i in range(2):
                    fc = 2 * g + i
                    gsl = 2 * (gi % 2) + i
                    pu = psU0 if (i % 2 == 0) else psU1
                    pur = "psU%d" % (i % 2)
                    for dcx in range(NDC):
                        TT(lambda t, i=i, dcx=dcx, slot=slot, pu=pu: t.matmul(
                            pu[:, :], W1t(slot, i, 0, dcx), UHb(dcx),
                            start=(dcx == 0), stop=False),
                           reads=[("wst1", slot), ("uh", dcx)], writes=[pur])
                        TT(lambda t, i=i, dcx=dcx, slot=slot, pu=pu: t.matmul(
                            pu[:, :], W1t(slot, i, 0, dcx), ULb(dcx),
                            start=False, stop=False),
                           reads=[("wst1", slot), ("ul", dcx)], writes=[pur])
                        TT(lambda t, i=i, dcx=dcx, slot=slot, pu=pu: t.matmul(
                            pu[:, :], W1t(slot, i, 1, dcx), UHb(dcx),
                            start=False, stop=(dcx == NDC - 1)),
                           reads=[("wst1", slot), ("uh", dcx)], writes=[pur])
                    S(lambda s, gsl=gsl, e=e, fc=fc, pu=pu: s.activation(
                        GGF, pu[:, :], AF.Gelu,
                        bias=b1_sb[:, e * 32 + fc: e * 32 + fc + 1], scale=1.0),
                      reads=[pur, "b1"], writes=["ggf"])
                    V(lambda v: v.tensor_mul(GGF, GGF, GBC),
                      reads=["ggf", "gbc"], writes=["ggf"])
                    S(lambda s, gsl=gsl: s.activation(GGH(gsl), GGF, AF.Identity),
                      reads=["ggf"], writes=[("ggh", gsl)])
                    V(lambda v, gsl=gsl: v.tensor_sub(GGL(gsl), GGF, GGH(gsl)),
                      reads=["ggf", ("ggh", gsl)], writes=[("ggl", gsl)])
                def emit_down(slot=slot, gp=gi % 2):
                    for dcx in range(NDC):
                        pd = psD0 if (dcx % 2 == 0) else psD1
                        pdr = "psD%d" % (dcx % 2)
                        for i in range(2):
                            gsl2 = 2 * gp + i
                            TT(lambda t, i=i, gsl2=gsl2, dcx=dcx, slot=slot, pd=pd: t.matmul(
                                pd[:, :], W2t(slot, i, 0, dcx), GGH(gsl2),
                                start=(i == 0), stop=False),
                               reads=[("wst2", slot), ("ggh", gsl2)], writes=[pdr])
                            TT(lambda t, i=i, gsl2=gsl2, dcx=dcx, slot=slot, pd=pd: t.matmul(
                                pd[:, :], W2t(slot, i, 0, dcx), GGL(gsl2),
                                start=False, stop=False),
                               reads=[("wst2", slot), ("ggl", gsl2)], writes=[pdr])
                            TT(lambda t, i=i, gsl2=gsl2, dcx=dcx, slot=slot, pd=pd: t.matmul(
                                pd[:, :], W2t(slot, i, 1, dcx), GGH(gsl2),
                                start=False, stop=(i == 1)),
                               reads=[("wst2", slot), ("ggh", gsl2)], writes=[pdr])
                        V(lambda v, dcx=dcx, pd=pd: v.tensor_add(ACC(dcx), ACC(dcx), pd[:, :]),
                          reads=[pdr, ("acc", dcx)], writes=[("acc", dcx)])
                pending_down.append(emit_down)
                if len(pending_down) > 1:
                    pending_down.pop(0)()
                if tq == 0 and do_split:
                    emit_split_for(gi + 2, fdeps=fence if gi < 2 else ())
            while pending_down:
                pending_down.pop(0)()
            for dcx in range(NDC):
                V(lambda v, dcx=dcx, t0=t0: v.tensor_add(xc(dcx, t0, 512), xc(dcx, t0, 512), ACC(dcx)),
                  reads=[("acc", dcx), ("x", dcx)], writes=[("x", dcx)])

    if "P3" in phases:
        emit_ffn(0, do_split=True, fence=p.fence(["ct", "st", "t2", "t3", "ctm", "stm", "rowk", "rowi", "strip0",
                "lnrow", "xh", "dsc", "dly", "dlx", "d5", "dsxy", "dsx", "dsy",
                "dsxx", "dnum", "dden", "dt", "dq", "dD", "dom", "da", "db",
                "dcnt", "fx1"] + [("x1", i) for i in range(NDC)]))

    if "P4" in phases:
        CHB = bigB[:, 0:2048]
        HT2 = lambda dcx: bigB[:, 2048 + dcx * 256: 2048 + (dcx + 1) * 256]
        RQ = bigB[:, 8192:9216]
        T1Q = bigB[:, 9216:10240]
        SQ4 = bigB[:, 10240:10752]
        VT4 = bigB[:, 6144:6656]
        hrow1 = lambda i: bigB[0:1, 11264 + i * 256: 11264 + (i + 1) * 256]
        hrow2 = lambda i: bigB[0:1, 12288 + i * 256: 12288 + (i + 1) * 256]
        ONES2K = bigB[:, 4096:6144]
        V(lambda v: v.memset(ONES2K, 1.0), writes=["ones2k"])
        NT256 = TC // 256

        def emit_h_tile(it):
            t0 = it * 256
            emit_ln(256, lambda dcx, t0=t0: xc(dcx, t0, 256), HT2, SQ4, hrow1,
                    [("x", dcx) for dcx in range(NDC)],
                    [("h", dcx) for dcx in range(NDC)], "p4sq", "hrow1")
            emit_ln(256, HT2, HT2, SQ4, hrow2,
                    [("h", dcx) for dcx in range(NDC)],
                    [("h", dcx) for dcx in range(NDC)], "p4sq", "hrow2")
            for dcx in range(NDC):
                V(lambda v, dcx=dcx: v.tensor_scalar(
                    HT2(dcx), HT2(dcx), battn_sb[:, 24 + dcx:25 + dcx],
                    battn_sb[:, 32 + dcx:33 + dcx], ALU.mult, ALU.add),
                  reads=[("h", dcx), "battn"], writes=[("h", dcx)])

        def make_stream(src_h, tag, pos):
            state = {"k": 0}

            def prefetch():
                k = state["k"]
                if k < len(pos):
                    _, dco2 = pos[k]
                    slot = k % 2
                    dmas(lambda s, dco2=dco2, slot=slot: s.dma_start(
                        out=wk_sb[:, slot * 1024:(slot + 1) * 1024], in_=src_h[dco2]),
                        writes=[(tag, slot)])
                    state["k"] += 1
            return prefetch

        pos = [(it, dco) for it in range(NT256) for dco in range(NDC)]
        pf1 = make_stream(wkb_h, "wkS", pos)
        pf1(); pf1()
        for it in range(NT256):
            t0 = it * 256
            emit_h_tile(it)
            if dbg and it == 6 and not dram.get("dbgH"):
                dram["dbgH"] = nc.dram_tensor("dbgH", [128, 2048], F32, kind="ExternalOutput").ap()
                dmas(lambda s: s.dma_start(out=dram["dbgH"], in_=bigB[:, 2048:4096]),
                     reads=[("h", i) for i in range(NDC)], writes=["dbgH"])
            for dco in range(NDC):
                slot = (it * NDC + dco) % 2
                pu = psU0 if dco % 2 == 0 else psU1
                pur = "psU%d" % (dco % 2)
                for dcx in range(NDC):
                    TT(lambda t, dcx=dcx, slot=slot, pu=pu: t.matmul(
                        pu[:, 0:256], wk_sb[:, slot * 1024 + dcx * 128: slot * 1024 + (dcx + 1) * 128],
                        HT2(dcx), start=(dcx == 0), stop=(dcx == NDC - 1)),
                       reads=[("wkS", slot), ("h", dcx)], writes=[pur])
                S(lambda s, dco=dco, t0=t0, pu=pu: s.activation(
                    bigA[:, dco * TC + t0: dco * TC + t0 + 256], pu[:, 0:256],
                    AF.Identity, bias=battn_sb[:, dco:dco + 1], scale=1.0),
                  reads=[pur, "battn"], writes=[("keys", dco)])
                pf1()
        if dbg:
            dbgK_h = nc.dram_tensor("dbgK", [128, 16384], F32, kind="ExternalOutput").ap()
            dmas(lambda s: s.dma_start(out=dbgK_h, in_=bigA[:]),
                 reads=[("keys", i) for i in range(NDC)], writes=["dbgK"])
        pf2 = make_stream(wkab_h, "wkS", pos)
        pf2(); pf2()
        for it in range(NT256):
            t0 = it * 256
            for dco in range(NDC):
                slot = (it * NDC + dco) % 2
                pd = psD0 if dco % 2 == 0 else psD1
                pdr = "psD%d" % (dco % 2)
                for dcx in range(NDC):
                    TT(lambda t, dcx=dcx, t0=t0, slot=slot, pd=pd: t.matmul(
                        pd[:, 0:256], wk_sb[:, slot * 1024 + dcx * 128: slot * 1024 + (dcx + 1) * 128],
                        bigA[:, dcx * TC + t0: dcx * TC + t0 + 256],
                        start=(dcx == 0), stop=(dcx == NDC - 1)),
                       reads=[("wkS", slot), ("keys", dcx)], writes=[pdr])
                S(lambda s, dco=dco, pd=pd: s.activation(
                    CHB[:, dco * 256:(dco + 1) * 256], pd[:, 0:256], AF.Exp,
                    bias=battn_sb[:, 40 + dco:41 + dco], scale=float(1.0 / 32.0)),
                  reads=[pdr, "bka32"], writes=["chb"])
                pf2()
            dmas(lambda s, it=it: s.dma_start(
                out=ew_h[:, :, it * 256:(it + 1) * 256].transpose([1, 0, 2]),
                in_=CHB.rearrange("p (c t) -> p c t", c=NDC)),
                reads=["chb"], writes=["ewh%d" % it])
        for dcx in range(NDC):
            dmas(lambda s, dcx=dcx: s.dma_start(out=CHB, in_=ew_h[dcx]),
                 reads=["ewh%d" % it2 for it2 in range(NT256)], writes=["chb"])
            V(lambda v: v.tensor_tensor_scan(CHB, ONES2K, CHB, 0.0, ALU.mult, ALU.add),
              reads=["chb", "ones2k"], writes=["chb"])
            V(lambda v, dcx=dcx: v.tensor_copy(tot_sb[:, dcx:dcx + 1], CHB[:, 2047:2048]),
              reads=["chb"], writes=["tot1"])
            dmas(lambda s, dcx=dcx: s.dma_start(out=ew_h[dcx], in_=CHB),
                 reads=["chb"], writes=["csh%d" % dcx])
        dmag(lambda g: g.dma_start(out=cc2i_h, in_=tot_sb[:, 0:8]),
             reads=["tot1"], writes=["cc2i"])
        p.step("gpsimd", lambda g: g.collective_compute(
            "AllGather", ALU.bypass, replica_groups=[list(range(NCORES))],
            ins=[cc2i_h.opt()], outs=[cc2o_h.opt()]),
            reads=["cc2i"], writes=["cc2o"], cc=True)
        dmag(lambda g: g.dma_start(out=gath_sb[:].rearrange("p (r c) -> p r c", r=8),
                                   in_=cc2o_h.transpose([1, 0, 2])),
             reads=["cc2o"], writes=["gath"])
        V(lambda v: v.memset(tot_sb[:, 8:16], 0.0), writes=["carry1"])
        for r8 in range(8):
            V(lambda v, r8=r8: v.scalar_tensor_tensor(
                tot_sb[:, 32:40], gath_sb[:, r8 * 8:(r8 + 1) * 8], mexc_sb[:, r8:r8 + 1],
                tot_sb[:, 8:16], ALU.mult, ALU.add),
              reads=["gath", "mexc", "carry1"], writes=["ctmp"])
            V(lambda v: v.tensor_copy(tot_sb[:, 8:16], tot_sb[:, 32:40]),
              reads=["ctmp"], writes=["carry1"])
        for dcx in range(NDC):
            dmas(lambda s, dcx=dcx: s.dma_start(out=CHB, in_=ew_h[dcx]),
                 reads=["csh%d" % dcx, "chb"], writes=["chb"])
            V(lambda v, dcx=dcx: v.tensor_scalar(CHB, CHB, tot_sb[:, 8 + dcx:9 + dcx], None, ALU.add),
              reads=["chb", "carry1"], writes=["chb"])
            for q in range(2):
                q0 = q * 1024
                V(lambda v, q0=q0: v.reciprocal(RQ, CHB[:, q0:q0 + 1024]),
                  reads=["chb"], writes=["rq"])
                if q == 0:
                    V(lambda v, dcx=dcx: v.tensor_scalar(
                        T1Q[:, 0:1], RQ[:, 0:1], tot_sb[:, 8 + dcx:9 + dcx], None, ALU.mult),
                      reads=["rq", "carry1"], writes=["t1q"])
                    V(lambda v: v.tensor_mul(T1Q[:, 1:1024], CHB[:, 0:1023], RQ[:, 1:1024]),
                      reads=["chb", "rq"], writes=["t1q"])
                else:
                    V(lambda v, q0=q0: v.tensor_mul(T1Q, CHB[:, q0 - 1:q0 + 1023], RQ),
                      reads=["chb", "rq"], writes=["t1q"])
                V(lambda v: v.tensor_scalar(T1Q, T1Q, -1.0, 1.0, ALU.mult, ALU.add),
                  reads=["t1q"], writes=["t1q"])
                V(lambda v, dcx=dcx, q0=q0: v.tensor_mul(
                    bigA[:, dcx * TC + q0: dcx * TC + q0 + 1024],
                    bigA[:, dcx * TC + q0: dcx * TC + q0 + 1024], T1Q),
                  reads=["t1q", ("keys", dcx)], writes=[("keys", dcx)])
        for dcx in range(NDC):
            V(lambda v, dcx=dcx: v.tensor_tensor_scan(
                bigA[:, dcx * TC: dcx * TC + TC], ONES2K, bigA[:, dcx * TC: dcx * TC + TC],
                0.0, ALU.mult, ALU.add),
              reads=[("keys", dcx), "ones2k"], writes=[("keys", dcx)])
            V(lambda v, dcx=dcx: v.tensor_copy(tot_sb[:, 16 + dcx:17 + dcx],
                                               bigA[:, dcx * TC + TC - 1: dcx * TC + TC]),
              reads=[("keys", dcx)], writes=["tot2"])
        dmag(lambda g: g.dma_start(out=cc3i_h, in_=tot_sb[:, 16:24]),
             reads=["tot2"], writes=["cc3i"])
        p.step("gpsimd", lambda g: g.collective_compute(
            "AllGather", ALU.bypass, replica_groups=[list(range(NCORES))],
            ins=[cc3i_h.opt()], outs=[cc3o_h.opt()]),
            reads=["cc3i"], writes=["cc3o"], cc=True)
        dmag(lambda g: g.dma_start(out=gath_sb[:].rearrange("p (r c) -> p r c", r=8),
                                   in_=cc3o_h.transpose([1, 0, 2])),
             reads=["cc3o"], writes=["gath"])
        V(lambda v: v.memset(tot_sb[:, 24:32], 0.0), writes=["carry2"])
        for r8 in range(8):
            V(lambda v, r8=r8: v.scalar_tensor_tensor(
                tot_sb[:, 32:40], gath_sb[:, r8 * 8:(r8 + 1) * 8], mexc_sb[:, r8:r8 + 1],
                tot_sb[:, 24:32], ALU.mult, ALU.add),
              reads=["gath", "mexc", "carry2"], writes=["ctmp"])
            V(lambda v: v.tensor_copy(tot_sb[:, 24:32], tot_sb[:, 32:40]),
              reads=["ctmp"], writes=["carry2"])
        for dcx in range(NDC):
            V(lambda v, dcx=dcx: v.tensor_scalar(
                bigA[:, dcx * TC: dcx * TC + TC], bigA[:, dcx * TC: dcx * TC + TC],
                tot_sb[:, 24 + dcx:25 + dcx], None, ALU.add),
              reads=[("keys", dcx), "carry2"], writes=[("keys", dcx)])
        pf3 = make_stream(wvb_h, "wkS", pos)
        pf3(); pf3()
        for it in range(NT256):
            t0 = it * 256
            emit_h_tile(it)
            if dbg and it == 6 and not dram.get("dbgH"):
                dram["dbgH"] = nc.dram_tensor("dbgH", [128, 2048], F32, kind="ExternalOutput").ap()
                dmas(lambda s: s.dma_start(out=dram["dbgH"], in_=bigB[:, 2048:4096]),
                     reads=[("h", i) for i in range(NDC)], writes=["dbgH"])
            for dco in range(NDC):
                slot = (it * NDC + dco) % 2
                pu = psU0 if dco % 2 == 0 else psU1
                pur = "psU%d" % (dco % 2)
                for dcx in range(NDC):
                    TT(lambda t, dcx=dcx, slot=slot, pu=pu: t.matmul(
                        pu[:, 0:256], wk_sb[:, slot * 1024 + dcx * 128: slot * 1024 + (dcx + 1) * 128],
                        HT2(dcx), start=(dcx == 0), stop=(dcx == NDC - 1)),
                       reads=[("wkS", slot), ("h", dcx)], writes=[pur])
                S(lambda s, dco=dco, pu=pu: s.activation(
                    VT4[:, 0:256], pu[:, 0:256], AF.Identity,
                    bias=battn_sb[:, 16 + dco:17 + dco], scale=1.0),
                  reads=[pur, "battn"], writes=["vtmp"])
                V(lambda v, dco=dco, t0=t0: v.tensor_mul(
                    VT4[:, 0:256], VT4[:, 0:256],
                    bigA[:, dco * TC + t0: dco * TC + t0 + 256]),
                  reads=["vtmp", ("keys", dco)], writes=["vtmp"])
                V(lambda v, dco=dco, t0=t0: v.tensor_add(
                    xc(dco, t0, 256), xc(dco, t0, 256), VT4[:, 0:256]),
                  reads=["vtmp", ("x", dco)], writes=[("x", dco)])
                pf3()

    if "P5" in phases:
        emit_ffn(1, do_split=("P3" not in phases), fence=p.fence(["ct", "st", "t2", "t3", "ctm", "stm", "rowk", "rowi", "strip0",
                "lnrow", "xh", "dsc", "dly", "dlx", "d5", "dsxy", "dsx", "dsy",
                "dsxx", "dnum", "dden", "dt", "dq", "dD", "dom", "da", "db",
                "dcnt", "fx1"] + [("x1", i) for i in range(NDC)]
            + [("keys", i) for i in range(NDC)]))

    dmas(lambda s: s.dma_start(
        out=yT_h.rearrange("(c p) t -> c p t", p=128).transpose([1, 0, 2]),
        in_=x_sb[:].rearrange("p (c t) -> p c t", c=NDC)),
        reads=[("x", dcx) for dcx in range(NDC)], writes=["yout"])
    if dbg:
        dbgA_h = nc.dram_tensor("dbgA", [128, 16384], F32, kind="ExternalOutput").ap()
        dbgB_h = nc.dram_tensor("dbgB", [128, 13312], F32, kind="ExternalOutput").ap()
        dbgr_h = nc.dram_tensor("dbgr", [1, 64], F32, kind="ExternalOutput").ap()
        dbgc_h = nc.dram_tensor("dbgc", [128, 48], F32, kind="ExternalOutput").ap()
        allreg = list(p.region_w.keys())
        dmas(lambda s: s.dma_start(out=dbgA_h, in_=bigA[:]), reads=allreg, writes=["dbg1"])
        dmas(lambda s: s.dma_start(out=dbgB_h, in_=bigB[:]), reads=[], writes=["dbg2"])
        dmas(lambda s: s.dma_start(out=dbgr_h, in_=g1_sb[:]), reads=[], writes=["dbg3"])
        dmas(lambda s: s.dma_start(out=dbgc_h, in_=bias_sb[:]), reads=[], writes=["dbg4"])
        dbgt_h = nc.dram_tensor("dbgt", [128, 40], F32, kind="ExternalOutput").ap()
        dmas(lambda s: s.dma_start(out=dbgt_h, in_=tot_sb[:]), reads=[], writes=["dbg5"])

    p.emit(nc, block, sems)
    es.close()
    return nc


def _prep_inputs(inputs):
    f32 = np.float32
    x = np.asarray(inputs["L0_attn_a_x"], f32).reshape(T, D)
    Wel = np.asarray(inputs["L0_attn_a_W_el"], f32)
    bel = np.asarray(inputs["L0_attn_a_b_el"], f32).reshape(24, 1)
    Wle = np.asarray(inputs["L0_attn_a_W_le"], f32)
    ble = np.ascontiguousarray(np.asarray(inputs["L0_attn_a_b_le"], f32).reshape(8, 128).T)
    A = np.array([[1, 1, 1, 0, 1, 1, 1, 0, 0, 0, 1, 1], [1, 1, 0, 1, 1, 1, 0, 0, 0, 1, 1, 0],
                  [1, 0, 1, 1, 1, 0, 0, 0, 1, 1, 0, 1], [0, 1, 1, 1, 0, 0, 0, 1, 1, 0, 1, 1],
                  [0, 0, 0, 0, 1, 1, 1, 1, 1, 1, 1, 1], [1, 1, 1, 1, 0, 0, 0, 0, 0, 0, 0, 0],
                  [0, 0, 0, 0, 1, 1, 1, 1, 0, 0, 0, 0], [1, 1, 0, 0, 1, 1, 0, 0, 1, 1, 0, 0],
                  [1, 0, 1, 0, 1, 0, 1, 0, 1, 0, 1, 0], [0, 1, 0, 1, 0, 1, 0, 1, 0, 1, 0, 1],
                  [1, 0, 0, 1, 1, 0, 0, 1, 1, 0, 0, 1], [0, 1, 1, 0, 0, 1, 1, 0, 0, 1, 1, 0]],
                 dtype=f32)
    H = np.concatenate([A.T, np.eye(12, dtype=f32)], axis=1)
    HT = np.ascontiguousarray(H.T)

    def colblocks(W):
        W4 = W.reshape(8, 128, 8, 128)
        return np.ascontiguousarray(W4.transpose(0, 3, 2, 1))

    wkb = colblocks(np.asarray(inputs["L0_attn_b_Wk"], f32)).reshape(8, 128, 1024)
    wkab = colblocks(np.asarray(inputs["L0_attn_b_Wka"], f32)).reshape(8, 128, 1024)
    wvb = colblocks(np.asarray(inputs["L0_attn_b_Wv"], f32)).reshape(8, 128, 1024)

    def ppcols(v):
        return np.ascontiguousarray(np.asarray(v, f32).reshape(8, 128).T)

    battn = np.concatenate([ppcols(inputs["L0_attn_b_bk"]), ppcols(inputs["L0_attn_b_bka"]),
                            ppcols(inputs["L0_attn_b_bv"]), ppcols(inputs["L0_attn_b_gamma"]),
                            ppcols(inputs["L0_attn_b_beta"])], axis=1)
    wel_blk = np.ascontiguousarray(Wel.reshape(8, 128, 24))
    rwb = np.ascontiguousarray(np.asarray(inputs["L0_ffn_router_w"], f32).T.reshape(8, 128, 8))
    b1 = np.asarray(inputs["L0_ffn_b1"], f32)
    b1v = np.ascontiguousarray(b1.reshape(8, 32, 128).transpose(2, 0, 1).reshape(128, 256))
    b2l = np.ascontiguousarray(np.asarray(inputs["L0_ffn_b2"], f32))
    w1 = np.asarray(inputs["L0_ffn_w1"], f32)
    w2 = np.asarray(inputs["L0_ffn_w2"], f32)
    w1b = np.ascontiguousarray(
        w1.reshape(8, 32, 128, 8, 128).transpose(0, 1, 4, 3, 2).reshape(8, 32, 128, 1024))
    w2b = np.ascontiguousarray(
        w2.reshape(8, 8, 128, 32, 128).transpose(0, 3, 4, 1, 2).reshape(8, 32, 128, 1024))
    sel = np.zeros((8, 8 * 128), f32)
    for e in range(8):
        sel[e, e * 128:(e + 1) * 128] = 1.0
    id128 = np.eye(128, dtype=f32)
    lxc = np.array([[2, 4, 8, 16, 32, 1, 1, 1]], dtype=f32)
    xhead = np.ascontiguousarray(x[0:1, :])

    shared = dict(wel=wel_blk, bel=bel, wle=Wle, ble=ble, htm=HT,
                  wkb=wkb, wkab=wkab, wvb=wvb, battn=battn, rwb=rwb,
                  b1v=b1v, b2l=b2l, sel=sel, id128=id128,
                  w1b=w1b, w2b=w2b, lxc=lxc, xhead=xhead)
    in_maps = []
    for j in range(NCORES):
        sh = x[j * TC:(j + 1) * TC, :]
        m = dict(shared)
        m["xT"] = np.ascontiguousarray(sh.T)
        m["mexc"] = np.broadcast_to(
            (np.arange(8) < j).astype(f32), (128, 8)).copy()
        in_maps.append(m)
    return in_maps


def kernel(**inputs):
    phases = inputs.pop("_phases", ALL_PHASES)
    dbg = inputs.pop("_dbg", False)
    in_maps = _prep_inputs(inputs)
    nc = build_nc(phases=phases, dbg=dbg)
    res = run_bass_kernel_spmd(nc, in_maps, core_ids=list(range(NCORES)))
    out = np.empty((T, D), np.float32)
    for j in range(NCORES):
        out[j * TC:(j + 1) * TC, :] = res.results[j]["yT"].T
    out = out.reshape(1, T, D)
    if dbg:
        return out, res
    return out


# version 07
# speedup vs baseline: 1.1061x; 1.0117x over previous
import numpy as np
import concourse.bass as bass
import concourse.mybir as mybir
from concourse.bass_utils import run_bass_kernel_spmd

F32 = mybir.dt.float32
BF16 = mybir.dt.bfloat16
I32 = mybir.dt.int32
AF = mybir.ActivationFunctionType
ALU = mybir.AluOpType
AX = mybir.AxisListType

NCORES = 8
T = 16384
TC = T // NCORES
D = 1024
NDC = 8
PI = float(np.pi)
TWO_PI = float(2 * np.pi)
HALF_PI = float(np.pi / 2)
LN_EPS = 1e-5
BIG_NEG = -1.0e30

ALL_PHASES = ("P1", "P2", "P3", "P4", "P5")
P4_STOP = 99


class Prog:
    QUEUES = ("qs", "qg", "qa")

    def __init__(self):
        self.steps = []
        self.region_w = {}
        self.region_r = {}

    def step(self, eng, fn, reads=(), writes=(), dma=None, cc=False, deps=()):
        dep = set(d for d in deps if d is not None)
        for rg in reads:
            w = self.region_w.get(rg)
            if w is not None:
                dep.add(w)
        for rg in writes:
            w = self.region_w.get(rg)
            if w is not None:
                dep.add(w)
            dep.update(self.region_r.get(rg, ()))
        sid = len(self.steps)
        counter = "cc" if cc else (dma if dma is not None else eng)
        self.steps.append([eng, fn, counter, sorted(dep)])
        for rg in reads:
            self.region_r.setdefault(rg, set()).add(sid)
        for rg in writes:
            self.region_w[rg] = sid
            self.region_r[rg] = set()
        return sid

    def fence(self, regions):
        out = set()
        for rg in regions:
            w = self.region_w.get(rg)
            if w is not None:
                out.add(w)
            out.update(self.region_r.get(rg, ()))
        return sorted(out)

    def emit(self, nc, block, sems):
        steps = self.steps
        needed = set()
        for st in steps:
            needed.update(st[3])
        ordinal = [None] * len(steps)
        counts = {}
        for sid, st in enumerate(steps):
            c = st[2]
            if sid in needed or c == "cc" or c in self.QUEUES:
                ordinal[sid] = counts.get(c, 0)
                counts[c] = ordinal[sid] + 1
        totals = {c: k * (16 if c in self.QUEUES else 1) for c, k in counts.items()}

        def wait_of(sid):
            c = steps[sid][2]
            v = ordinal[sid] + 1
            return c, (v * 16 if c in self.QUEUES else v)

        def emit_engine(eng_name, eng_obj):
            last = {}
            for sid, st in enumerate(steps):
                issuer, fn, counter, deps = st
                if issuer != eng_name:
                    continue
                for d in deps:
                    dc_, dv = wait_of(d)
                    if (eng_name == "tensor" and dc_ == "tensor"
                            and steps[d][2] == "tensor"):
                        continue
                    if last.get(dc_, 0) < dv:
                        eng_obj.wait_ge(sems[dc_], dv)
                        last[dc_] = dv
                inst = fn(eng_obj)
                if ordinal[sid] is not None:
                    inst.then_inc(sems[counter], 16 if counter in self.QUEUES else 1)

        @block.sync
        def _(sync):
            emit_engine("sync", sync)
            for c in sorted(totals):
                if c != "sync" and totals[c]:
                    sync.wait_ge(sems[c], totals[c])

        @block.scalar
        def _(scalar):
            emit_engine("scalar", scalar)

        @block.vector
        def _(vector):
            emit_engine("vector", vector)

        @block.tensor
        def _(tensor):
            emit_engine("tensor", tensor)

        @block.gpsimd
        def _(gpsimd):
            emit_engine("gpsimd", gpsimd)


def build_nc(phases=ALL_PHASES, dbg=False):
    import contextlib
    nc = bass.Bass("TRN2", target_bir_lowering=False)
    dram = {}

    def din(name, shape, dtype=F32):
        dram[name] = nc.dram_tensor(name, shape, dtype, kind="ExternalInput").ap()
        return dram[name]

    xT_h = din("xT", [D, TC])
    xhead_h = din("xhead", [1, D])
    wel_h = din("wel", [NDC, 128, 24])
    bel_h = din("bel", [24, 1])
    wle_h = din("wle", [24, D])
    ble_h = din("ble", [128, 8])
    ht_h = din("htm", [24, 12])
    wkb_h = din("wkb", [NDC, 128, D])
    wkab_h = din("wkab", [NDC, 128, D])
    wvb_h = din("wvb", [NDC, 128, D])
    battn_h = din("battn", [128, 40])
    rwb_h = din("rwb", [NDC, 128, 8])
    b1_h = din("b1v", [128, 8 * 32])
    b2l_h = din("b2l", [8, D])
    sel_h = din("sel", [8, 8 * 128])
    id128_h = din("id128", [128, 128])
    w1_h = din("w1b", [8, 32, 128, D])
    w2_h = din("w2b", [8, 32, 128, D])
    mexc_h = din("mexc", [128, 8])
    lx_h = din("lxc", [1, 8])

    yT_h = nc.dram_tensor("yT", [D, TC], F32, kind="ExternalOutput").ap()

    w1s_h = nc.dram_tensor("w1s", [8, 16, 128, 4096], BF16).ap()
    w2s_h = nc.dram_tensor("w2s", [8, 16, 128, 4096], BF16).ap()
    ew_h = nc.dram_tensor("ew_spill", [NDC, 128, TC], F32,
                          kind=("ExternalOutput" if dbg else "Internal")).ap()
    cc1i_h = nc.dram_tensor("cc1i", [1, 8], F32).ap()
    cc1o_h = nc.dram_tensor("cc1o", [8, 8], F32).ap()
    cc2i_h = nc.dram_tensor("cc2i", [128, 8], F32).ap()
    cc2o_h = nc.dram_tensor("cc2o", [8, 128, 8], F32).ap()
    cc3i_h = nc.dram_tensor("cc3i", [128, 8], F32).ap()
    cc3o_h = nc.dram_tensor("cc3o", [8, 128, 8], F32).ap()

    es = contextlib.ExitStack()

    def sb(name, shape, dtype=F32):
        return es.enter_context(nc.sbuf_tensor(name, shape, dtype))

    x_sb = sb("x_sb", [128, NDC * TC])
    bigA = sb("bigA", [128, 16384])
    bigB = sb("bigB", [128, 13824])
    wk_sb = sb("wk_sb", [128, 2048])
    wel_sb = sb("wel_sb", [128, NDC * 24])
    ht_sb = sb("ht_sb", [24, 12])
    b1_sb = sb("b1_sb", [128, 8 * 32])
    b2l_sb = sb("b2l_sb", [8, D])
    sel_sb = sb("sel_sb", [8, 8 * 128])
    id128_sb = sb("id128_sb", [128, 128])
    rl_sb = sb("rl_sb", [8, 512])
    gates_sb = sb("gates_sb", [8, 512])
    gtm_sb = sb("gtm_sb", [128, 64])
    battn_sb = sb("battn_sb", [128, 48])
    bias_sb = sb("bias_sb", [128, 48])
    rwb_sb = sb("rwb_sb", [128, NDC * 8])
    rws_sb = sb("rws_sb", [128, 64])
    mexc_sb = sb("mexc_sb", [128, 8])
    tot_sb = sb("tot_sb", [128, 40])
    mm_sb = sb("mm_sb", [128, 8])
    ones128 = sb("ones128", [128, 1])
    onesr = sb("onesr", [1, 128])
    ones12 = sb("ones12", [12, 1])
    kid_i = sb("kid_i", [128, 8], I32)
    gath_sb = sb("gath_sb", [128, 64])
    g1_sb = sb("g1_sb", [1, 64])

    def psmk(name, shape):
        return es.enter_context(nc.psum_tensor(name, shape, F32))

    psA = psmk("psA", [128, 512])
    psB = psmk("psB", [128, 512])
    psU0 = psmk("psU0", [128, 512])
    psU1 = psmk("psU1", [128, 512])
    psD0 = psmk("psD0", [128, 512])
    psD1 = psmk("psD1", [128, 512])
    psC = psmk("psC", [128, 1024])

    sems = {}
    for cname in ("sync", "scalar", "vector", "tensor", "gpsimd", "qs", "qg", "qa", "cc"):
        sems[cname] = es.enter_context(nc.semaphore("sem_" + cname))

    block = es.enter_context(nc.Block())
    p = Prog()

    def xc(dc, t0, tn):
        return x_sb[:, dc * TC + t0: dc * TC + t0 + tn]

    def dmas(fn, reads=(), writes=(), deps=()):
        return p.step("sync", fn, reads=reads, writes=writes, dma="qs", deps=deps)

    def dmag(fn, reads=(), writes=(), deps=()):
        return p.step("gpsimd", fn, reads=reads, writes=writes, dma="qg", deps=deps)

    def dmaa(fn, reads=(), writes=(), deps=()):
        return p.step("scalar", fn, reads=reads, writes=writes, dma="qa", deps=deps)

    def V(fn, reads=(), writes=(), deps=()):
        return p.step("vector", fn, reads=reads, writes=writes, deps=deps)

    def S(fn, reads=(), writes=(), deps=()):
        return p.step("scalar", fn, reads=reads, writes=writes, deps=deps)

    def TT(fn, reads=(), writes=(), deps=()):
        return p.step("tensor", fn, reads=reads, writes=writes, deps=deps)

    def G(fn, reads=(), writes=(), deps=()):
        return p.step("gpsimd", fn, reads=reads, writes=writes, deps=deps)

    dmas(lambda s: s.dma_start(
        out=x_sb[:].rearrange("p (c t) -> p c t", c=NDC),
        in_=xT_h.rearrange("(c p) t -> c p t", p=128).transpose([1, 0, 2])),
        writes=[("x", dc) for dc in range(NDC)])
    dmas(lambda s: s.dma_start(
        out=wel_sb[:].rearrange("p (c m) -> p c m", c=NDC),
        in_=wel_h.transpose([1, 0, 2])), writes=["wel"])
    dmas(lambda s: s.dma_start(out=bigB[0:24, 8192:9216], in_=wle_h), writes=["wle"])
    dmas(lambda s: s.dma_start(out=ht_sb[:], in_=ht_h), writes=["htm"])
    dmas(lambda s: s.dma_start(out=battn_sb[:, 0:40], in_=battn_h), writes=["battn"])
    dmas(lambda s: s.dma_start(out=b1_sb[:], in_=b1_h), writes=["b1"])
    dmas(lambda s: s.dma_start(out=b2l_sb[:], in_=b2l_h), writes=["b2l"])
    dmas(lambda s: s.dma_start(out=sel_sb[:], in_=sel_h), writes=["sel"])
    dmas(lambda s: s.dma_start(out=id128_sb[:], in_=id128_h), writes=["id128"])
    dmas(lambda s: s.dma_start(
        out=rwb_sb[:].rearrange("p (c e) -> p c e", c=NDC),
        in_=rwb_h.transpose([1, 0, 2])), writes=["rwb"])
    dmas(lambda s: s.dma_start(out=mexc_sb[:], in_=mexc_h), writes=["mexc"])
    dmas(lambda s: s.dma_start(out=bias_sb[:, 0:8], in_=ble_h), writes=["ble"])
    dmas(lambda s: s.dma_start(out=g1_sb[0:1, 32:40], in_=lx_h), writes=["lxrow"])
    dmas(lambda s: s.dma_start(out=bias_sb[0:24, 32:33], in_=bel_h), writes=["bel"])

    V(lambda v: v.memset(ones128[:], 1.0), writes=["ones128"])
    V(lambda v: v.memset(onesr[:], 1.0), writes=["onesr"])
    V(lambda v: v.memset(ones12[:], 1.0), writes=["ones12"])
    V(lambda v: v.memset(bias_sb[:, 24:25], HALF_PI), writes=["hpi"])
    V(lambda v: v.tensor_scalar(bias_sb[0:24, 33:34], bias_sb[0:24, 32:33], -1.0, None, ALU.mult),
      reads=["bel"], writes=["negbel"])
    V(lambda v: v.tensor_scalar(battn_sb[:, 40:48], battn_sb[:, 8:16], float(1.0 / 32.0), None, ALU.mult),
      reads=["battn"], writes=["bka32"])
    G(lambda g: g.iota(kid_i[:, :], [[128, 8]], channel_multiplier=1), writes=["kid"])
    rws_bf = rws_sb[:].bitcast(BF16)
    V(lambda v: v.tensor_copy(rws_bf[:, 0:64], rwb_sb[:]), reads=["rwb"], writes=["rws"])
    V(lambda v: v.tensor_sub(rws_bf[:, 64:128], rwb_sb[:], rws_bf[:, 0:64]),
      reads=["rwb", "rws"], writes=["rws"])

    def emit_ln(tn, src_ap, dst_ap, sq_ap, rowb, src_regions, dst_regions,
                sq_region, row_region, post_chunk=None):
        for dcx in range(NDC):
            TT(lambda t, dcx=dcx: t.matmul(psA[0:1, 0:tn], ones128[:], src_ap(dcx),
                                           start=(dcx == 0), stop=(dcx == NDC - 1)),
               reads=[src_regions[dcx], "ones128"], writes=["psA"])
        sqv = sq_ap[:, 0:tn]
        for dcx in range(NDC):
            S(lambda s, dcx=dcx: s.activation(sqv, src_ap(dcx), AF.Square),
              reads=[src_regions[dcx]], writes=[sq_region])
            TT(lambda t, dcx=dcx: t.matmul(psB[0:1, 0:tn], ones128[:], sqv,
                                           start=(dcx == 0), stop=(dcx == NDC - 1)),
               reads=[sq_region, "ones128"], writes=["psB"])
        V(lambda v: v.tensor_scalar(rowb(0), psA[0:1, 0:tn], float(1.0 / D), None, ALU.mult),
          reads=["psA"], writes=[row_region])
        V(lambda v: v.tensor_scalar(rowb(1), psB[0:1, 0:tn], float(1.0 / D), None, ALU.mult),
          reads=["psB", row_region], writes=[row_region])
        V(lambda v: v.tensor_mul(rowb(2), rowb(0), rowb(0)),
          reads=[row_region], writes=[row_region])
        V(lambda v: v.tensor_sub(rowb(1), rowb(1), rowb(2)),
          reads=[row_region], writes=[row_region])
        V(lambda v: v.tensor_scalar(rowb(1), rowb(1), LN_EPS, None, ALU.add),
          reads=[row_region], writes=[row_region])
        V(lambda v: v.reciprocal(rowb(2), rowb(1)),
          reads=[row_region], writes=[row_region])
        S(lambda s: s.activation(rowb(1), rowb(2), AF.Sqrt),
          reads=[row_region], writes=[row_region])
        TT(lambda t: t.matmul(psD0[:, 0:tn], onesr[:], rowb(0), start=True, stop=True),
           reads=[row_region, "onesr"], writes=["psD0"])
        TT(lambda t: t.matmul(psD1[:, 0:tn], onesr[:], rowb(1), start=True, stop=True),
           reads=[row_region, "onesr"], writes=["psD1"])
        for dcx in range(NDC):
            V(lambda v, dcx=dcx: v.tensor_sub(dst_ap(dcx), src_ap(dcx), psD0[:, 0:tn]),
              reads=[src_regions[dcx], "psD0"], writes=[dst_regions[dcx]])
            V(lambda v, dcx=dcx: v.tensor_mul(dst_ap(dcx), dst_ap(dcx), psD1[:, 0:tn]),
              reads=[dst_regions[dcx], "psD1"], writes=[dst_regions[dcx]])
            if post_chunk is not None:
                post_chunk(dcx)

    NT512 = TC // 512
    rowslot = lambda i: bigA[0:1, 10240 + i * 512: 10240 + (i + 1) * 512]
    rowk_f = bigA[0:1, 12288:13312]
    rowi_i = bigA[0:1, 13312:14336].bitcast(I32)
    arow = lambda a, b: bigA[0:1, 14336 + a: 14336 + b]
    asc = lambda i: bigA[0:1, 15360 + i: 15360 + i + 1]
    fx1_row = bigA[0:1, 15488:16000]
    ubuf = lambda dcx: bigB[:, dcx * 512:(dcx + 1) * 512]
    wle_sb = bigB[0:24, 8192:9216]
    bits_sb = bigB[0:24, 9216:9728]
    syn_sb = bigB[0:12, 9728:10240]
    sqbuf = bigB[:, 4096:4608]
    dcmp = bigB[0:1, 4608:5632]
    fx2_row = bigB[0:1, 5632:6144]
    onetwob = bigB[0:12, 6144:6656]
    aoutb = bigB[:, 6656:7168]
    dsq = bigB[0:1, 7168:8192]

    if "P1" in phases:
        G(lambda g: g.iota(rowi_i, [[1, 1024]], channel_multiplier=0), writes=["rowi"])
        V(lambda v: v.tensor_copy(rowk_f, rowi_i), reads=["rowi"], writes=["rowk"])
        V(lambda v: v.memset(mm_sb[:, 0:2], BIG_NEG), writes=["mmacc"])
        for it in range(NT512):
            t0 = it * 512
            emit_ln(512, lambda dcx, t0=t0: xc(dcx, t0, 512), ubuf, sqbuf, rowslot,
                    [("x", dcx) for dcx in range(NDC)],
                    [("u", dcx) for dcx in range(NDC)], "sq", "lnrow")
            for dcx in range(NDC):
                V(lambda v, dcx=dcx: v.reduce_max(mm_sb[:, 2:3], ubuf(dcx), axis=AX.X),
                  reads=[("u", dcx)], writes=["mmtmp"])
                V(lambda v: v.tensor_max(mm_sb[:, 0:1], mm_sb[:, 0:1], mm_sb[:, 2:3]),
                  reads=["mmtmp", "mmacc"], writes=["mmacc"])
                V(lambda v, dcx=dcx: v.tensor_reduce(mm_sb[:, 2:3], ubuf(dcx), AX.X, ALU.min, negate=True),
                  reads=[("u", dcx)], writes=["mmtmp"])
                V(lambda v: v.tensor_max(mm_sb[:, 1:2], mm_sb[:, 1:2], mm_sb[:, 2:3]),
                  reads=["mmtmp", "mmacc"], writes=["mmacc"])
        TT(lambda t: t.transpose(psC[0:2, 0:128], mm_sb[:, 0:2], id128_sb[:]),
           reads=["mmacc", "id128"], writes=["psC"])
        V(lambda v: v.reduce_max(mm_sb[0:2, 4:5], psC[0:2, 0:128], axis=AX.X),
          reads=["psC"], writes=["mmred"])
        dmas(lambda s: s.dma_start(out=g1_sb[0:1, 0:2], in_=mm_sb[0:2, 4:5]),
             reads=["mmred"], writes=["e1row"])
        V(lambda v: v.memset(g1_sb[0:1, 2:8], 0.0), writes=["e1pad"])
        dmag(lambda g: g.dma_start(out=cc1i_h, in_=g1_sb[0:1, 0:8]),
             reads=["e1row", "e1pad"], writes=["cc1i"])
        p.step("gpsimd", lambda g: g.collective_compute(
            "AllGather", ALU.bypass, replica_groups=[list(range(NCORES))],
            ins=[cc1i_h.opt()], outs=[cc1o_h.opt()]),
            reads=["cc1i"], writes=["cc1o"], cc=True)
        dmag(lambda g: g.dma_start(out=gath_sb[0:2, 0:8], in_=cc1o_h.transpose([1, 0])[0:2], allow_slow_non_contiguous=True),
             reads=["cc1o"], writes=["e1g"])
        V(lambda v: v.reduce_max(gath_sb[0:2, 8:9], gath_sb[0:2, 0:8], axis=AX.X),
          reads=["e1g"], writes=["e1red"])
        dmas(lambda s: s.dma_start(out=g1_sb[0:1, 8:10], in_=gath_sb[0:2, 8:9]),
             reads=["e1red"], writes=["gmm"])

        dmas(lambda s: s.dma_start(out=arow(0, 1024), in_=xhead_h), writes=["xh"])
        V(lambda v: v.reduce_sum(asc(0), arow(0, 1024), axis=AX.X),
          reads=["xh"], writes=["dsc"])
        V(lambda v: v.tensor_scalar(asc(0), asc(0), float(1.0 / D), None, ALU.mult),
          reads=["dsc"], writes=["dsc"])
        S(lambda s: s.activation(dsq, arow(0, 1024), AF.Square),
          reads=["xh"], writes=["dsqr"])
        V(lambda v: v.reduce_sum(asc(1), dsq, axis=AX.X),
          reads=["dsqr", "dsc"], writes=["dsc"])
        V(lambda v: v.tensor_scalar(asc(1), asc(1), float(1.0 / D), None, ALU.mult),
          reads=["dsc"], writes=["dsc"])
        V(lambda v: v.tensor_mul(asc(2), asc(0), asc(0)), reads=["dsc"], writes=["dsc"])
        V(lambda v: v.tensor_sub(asc(1), asc(1), asc(2)), reads=["dsc"], writes=["dsc"])
        V(lambda v: v.tensor_scalar(asc(1), asc(1), LN_EPS, None, ALU.add),
          reads=["dsc"], writes=["dsc"])
        V(lambda v: v.reciprocal(asc(2), asc(1)), reads=["dsc"], writes=["dsc"])
        S(lambda s: s.activation(asc(3), asc(2), AF.Sqrt), reads=["dsc"], writes=["dsc"])
        V(lambda v: v.tensor_scalar(arow(0, 1024), arow(0, 1024), asc(0), None, ALU.subtract),
          reads=["xh", "dsc"], writes=["xh"])
        V(lambda v: v.tensor_scalar(arow(0, 1024), arow(0, 1024), asc(3), None, ALU.mult),
          reads=["xh", "dsc"], writes=["xh"])
        V(lambda v: v.tensor_add(asc(4), g1_sb[0:1, 8:9], g1_sb[0:1, 9:10]),
          reads=["gmm"], writes=["dsc"])
        V(lambda v: v.tensor_scalar(asc(4), asc(4), 1e-10, None, ALU.add),
          reads=["dsc"], writes=["dsc"])
        V(lambda v: v.reciprocal(asc(5), asc(4)), reads=["dsc"], writes=["dsc"])
        V(lambda v: v.tensor_scalar(arow(0, 1024), arow(0, 1024), g1_sb[0:1, 9:10], None, ALU.add),
          reads=["xh", "gmm"], writes=["xh"])
        V(lambda v: v.tensor_scalar(arow(0, 1024), arow(0, 1024), asc(5), None, ALU.mult),
          reads=["xh", "dsc"], writes=["xh"])
        for i, sc2 in enumerate((4, 16, 64, 256, 1024)):
            V(lambda v, sc2=sc2: v.reduce_sum(asc(6), arow(0, sc2), axis=AX.X),
              reads=["xh"], writes=["dsc"])
            V(lambda v, sc2=sc2: v.tensor_scalar(asc(6), asc(6), float(1.0 / sc2), None, ALU.mult),
              reads=["dsc"], writes=["dsc"])
            V(lambda v, sc2=sc2: v.tensor_scalar(dcmp[0:1, 0:sc2], arow(0, sc2), asc(6), None, ALU.is_gt),
              reads=["xh", "dsc"], writes=["dcmpr"])
            V(lambda v, sc2=sc2, i=i: v.reduce_sum(asc(16 + i), dcmp[0:1, 0:sc2], axis=AX.X),
              reads=["dcmpr"], writes=["dcnt"])
            V(lambda v, i=i: v.tensor_scalar(asc(16 + i), asc(16 + i), 1.0, None, ALU.max),
              reads=["dcnt"], writes=["dcnt"])
        S(lambda s: s.activation(bigA[0:1, 15376:15381], bigA[0:1, 15376:15381], AF.Ln),
          reads=["dcnt"], writes=["dly"])
        S(lambda s: s.activation(bigA[0:1, 15384:15389], g1_sb[0:1, 32:37], AF.Ln),
          reads=["lxrow"], writes=["dlx"])
        ly = bigA[0:1, 15376:15381]
        lx = bigA[0:1, 15384:15389]
        tmp5 = bigA[0:1, 15392:15397]
        V(lambda v: v.tensor_mul(tmp5, lx, ly), reads=["dly", "dlx"], writes=["d5"])
        V(lambda v: v.reduce_sum(asc(40), tmp5, axis=AX.X), reads=["d5"], writes=["dsxy"])
        V(lambda v: v.reduce_sum(asc(41), lx, axis=AX.X), reads=["dlx"], writes=["dsx"])
        V(lambda v: v.reduce_sum(asc(42), ly, axis=AX.X), reads=["dly"], writes=["dsy"])
        V(lambda v: v.tensor_mul(tmp5, lx, lx), reads=["dlx", "d5"], writes=["d5"])
        V(lambda v: v.reduce_sum(asc(43), tmp5, axis=AX.X), reads=["d5"], writes=["dsxx"])
        V(lambda v: v.tensor_scalar(asc(44), asc(40), 5.0, None, ALU.mult),
          reads=["dsxy"], writes=["dnum"])
        V(lambda v: v.tensor_mul(asc(45), asc(41), asc(42)),
          reads=["dsx", "dsy"], writes=["dt"])
        V(lambda v: v.tensor_sub(asc(44), asc(44), asc(45)),
          reads=["dnum", "dt"], writes=["dnum"])
        V(lambda v: v.tensor_scalar(asc(46), asc(43), 5.0, None, ALU.mult),
          reads=["dsxx"], writes=["dden"])
        V(lambda v: v.tensor_mul(asc(45), asc(41), asc(41)),
          reads=["dsx"], writes=["dt"])
        V(lambda v: v.tensor_sub(asc(46), asc(46), asc(45)),
          reads=["dden", "dt"], writes=["dden"])
        V(lambda v: v.reciprocal(asc(45), asc(46)), reads=["dden"], writes=["dt"])
        V(lambda v: v.tensor_mul(asc(47), asc(44), asc(45)),
          reads=["dnum", "dt"], writes=["dq"])
        V(lambda v: v.tensor_scalar(asc(47), asc(47), -1.0, 1.0, ALU.mult, ALU.max),
          reads=["dq"], writes=["dq"])
        V(lambda v: v.tensor_scalar(asc(48), asc(47), 2.0, None, ALU.min),
          reads=["dq"], writes=["dD"])
        V(lambda v: v.tensor_scalar(asc(49), asc(48), 0.2, 1.0, ALU.mult, ALU.add),
          reads=["dD"], writes=["dom"])
        V(lambda v: v.tensor_mul(asc(49), asc(49), asc(48)),
          reads=["dom", "dD"], writes=["dom"])
        V(lambda v: v.tensor_scalar(asc(50), asc(48), 0.5, 1.0, ALU.mult, ALU.add),
          reads=["dD"], writes=["da"])
        V(lambda v: v.tensor_scalar(asc(51), asc(48), 0.1, None, ALU.mult),
          reads=["dD"], writes=["db"])
        TT(lambda t: t.matmul(psB[:, 0:3], onesr[:], bigA[0:1, 15409:15412], start=True, stop=True),
           reads=["dom", "da", "db", "onesr"], writes=["psB"])
        V(lambda v: v.tensor_copy(bias_sb[:, 25:28], psB[:, 0:3]),
          reads=["psB"], writes=["dbc"])

        V(lambda v: v.tensor_copy(gtm_sb[:, 0:8], kid_i[:, :]), reads=["kid"], writes=["gt0"])
        V(lambda v: v.tensor_scalar(gtm_sb[:, 8:16], gtm_sb[:, 0:8], -1.0, 1024.0, ALU.mult, ALU.add),
          reads=["gt0"], writes=["gt1"])
        V(lambda v: v.tensor_tensor(gtm_sb[:, 0:8], gtm_sb[:, 0:8], gtm_sb[:, 8:16], ALU.min),
          reads=["gt0", "gt1"], writes=["gt0"])
        V(lambda v: v.tensor_scalar(gtm_sb[:, 0:8], gtm_sb[:, 0:8], float(1.0 / 1024.0), 1e-8, ALU.mult, ALU.add),
          reads=["gt0"], writes=["gt0"])
        V(lambda v: v.tensor_scalar(gtm_sb[:, 8:16], gtm_sb[:, 0:8], bias_sb[:, 26:27], None, ALU.mult),
          reads=["gt0", "dbc"], writes=["gt1"])
        V(lambda v: v.tensor_scalar(gtm_sb[:, 8:16], gtm_sb[:, 8:16], bias_sb[:, 25:26], None, ALU.add),
          reads=["gt1", "dbc"], writes=["gt1"])
        V(lambda v: v.tensor_scalar(gtm_sb[:, 16:24], gtm_sb[:, 8:16], PI, None, ALU.is_gt),
          reads=["gt1"], writes=["gt2"])
        V(lambda v: v.scalar_tensor_tensor(gtm_sb[:, 8:16], gtm_sb[:, 16:24], -TWO_PI,
                                           gtm_sb[:, 8:16], ALU.mult, ALU.add),
          reads=["gt2", "gt1"], writes=["gt1"])
        S(lambda s: s.activation(gtm_sb[:, 16:24], gtm_sb[:, 8:16], AF.Sin),
          reads=["gt1"], writes=["gt2"])
        S(lambda s: s.activation(gtm_sb[:, 24:32], gtm_sb[:, 0:8], AF.Square),
          reads=["gt0"], writes=["gt3"])
        V(lambda v: v.tensor_scalar(gtm_sb[:, 24:32], gtm_sb[:, 24:32], bias_sb[:, 27:28], None, ALU.mult),
          reads=["gt3", "dbc"], writes=["gt3"])
        V(lambda v: v.tensor_scalar(gtm_sb[:, 8:16], gtm_sb[:, 0:8], -1.0, None, ALU.mult),
          reads=["gt0"], writes=["gt1"])
        V(lambda v: v.tensor_scalar(gtm_sb[:, 8:16], gtm_sb[:, 8:16], bias_sb[:, 25:26], None, ALU.add),
          reads=["gt1", "dbc"], writes=["gt1"])
        V(lambda v: v.tensor_add(gtm_sb[:, 8:16], gtm_sb[:, 8:16], gtm_sb[:, 24:32]),
          reads=["gt1", "gt3"], writes=["gt1"])
        S(lambda s: s.activation(gtm_sb[:, 24:32], gtm_sb[:, 8:16], AF.Sin),
          reads=["gt1"], writes=["gt3"])
        V(lambda v: v.tensor_mul(bias_sb[:, 16:24], gtm_sb[:, 16:24], gtm_sb[:, 24:32]),
          reads=["gt2", "gt3"], writes=["fim"])
        S(lambda s: s.activation(gtm_sb[:, 24:32], gtm_sb[:, 8:16], AF.Sin,
                                 bias=bias_sb[:, 24:25], scale=-1.0),
          reads=["gt1", "hpi"], writes=["gt3"])
        V(lambda v: v.tensor_mul(bias_sb[:, 8:16], gtm_sb[:, 16:24], gtm_sb[:, 24:32]),
          reads=["gt2", "gt3"], writes=["fre"])
        V(lambda v: v.tensor_scalar(bias_sb[:, 8:16], bias_sb[:, 8:16], float(1.0 / 1024.0), None, ALU.mult),
          reads=["fre"], writes=["fre"])
        V(lambda v: v.tensor_scalar(bias_sb[:, 16:24], bias_sb[:, 16:24], float(1.0 / 1024.0), None, ALU.mult),
          reads=["fim"], writes=["fim"])
        V(lambda v: v.tensor_scalar(bias_sb[:, 36:44], bias_sb[:, 16:24], -1.0, None, ALU.mult),
          reads=["fim"], writes=["fimn"])

        CT = bigA[:, 0:1024]
        STb = bigA[:, 1024:2048]
        T2b = bigA[:, 2048:3072]
        T3b = bigA[:, 3072:4096]
        CTm = bigA[:, 4096:4224]
        STm = bigA[:, 4224:4352]
        for kc in range(NDC):
            krow = rowk_f[0:1, 128 * kc: 128 * kc + 128]
            for h in range(2):
                pu = psU0 if h == 0 else psU1
                pur = "psU%d" % h
                TT(lambda t, h=h, krow=krow, pu=pu: t.matmul(
                    pu[:, :], krow,
                    rowk_f[0:1, h * 512:(h + 1) * 512], start=True, stop=True),
                   reads=["rowk"], writes=[pur])
                V(lambda v, h=h, pu=pu: v.tensor_copy(
                    T3b[:, h * 512:(h + 1) * 512].bitcast(I32), pu[:, :]),
                  reads=[pur], writes=["t3"])
                V(lambda v, h=h: v.tensor_scalar(
                    T2b[:, h * 512:(h + 1) * 512].bitcast(I32),
                    T3b[:, h * 512:(h + 1) * 512].bitcast(I32), 1023, None, ALU.bitwise_and),
                  reads=["t3"], writes=["t2"])
                V(lambda v, h=h: v.tensor_copy(
                    CT[:, h * 512:(h + 1) * 512], T2b[:, h * 512:(h + 1) * 512].bitcast(I32)),
                  reads=["t2"], writes=["ct"])
            V(lambda v: v.tensor_scalar(T2b, CT, 512.0, None, ALU.is_ge),
              reads=["ct"], writes=["t2"])
            V(lambda v: v.tensor_scalar(T3b, CT, float(TWO_PI / 1024.0), None, ALU.mult),
              reads=["ct"], writes=["t3"])
            V(lambda v: v.scalar_tensor_tensor(STb, T2b, -TWO_PI, T3b, ALU.mult, ALU.add),
              reads=["t2", "t3"], writes=["st"])
            S(lambda s: s.activation(T2b, STb, AF.Abs), reads=["st"], writes=["t2"])
            S(lambda s: s.activation(CT, T2b, AF.Sin, bias=bias_sb[:, 24:25], scale=-1.0),
              reads=["t2", "hpi"], writes=["ct"])
            S(lambda s: s.activation(STb, STb, AF.Sin), reads=["st"], writes=["st"])
            V(lambda v: v.tensor_copy(CTm, CT[:, 0:128]), reads=["ct"], writes=["ctm"])
            V(lambda v: v.tensor_copy(STm, STb[:, 0:128]), reads=["st"], writes=["stm"])
            fre_c = bias_sb[:, 8 + kc: 9 + kc]
            fim_c = bias_sb[:, 16 + kc: 17 + kc]
            fimn_c = bias_sb[:, 36 + kc: 37 + kc]
            V(lambda v, a=fre_c: v.tensor_scalar(T3b, STb, a, None, ALU.mult),
              reads=["st", "fre"], writes=["t3"])
            V(lambda v, a=fim_c: v.scalar_tensor_tensor(T2b, CT, a, T3b, ALU.mult, ALU.add),
              reads=["ct", "t3", "fim"], writes=["t2"])
            V(lambda v, a=fre_c: v.tensor_scalar(T3b, CT, a, None, ALU.mult),
              reads=["ct", "fre"], writes=["t3"])
            V(lambda v, a=fimn_c: v.scalar_tensor_tensor(CT, STb, a, T3b, ALU.mult, ALU.add),
              reads=["st", "t3", "fimn"], writes=["ct"])
            for h in range(2):
                TT(lambda t, h=h, kc=kc: t.matmul(
                    psC[:, h * 512:(h + 1) * 512], CTm, CT[:, h * 512:(h + 1) * 512],
                    start=(kc == 0), stop=False),
                   reads=["ctm", "ct"], writes=["psC"])
                TT(lambda t, h=h, kc=kc: t.matmul(
                    psC[:, h * 512:(h + 1) * 512], STm, T2b[:, h * 512:(h + 1) * 512],
                    start=False, stop=(kc == NDC - 1)),
                   reads=["stm", "t2"], writes=["psC"])
        for h in range(2):
            V(lambda v, h=h: v.tensor_copy(bigA[:, 5120 + h * 512: 5120 + (h + 1) * 512],
                                           psC[:, h * 512:(h + 1) * 512]),
              reads=["psC"], writes=["strip0"])

    if "P2" in phases:
        strip = lambda j: bigA[:, 5120 + j * 128: 5120 + (j + 1) * 128]
        X1 = lambda ncc: bigA[:, 6144 + ncc * 512: 6144 + (ncc + 1) * 512]
        for it in range(NT512):
            t0 = it * 512
            emit_ln(512, lambda dcx, t0=t0: xc(dcx, t0, 512), ubuf, sqbuf, rowslot,
                    [("x", dcx) for dcx in range(NDC)],
                    [("u", dcx) for dcx in range(NDC)], "sq", "lnrow")
            for ncc in range(NDC):
                pu = psU0 if ncc % 2 == 0 else psU1
                pur = "psU%d" % (ncc % 2)
                for mc in range(NDC):
                    TT(lambda t, ncc=ncc, mc=mc, pu=pu: t.matmul(
                        pu[:, :], strip((ncc - mc) % 8), ubuf(mc),
                        start=(mc == 0), stop=(mc == NDC - 1)),
                       reads=["strip0", ("u", mc)], writes=[pur])
                V(lambda v, ncc=ncc, pu=pu: v.tensor_copy(X1(ncc), pu[:, :]),
                  reads=[pur], writes=[("x1", ncc)])
            for dcx in range(NDC):
                TT(lambda t, dcx=dcx: t.matmul(
                    psA[0:24, :], wel_sb[:, dcx * 24:(dcx + 1) * 24], X1(dcx),
                    start=(dcx == 0), stop=(dcx == NDC - 1)),
                   reads=["wel", ("x1", dcx)], writes=["psA"])
            S(lambda s: s.activation(bits_sb, psA[0:24, :], AF.Identity,
                                     bias=bias_sb[0:24, 32:33], scale=1.0),
              reads=["psA", "bel"], writes=["bits"])
            V(lambda v: v.tensor_scalar(bits_sb, bits_sb, 0.0, None, ALU.is_gt),
              reads=["bits"], writes=["bits"])
            TT(lambda t: t.matmul(psB[0:12, :], ht_sb[:, :], bits_sb,
                                  start=True, stop=True),
               reads=["htm", "bits"], writes=["psB"])
            V(lambda v: v.tensor_scalar(syn_sb, psB[0:12, :], 0.5, None, ALU.is_gt),
              reads=["psB"], writes=["syn"])
            TT(lambda t: t.matmul(psA[0:1, :], ones12[:], syn_sb, start=True, stop=True),
               reads=["ones12", "syn"], writes=["psA"])
            V(lambda v: v.tensor_scalar(fx1_row, psA[0:1, :], 0.5, None, ALU.is_gt),
              reads=["psA"], writes=["fx1"])
            V(lambda v: v.tensor_scalar(fx2_row, psA[0:1, :], 3.5, None, ALU.is_le),
              reads=["psA"], writes=["fx2"])
            V(lambda v: v.tensor_mul(fx1_row, fx1_row, fx2_row),
              reads=["fx1", "fx2"], writes=["fx1"])
            TT(lambda t: t.matmul(psB[0:12, :], onesr[0:1, 0:12], fx1_row,
                                  start=True, stop=True),
               reads=["onesr", "fx1"], writes=["psB"])
            V(lambda v: v.tensor_tensor(syn_sb, syn_sb, psB[0:12, :], ALU.mult),
              reads=["syn", "psB"], writes=["syn"])
            V(lambda v: v.tensor_scalar(onetwob, bits_sb[0:12, 0:512], -2.0, 1.0, ALU.mult, ALU.add),
              reads=["bits"], writes=["otb"])
            V(lambda v: v.tensor_mul(syn_sb, syn_sb, onetwob),
              reads=["syn", "otb"], writes=["syn"])
            V(lambda v: v.tensor_add(bits_sb[0:12, 0:512], bits_sb[0:12, 0:512], syn_sb),
              reads=["bits", "syn"], writes=["bits"])
            for dcx in range(NDC):
                TT(lambda t, dcx=dcx: t.matmul(
                    psB[:, :], wle_sb[:, dcx * 128:(dcx + 1) * 128], bits_sb,
                    start=True, stop=True),
                   reads=["wle", "bits"], writes=["psB"])
                S(lambda s, dcx=dcx: s.activation(aoutb, psB[:, :], AF.Identity,
                                                  bias=bias_sb[:, dcx:dcx + 1], scale=1.0),
                  reads=["psB", "ble"], writes=["aout"])
                V(lambda v, dcx=dcx: v.tensor_add(aoutb, aoutb, X1(dcx)),
                  reads=["aout", ("x1", dcx)], writes=["aout"])
                V(lambda v, dcx=dcx, t0=t0: v.tensor_add(xc(dcx, t0, 512), xc(dcx, t0, 512), aoutb),
                  reads=["aout", ("x", dcx)], writes=[("x", dcx)])

    def emit_ffn(call, fence=(), do_split=False):
        bigAb_ = bigA[:].bitcast(BF16)

        SPB2 = bigB[:, 11776:13824]

        def emit_split_for(gi2, fdeps=()):
            if not do_split or gi2 >= 128:
                return
            e2, g2 = gi2 // 16, gi2 % 16
            s2 = gi2 % 4
            for wi, (wsrc, wdst, wtag, cbase, ldq, buf) in enumerate((
                    (w1_h, w1s_h, "w1s", s2 * 8192, dmas, wk_sb[:, 0:2048]),
                    (w2_h, w2s_h, "w2s", s2 * 8192 + 4096, dmag, SPB2))):
                wr = "wst%d" % (wi + 1)
                hi = bigAb_[:, cbase: cbase + 2048]
                lo = bigAb_[:, cbase + 2048: cbase + 4096]
                ldeps = list(fdeps)
                if wi == 1 and gi2 == 0:
                    ldeps += p.fence(["utmp", "frow", "fsq"])
                ldq(lambda s, wsrc=wsrc, e2=e2, g2=g2, buf=buf: s.dma_start(
                    out=buf, in_=wsrc[e2, 2 * g2: 2 * g2 + 2].transpose([1, 0, 2])),
                    writes=[("spin", wi)], deps=ldeps)
                S(lambda s, hi=hi, buf=buf: s.activation(hi, buf, AF.Identity),
                  reads=[("spin", wi)], writes=[(wr, s2)], deps=list(fdeps))
                V(lambda v, hi=hi, lo=lo, buf=buf: v.tensor_sub(lo, buf, hi),
                  reads=[("spin", wi), (wr, s2)], writes=[(wr, s2)])
                dmaa(lambda s, wdst=wdst, e2=e2, g2=g2, cbase=cbase: s.dma_start(
                    out=wdst[e2, g2], in_=bigAb_[:, cbase: cbase + 4096]),
                    reads=[(wr, s2)], writes=[(wtag, e2, g2)])
        UHb = lambda dcx: bigB[:, 0:2048].bitcast(BF16)[:, dcx * 512:(dcx + 1) * 512]
        ULb = lambda dcx: bigB[:, 2048:4096].bitcast(BF16)[:, dcx * 512:(dcx + 1) * 512]
        ACC = lambda dcx: bigB[:, 4096 + dcx * 512: 4096 + (dcx + 1) * 512]
        GGF = bigB[:, 8192:8704]
        GGH = lambda i: bigB[:, 8704 + i * 256: 8704 + (i + 1) * 256].bitcast(BF16)
        GGL = lambda i: bigB[:, 9728 + i * 256: 9728 + (i + 1) * 256].bitcast(BF16)
        GBC = bigB[:, 10752:11264]
        SQF = bigB[:, 11264:11776]
        UTMP = bigB[:, 11776:12288]
        frow = lambda i: bigB[0:1, 12288 + i * 512: 12288 + (i + 1) * 512]
        bigAb = bigA[:].bitcast(BF16)

        def W1t(s_, i, hilo, dcx):
            base = s_ * 8192 + hilo * 2048 + i * 1024
            return bigAb[:, base + dcx * 128: base + (dcx + 1) * 128]

        def W2t(s_, i, hilo, dcx):
            base = s_ * 8192 + 4096 + hilo * 2048 + i * 1024
            return bigAb[:, base + dcx * 128: base + (dcx + 1) * 128]

        groups = [(e, g) for e in range(8) for g in range(16)]
        for tq in range(4):
            t0 = tq * 512

            def usplit(dcx):
                V(lambda v, dcx=dcx: v.tensor_copy(UHb(dcx), UTMP),
                  reads=["utmp"], writes=[("uh", dcx)])
                V(lambda v, dcx=dcx: v.tensor_sub(ULb(dcx), UTMP, UHb(dcx)),
                  reads=["utmp", ("uh", dcx)], writes=[("ul", dcx)])

            emit_ln(512, lambda dcx, t0=t0: xc(dcx, t0, 512), lambda dcx: UTMP,
                    SQF, frow,
                    [("x", dcx) for dcx in range(NDC)],
                    ["utmp"] * NDC, "fsq", "frow", post_chunk=usplit)
            for dcx in range(NDC):
                TT(lambda t, dcx=dcx: t.matmul(psA[0:8, :], rws_bf[:, dcx * 8:(dcx + 1) * 8],
                                               UHb(dcx), start=(dcx == 0), stop=False),
                   reads=["rws", ("uh", dcx)], writes=["psA"])
                TT(lambda t, dcx=dcx: t.matmul(psA[0:8, :], rws_bf[:, dcx * 8:(dcx + 1) * 8],
                                               ULb(dcx), start=False, stop=False),
                   reads=["rws", ("ul", dcx)], writes=["psA"])
                TT(lambda t, dcx=dcx: t.matmul(psA[0:8, :], rws_bf[:, 64 + dcx * 8: 64 + (dcx + 1) * 8],
                                               UHb(dcx), start=False, stop=(dcx == NDC - 1)),
                   reads=["rws", ("uh", dcx)], writes=["psA"])
            V(lambda v: v.tensor_copy(rl_sb[:, :], psA[0:8, :]), reads=["psA"], writes=["rl"])
            for b in range(4):
                TM = gtm_sb[:, b * 16: b * 16 + 8]
                m1c = gtm_sb[:, b * 16 + 8: b * 16 + 9]
                n1c = gtm_sb[:, b * 16 + 9: b * 16 + 10]
                m2c = gtm_sb[:, b * 16 + 10: b * 16 + 11]
                n2c = gtm_sb[:, b * 16 + 11: b * 16 + 12]
                pc = gtm_sb[:, b * 16 + 12: b * 16 + 13]
                ac = gtm_sb[:, b * 16 + 13: b * 16 + 14]
                g2c = gtm_sb[:, b * 16 + 14: b * 16 + 15]
                C1 = bigB[:, 11264 + b * 32 + 0: 11264 + b * 32 + 8]
                C2 = bigB[:, 11264 + b * 32 + 8: 11264 + b * 32 + 16]
                TMX = bigB[:, 11264 + b * 32 + 16: 11264 + b * 32 + 24]
                rg = "g%d" % b
                TT(lambda t, b=b: t.transpose(psB[:, 0:8], rl_sb[:, b * 128:(b + 1) * 128],
                                              id128_sb[0:8, 0:8]),
                   reads=["rl", "id128"], writes=["psB"])
                V(lambda v, TM=TM: v.tensor_copy(TM, psB[:, 0:8]), reads=["psB"], writes=[rg])
                V(lambda v, TM=TM, m1c=m1c: v.reduce_max(m1c, TM, axis=AX.X), reads=[rg], writes=[rg])
                V(lambda v, TM=TM, m1c=m1c, C1=C1: v.tensor_scalar(C1, TM, m1c, None, ALU.is_equal),
                  reads=[rg, "fsq"], writes=[rg, "fsq"])
                V(lambda v, C1=C1, n1c=n1c: v.reduce_sum(n1c, C1, axis=AX.X), reads=[rg, "fsq"], writes=[rg])
                V(lambda v, C1=C1, TM=TM, TMX=TMX: v.scalar_tensor_tensor(TMX, C1, BIG_NEG, TM, ALU.mult, ALU.add),
                  reads=[rg, "fsq"], writes=[rg, "fsq"])
                V(lambda v, TMX=TMX, m2c=m2c: v.reduce_max(m2c, TMX, axis=AX.X), reads=[rg, "fsq"], writes=[rg])
                V(lambda v, TMX=TMX, m2c=m2c, C2=C2: v.tensor_scalar(C2, TMX, m2c, None, ALU.is_equal),
                  reads=[rg, "fsq"], writes=[rg, "fsq"])
                V(lambda v, C2=C2, n2c=n2c: v.reduce_sum(n2c, C2, axis=AX.X), reads=[rg, "fsq"], writes=[rg])
                V(lambda v, n1c=n1c, pc=pc: v.tensor_scalar(pc, n1c, -1.0, 1.0, ALU.add, ALU.min),
                  reads=[rg], writes=[rg])
                V(lambda v, m1c=m1c, m2c=m2c, n1c=n1c: v.tensor_sub(n1c, m1c, m2c),
                  reads=[rg], writes=[rg])
                V(lambda v, n1c=n1c, pc=pc: v.tensor_scalar(n1c, n1c, pc, None, ALU.mult),
                  reads=[rg], writes=[rg])
                V(lambda v, n1c=n1c, m2c=m2c: v.tensor_add(n1c, n1c, m2c),
                  reads=[rg], writes=[rg])
                V(lambda v, m1c=m1c, n1c=n1c: v.tensor_sub(n1c, m1c, n1c),
                  reads=[rg], writes=[rg])
                S(lambda s, n1c=n1c, ac=ac: s.activation(ac, n1c, AF.Sigmoid),
                  reads=[rg], writes=[rg + "a"])
                V(lambda v, ac=ac, m1c=m1c: v.tensor_scalar(m1c, ac, -1.0, 1.0, ALU.mult, ALU.add),
                  reads=[rg + "a"], writes=[rg])
                V(lambda v, pc=pc: v.tensor_scalar(pc, pc, -1.0, 1.0, ALU.mult, ALU.add),
                  reads=[rg], writes=[rg])
                V(lambda v, m1c=m1c, pc=pc: v.tensor_mul(m1c, m1c, pc),
                  reads=[rg], writes=[rg])
                V(lambda v, n2c=n2c, m2c=m2c: v.reciprocal(m2c, n2c),
                  reads=[rg], writes=[rg])
                V(lambda v, m1c=m1c, m2c=m2c, g2c=g2c: v.tensor_mul(g2c, m1c, m2c),
                  reads=[rg], writes=[rg])
                V(lambda v, C2=C2, g2c=g2c: v.tensor_scalar(C2, C2, g2c, None, ALU.mult),
                  reads=[rg, "fsq"], writes=[rg, "fsq"])
                V(lambda v, C1=C1, ac=ac, C2=C2, TM=TM: v.scalar_tensor_tensor(TM, C1, ac, C2, ALU.mult, ALU.add),
                  reads=[rg, rg + "a", "fsq"], writes=[rg])
                TT(lambda t, TM=TM: t.transpose(psB[0:8, 0:128], TM, id128_sb[:, :]),
                   reads=[rg, "id128"], writes=["psB"])
                V(lambda v, b=b: v.tensor_copy(gates_sb[:, b * 128:(b + 1) * 128], psB[0:8, 0:128]),
                  reads=["psB"], writes=["gates"])
            for dcx in range(NDC):
                TT(lambda t, dcx=dcx: t.matmul(psB[:, :], b2l_sb[:, dcx * 128:(dcx + 1) * 128],
                                               gates_sb[:, :], start=True, stop=True),
                   reads=["b2l", "gates"], writes=["psB"])
                V(lambda v, dcx=dcx: v.tensor_copy(ACC(dcx), psB[:, :]),
                  reads=["psB"], writes=[("acc", dcx)])
            pending_down = []
            if tq == 0 and do_split:
                emit_split_for(0, fdeps=fence)
                emit_split_for(1, fdeps=fence)
            for gi, (e, g) in enumerate(groups):
                slot = gi % 4
                fdeps = list(fence) if (tq == 0 and gi < 4 and not do_split) else []
                if not (tq == 0 and do_split):
                    dmas(lambda s, e=e, g=g, slot=slot: s.dma_start(
                        out=bigAb[:, slot * 8192: slot * 8192 + 4096],
                        in_=w1s_h[e, g]),
                        writes=[("wst1", slot)], deps=fdeps,
                        reads=[("w1s", e, g)])
                    dmas(lambda s, e=e, g=g, slot=slot: s.dma_start(
                        out=bigAb[:, slot * 8192 + 4096: slot * 8192 + 8192],
                        in_=w2s_h[e, g]),
                        writes=[("wst2", slot)],
                        deps=fdeps,
                        reads=[("w2s", e, g)])
                if g == 0:
                    TT(lambda t, e=e: t.matmul(psB[:, :], sel_sb[:, e * 128:(e + 1) * 128],
                                               gates_sb[:, :], start=True, stop=True),
                       reads=["sel", "gates"], writes=["psB"])
                    V(lambda v: v.tensor_copy(GBC, psB[:, :]), reads=["psB"], writes=["gbc"])
                for i in range(2):
                    fc = 2 * g + i
                    gsl = 2 * (gi % 2) + i
                    pu = psU0 if (i % 2 == 0) else psU1
                    pur = "psU%d" % (i % 2)
                    for dcx in range(NDC):
                        TT(lambda t, i=i, dcx=dcx, slot=slot, pu=pu: t.matmul(
                            pu[:, :], W1t(slot, i, 0, dcx), UHb(dcx),
                            start=(dcx == 0), stop=False),
                           reads=[("wst1", slot), ("uh", dcx)], writes=[pur])
                        TT(lambda t, i=i, dcx=dcx, slot=slot, pu=pu: t.matmul(
                            pu[:, :], W1t(slot, i, 0, dcx), ULb(dcx),
                            start=False, stop=False),
                           reads=[("wst1", slot), ("ul", dcx)], writes=[pur])
                        TT(lambda t, i=i, dcx=dcx, slot=slot, pu=pu: t.matmul(
                            pu[:, :], W1t(slot, i, 1, dcx), UHb(dcx),
                            start=False, stop=(dcx == NDC - 1)),
                           reads=[("wst1", slot), ("uh", dcx)], writes=[pur])
                    S(lambda s, gsl=gsl, e=e, fc=fc, pu=pu: s.activation(
                        GGF, pu[:, :], AF.Gelu,
                        bias=b1_sb[:, e * 32 + fc: e * 32 + fc + 1], scale=1.0),
                      reads=[pur, "b1"], writes=["ggf"])
                    V(lambda v: v.tensor_mul(GGF, GGF, GBC),
                      reads=["ggf", "gbc"], writes=["ggf"])
                    S(lambda s, gsl=gsl: s.activation(GGH(gsl), GGF, AF.Identity),
                      reads=["ggf"], writes=[("ggh", gsl)])
                    V(lambda v, gsl=gsl: v.tensor_sub(GGL(gsl), GGF, GGH(gsl)),
                      reads=["ggf", ("ggh", gsl)], writes=[("ggl", gsl)])
                def emit_down(slot=slot, gp=gi % 2):
                    for dcx in range(NDC):
                        pd = psD0 if (dcx % 2 == 0) else psD1
                        pdr = "psD%d" % (dcx % 2)
                        for i in range(2):
                            gsl2 = 2 * gp + i
                            TT(lambda t, i=i, gsl2=gsl2, dcx=dcx, slot=slot, pd=pd: t.matmul(
                                pd[:, :], W2t(slot, i, 0, dcx), GGH(gsl2),
                                start=(i == 0), stop=False),
                               reads=[("wst2", slot), ("ggh", gsl2)], writes=[pdr])
                            TT(lambda t, i=i, gsl2=gsl2, dcx=dcx, slot=slot, pd=pd: t.matmul(
                                pd[:, :], W2t(slot, i, 0, dcx), GGL(gsl2),
                                start=False, stop=False),
                               reads=[("wst2", slot), ("ggl", gsl2)], writes=[pdr])
                            TT(lambda t, i=i, gsl2=gsl2, dcx=dcx, slot=slot, pd=pd: t.matmul(
                                pd[:, :], W2t(slot, i, 1, dcx), GGH(gsl2),
                                start=False, stop=(i == 1)),
                               reads=[("wst2", slot), ("ggh", gsl2)], writes=[pdr])
                        V(lambda v, dcx=dcx, pd=pd: v.tensor_add(ACC(dcx), ACC(dcx), pd[:, :]),
                          reads=[pdr, ("acc", dcx)], writes=[("acc", dcx)])
                pending_down.append(emit_down)
                if len(pending_down) > 1:
                    pending_down.pop(0)()
                if tq == 0 and do_split:
                    emit_split_for(gi + 2, fdeps=fence if gi < 2 else ())
            while pending_down:
                pending_down.pop(0)()
            for dcx in range(NDC):
                V(lambda v, dcx=dcx, t0=t0: v.tensor_add(xc(dcx, t0, 512), xc(dcx, t0, 512), ACC(dcx)),
                  reads=[("acc", dcx), ("x", dcx)], writes=[("x", dcx)])

    if "P3" in phases:
        emit_ffn(0, do_split=True, fence=p.fence(["ct", "st", "t2", "t3", "ctm", "stm", "rowk", "rowi", "strip0",
                "lnrow", "xh", "dsc", "dly", "dlx", "d5", "dsxy", "dsx", "dsy",
                "dsxx", "dnum", "dden", "dt", "dq", "dD", "dom", "da", "db",
                "dcnt", "fx1"] + [("x1", i) for i in range(NDC)]))

    if "P4" in phases:
        CHB = bigB[:, 0:2048]
        HT2 = lambda dcx: bigB[:, 2048 + dcx * 256: 2048 + (dcx + 1) * 256]
        RQ = bigB[:, 8192:9216]
        T1Q = bigB[:, 9216:10240]
        SQ4 = bigB[:, 10240:10752]
        VT4 = bigB[:, 6144:6656]
        hrow1 = lambda i: bigB[0:1, 11264 + i * 256: 11264 + (i + 1) * 256]
        hrow2 = lambda i: bigB[0:1, 12288 + i * 256: 12288 + (i + 1) * 256]
        ONES2K = bigB[:, 4096:6144]
        V(lambda v: v.memset(ONES2K, 1.0), writes=["ones2k"])
        NT256 = TC // 256

        def emit_h_tile(it):
            t0 = it * 256
            emit_ln(256, lambda dcx, t0=t0: xc(dcx, t0, 256), HT2, SQ4, hrow1,
                    [("x", dcx) for dcx in range(NDC)],
                    [("h", dcx) for dcx in range(NDC)], "p4sq", "hrow1")
            emit_ln(256, HT2, HT2, SQ4, hrow2,
                    [("h", dcx) for dcx in range(NDC)],
                    [("h", dcx) for dcx in range(NDC)], "p4sq", "hrow2")
            for dcx in range(NDC):
                V(lambda v, dcx=dcx: v.tensor_scalar(
                    HT2(dcx), HT2(dcx), battn_sb[:, 24 + dcx:25 + dcx],
                    battn_sb[:, 32 + dcx:33 + dcx], ALU.mult, ALU.add),
                  reads=[("h", dcx), "battn"], writes=[("h", dcx)])

        def make_stream(src_h, tag, pos):
            state = {"k": 0}

            def prefetch():
                k = state["k"]
                if k < len(pos):
                    _, dco2 = pos[k]
                    slot = k % 2
                    dmas(lambda s, dco2=dco2, slot=slot: s.dma_start(
                        out=wk_sb[:, slot * 1024:(slot + 1) * 1024], in_=src_h[dco2]),
                        writes=[(tag, slot)])
                    state["k"] += 1
            return prefetch

        pos = [(it, dco) for it in range(NT256) for dco in range(NDC)]
        pf1 = make_stream(wkb_h, "wkS", pos)
        pf1(); pf1()
        for it in range(NT256):
            t0 = it * 256
            emit_h_tile(it)
            if dbg and it == 6 and not dram.get("dbgH"):
                dram["dbgH"] = nc.dram_tensor("dbgH", [128, 2048], F32, kind="ExternalOutput").ap()
                dmas(lambda s: s.dma_start(out=dram["dbgH"], in_=bigB[:, 2048:4096]),
                     reads=[("h", i) for i in range(NDC)], writes=["dbgH"])
            for dco in range(NDC):
                slot = (it * NDC + dco) % 2
                pu = psU0 if dco % 2 == 0 else psU1
                pur = "psU%d" % (dco % 2)
                for dcx in range(NDC):
                    TT(lambda t, dcx=dcx, slot=slot, pu=pu: t.matmul(
                        pu[:, 0:256], wk_sb[:, slot * 1024 + dcx * 128: slot * 1024 + (dcx + 1) * 128],
                        HT2(dcx), start=(dcx == 0), stop=(dcx == NDC - 1)),
                       reads=[("wkS", slot), ("h", dcx)], writes=[pur])
                S(lambda s, dco=dco, t0=t0, pu=pu: s.activation(
                    bigA[:, dco * TC + t0: dco * TC + t0 + 256], pu[:, 0:256],
                    AF.Identity, bias=battn_sb[:, dco:dco + 1], scale=1.0),
                  reads=[pur, "battn"], writes=[("keys", dco)])
                pf1()
        if dbg:
            dbgK_h = nc.dram_tensor("dbgK", [128, 16384], F32, kind="ExternalOutput").ap()
            dmas(lambda s: s.dma_start(out=dbgK_h, in_=bigA[:]),
                 reads=[("keys", i) for i in range(NDC)], writes=["dbgK"])
        pf2 = make_stream(wkab_h, "wkS", pos)
        pf2(); pf2()
        for it in range(NT256):
            t0 = it * 256
            for dco in range(NDC):
                slot = (it * NDC + dco) % 2
                pd = psD0 if dco % 2 == 0 else psD1
                pdr = "psD%d" % (dco % 2)
                for dcx in range(NDC):
                    TT(lambda t, dcx=dcx, t0=t0, slot=slot, pd=pd: t.matmul(
                        pd[:, 0:256], wk_sb[:, slot * 1024 + dcx * 128: slot * 1024 + (dcx + 1) * 128],
                        bigA[:, dcx * TC + t0: dcx * TC + t0 + 256],
                        start=(dcx == 0), stop=(dcx == NDC - 1)),
                       reads=[("wkS", slot), ("keys", dcx)], writes=[pdr])
                S(lambda s, dco=dco, pd=pd: s.activation(
                    CHB[:, dco * 256:(dco + 1) * 256], pd[:, 0:256], AF.Exp,
                    bias=battn_sb[:, 40 + dco:41 + dco], scale=float(1.0 / 32.0)),
                  reads=[pdr, "bka32"], writes=["chb"])
                pf2()
            dmas(lambda s, it=it: s.dma_start(
                out=ew_h[:, :, it * 256:(it + 1) * 256].transpose([1, 0, 2]),
                in_=CHB.rearrange("p (c t) -> p c t", c=NDC)),
                reads=["chb"], writes=["ewh%d" % it])
        for dcx in range(NDC):
            dmas(lambda s, dcx=dcx: s.dma_start(out=CHB, in_=ew_h[dcx]),
                 reads=["ewh%d" % it2 for it2 in range(NT256)], writes=["chb"])
            V(lambda v: v.tensor_tensor_scan(CHB, ONES2K, CHB, 0.0, ALU.mult, ALU.add),
              reads=["chb", "ones2k"], writes=["chb"])
            V(lambda v, dcx=dcx: v.tensor_copy(tot_sb[:, dcx:dcx + 1], CHB[:, 2047:2048]),
              reads=["chb"], writes=["tot1"])
            dmas(lambda s, dcx=dcx: s.dma_start(out=ew_h[dcx], in_=CHB),
                 reads=["chb"], writes=["csh%d" % dcx])
        dmag(lambda g: g.dma_start(out=cc2i_h, in_=tot_sb[:, 0:8]),
             reads=["tot1"], writes=["cc2i"])
        p.step("gpsimd", lambda g: g.collective_compute(
            "AllGather", ALU.bypass, replica_groups=[list(range(NCORES))],
            ins=[cc2i_h.opt()], outs=[cc2o_h.opt()]),
            reads=["cc2i"], writes=["cc2o"], cc=True)
        dmag(lambda g: g.dma_start(out=gath_sb[:].rearrange("p (r c) -> p r c", r=8),
                                   in_=cc2o_h.transpose([1, 0, 2])),
             reads=["cc2o"], writes=["gath"])
        V(lambda v: v.memset(tot_sb[:, 8:16], 0.0), writes=["carry1"])
        for r8 in range(8):
            V(lambda v, r8=r8: v.scalar_tensor_tensor(
                tot_sb[:, 32:40], gath_sb[:, r8 * 8:(r8 + 1) * 8], mexc_sb[:, r8:r8 + 1],
                tot_sb[:, 8:16], ALU.mult, ALU.add),
              reads=["gath", "mexc", "carry1"], writes=["ctmp"])
            V(lambda v: v.tensor_copy(tot_sb[:, 8:16], tot_sb[:, 32:40]),
              reads=["ctmp"], writes=["carry1"])
        for dcx in range(NDC):
            dmas(lambda s, dcx=dcx: s.dma_start(out=CHB, in_=ew_h[dcx]),
                 reads=["csh%d" % dcx, "chb"], writes=["chb"])
            V(lambda v, dcx=dcx: v.tensor_scalar(CHB, CHB, tot_sb[:, 8 + dcx:9 + dcx], None, ALU.add),
              reads=["chb", "carry1"], writes=["chb"])
            for q in range(2):
                q0 = q * 1024
                V(lambda v, q0=q0: v.reciprocal(RQ, CHB[:, q0:q0 + 1024]),
                  reads=["chb"], writes=["rq"])
                if q == 0:
                    V(lambda v, dcx=dcx: v.tensor_scalar(
                        T1Q[:, 0:1], RQ[:, 0:1], tot_sb[:, 8 + dcx:9 + dcx], None, ALU.mult),
                      reads=["rq", "carry1"], writes=["t1q"])
                    V(lambda v: v.tensor_mul(T1Q[:, 1:1024], CHB[:, 0:1023], RQ[:, 1:1024]),
                      reads=["chb", "rq"], writes=["t1q"])
                else:
                    V(lambda v, q0=q0: v.tensor_mul(T1Q, CHB[:, q0 - 1:q0 + 1023], RQ),
                      reads=["chb", "rq"], writes=["t1q"])
                V(lambda v: v.tensor_scalar(T1Q, T1Q, -1.0, 1.0, ALU.mult, ALU.add),
                  reads=["t1q"], writes=["t1q"])
                V(lambda v, dcx=dcx, q0=q0: v.tensor_mul(
                    bigA[:, dcx * TC + q0: dcx * TC + q0 + 1024],
                    bigA[:, dcx * TC + q0: dcx * TC + q0 + 1024], T1Q),
                  reads=["t1q", ("keys", dcx)], writes=[("keys", dcx)])
        for dcx in range(NDC):
            V(lambda v, dcx=dcx: v.tensor_tensor_scan(
                bigA[:, dcx * TC: dcx * TC + TC], ONES2K, bigA[:, dcx * TC: dcx * TC + TC],
                0.0, ALU.mult, ALU.add),
              reads=[("keys", dcx), "ones2k"], writes=[("keys", dcx)])
            V(lambda v, dcx=dcx: v.tensor_copy(tot_sb[:, 16 + dcx:17 + dcx],
                                               bigA[:, dcx * TC + TC - 1: dcx * TC + TC]),
              reads=[("keys", dcx)], writes=["tot2"])
        dmag(lambda g: g.dma_start(out=cc3i_h, in_=tot_sb[:, 16:24]),
             reads=["tot2"], writes=["cc3i"])
        p.step("gpsimd", lambda g: g.collective_compute(
            "AllGather", ALU.bypass, replica_groups=[list(range(NCORES))],
            ins=[cc3i_h.opt()], outs=[cc3o_h.opt()]),
            reads=["cc3i"], writes=["cc3o"], cc=True)
        dmag(lambda g: g.dma_start(out=gath_sb[:].rearrange("p (r c) -> p r c", r=8),
                                   in_=cc3o_h.transpose([1, 0, 2])),
             reads=["cc3o"], writes=["gath"])
        V(lambda v: v.memset(tot_sb[:, 24:32], 0.0), writes=["carry2"])
        for r8 in range(8):
            V(lambda v, r8=r8: v.scalar_tensor_tensor(
                tot_sb[:, 32:40], gath_sb[:, r8 * 8:(r8 + 1) * 8], mexc_sb[:, r8:r8 + 1],
                tot_sb[:, 24:32], ALU.mult, ALU.add),
              reads=["gath", "mexc", "carry2"], writes=["ctmp"])
            V(lambda v: v.tensor_copy(tot_sb[:, 24:32], tot_sb[:, 32:40]),
              reads=["ctmp"], writes=["carry2"])
        for dcx in range(NDC):
            V(lambda v, dcx=dcx: v.tensor_scalar(
                bigA[:, dcx * TC: dcx * TC + TC], bigA[:, dcx * TC: dcx * TC + TC],
                tot_sb[:, 24 + dcx:25 + dcx], None, ALU.add),
              reads=[("keys", dcx), "carry2"], writes=[("keys", dcx)])
        pf3 = make_stream(wvb_h, "wkS", pos)
        pf3(); pf3()
        for it in range(NT256):
            t0 = it * 256
            emit_h_tile(it)
            if dbg and it == 6 and not dram.get("dbgH"):
                dram["dbgH"] = nc.dram_tensor("dbgH", [128, 2048], F32, kind="ExternalOutput").ap()
                dmas(lambda s: s.dma_start(out=dram["dbgH"], in_=bigB[:, 2048:4096]),
                     reads=[("h", i) for i in range(NDC)], writes=["dbgH"])
            for dco in range(NDC):
                slot = (it * NDC + dco) % 2
                pu = psU0 if dco % 2 == 0 else psU1
                pur = "psU%d" % (dco % 2)
                for dcx in range(NDC):
                    TT(lambda t, dcx=dcx, slot=slot, pu=pu: t.matmul(
                        pu[:, 0:256], wk_sb[:, slot * 1024 + dcx * 128: slot * 1024 + (dcx + 1) * 128],
                        HT2(dcx), start=(dcx == 0), stop=(dcx == NDC - 1)),
                       reads=[("wkS", slot), ("h", dcx)], writes=[pur])
                S(lambda s, dco=dco, pu=pu: s.activation(
                    VT4[:, 0:256], pu[:, 0:256], AF.Identity,
                    bias=battn_sb[:, 16 + dco:17 + dco], scale=1.0),
                  reads=[pur, "battn"], writes=["vtmp"])
                V(lambda v, dco=dco, t0=t0: v.tensor_mul(
                    VT4[:, 0:256], VT4[:, 0:256],
                    bigA[:, dco * TC + t0: dco * TC + t0 + 256]),
                  reads=["vtmp", ("keys", dco)], writes=["vtmp"])
                V(lambda v, dco=dco, t0=t0: v.tensor_add(
                    xc(dco, t0, 256), xc(dco, t0, 256), VT4[:, 0:256]),
                  reads=["vtmp", ("x", dco)], writes=[("x", dco)])
                pf3()

    if "P5" in phases:
        emit_ffn(1, do_split=("P3" not in phases), fence=p.fence(["ct", "st", "t2", "t3", "ctm", "stm", "rowk", "rowi", "strip0",
                "lnrow", "xh", "dsc", "dly", "dlx", "d5", "dsxy", "dsx", "dsy",
                "dsxx", "dnum", "dden", "dt", "dq", "dD", "dom", "da", "db",
                "dcnt", "fx1"] + [("x1", i) for i in range(NDC)]
            + [("keys", i) for i in range(NDC)]))

    dmas(lambda s: s.dma_start(
        out=yT_h.rearrange("(c p) t -> c p t", p=128).transpose([1, 0, 2]),
        in_=x_sb[:].rearrange("p (c t) -> p c t", c=NDC)),
        reads=[("x", dcx) for dcx in range(NDC)], writes=["yout"])
    if dbg:
        dbgA_h = nc.dram_tensor("dbgA", [128, 16384], F32, kind="ExternalOutput").ap()
        dbgB_h = nc.dram_tensor("dbgB", [128, 13312], F32, kind="ExternalOutput").ap()
        dbgr_h = nc.dram_tensor("dbgr", [1, 64], F32, kind="ExternalOutput").ap()
        dbgc_h = nc.dram_tensor("dbgc", [128, 48], F32, kind="ExternalOutput").ap()
        allreg = list(p.region_w.keys())
        dmas(lambda s: s.dma_start(out=dbgA_h, in_=bigA[:]), reads=allreg, writes=["dbg1"])
        dmas(lambda s: s.dma_start(out=dbgB_h, in_=bigB[:]), reads=[], writes=["dbg2"])
        dmas(lambda s: s.dma_start(out=dbgr_h, in_=g1_sb[:]), reads=[], writes=["dbg3"])
        dmas(lambda s: s.dma_start(out=dbgc_h, in_=bias_sb[:]), reads=[], writes=["dbg4"])
        dbgt_h = nc.dram_tensor("dbgt", [128, 40], F32, kind="ExternalOutput").ap()
        dmas(lambda s: s.dma_start(out=dbgt_h, in_=tot_sb[:]), reads=[], writes=["dbg5"])

    p.emit(nc, block, sems)
    es.close()
    return nc


def _prep_inputs(inputs):
    f32 = np.float32
    x = np.asarray(inputs["L0_attn_a_x"], f32).reshape(T, D)
    Wel = np.asarray(inputs["L0_attn_a_W_el"], f32)
    bel = np.asarray(inputs["L0_attn_a_b_el"], f32).reshape(24, 1)
    Wle = np.asarray(inputs["L0_attn_a_W_le"], f32)
    ble = np.ascontiguousarray(np.asarray(inputs["L0_attn_a_b_le"], f32).reshape(8, 128).T)
    A = np.array([[1, 1, 1, 0, 1, 1, 1, 0, 0, 0, 1, 1], [1, 1, 0, 1, 1, 1, 0, 0, 0, 1, 1, 0],
                  [1, 0, 1, 1, 1, 0, 0, 0, 1, 1, 0, 1], [0, 1, 1, 1, 0, 0, 0, 1, 1, 0, 1, 1],
                  [0, 0, 0, 0, 1, 1, 1, 1, 1, 1, 1, 1], [1, 1, 1, 1, 0, 0, 0, 0, 0, 0, 0, 0],
                  [0, 0, 0, 0, 1, 1, 1, 1, 0, 0, 0, 0], [1, 1, 0, 0, 1, 1, 0, 0, 1, 1, 0, 0],
                  [1, 0, 1, 0, 1, 0, 1, 0, 1, 0, 1, 0], [0, 1, 0, 1, 0, 1, 0, 1, 0, 1, 0, 1],
                  [1, 0, 0, 1, 1, 0, 0, 1, 1, 0, 0, 1], [0, 1, 1, 0, 0, 1, 1, 0, 0, 1, 1, 0]],
                 dtype=f32)
    H = np.concatenate([A.T, np.eye(12, dtype=f32)], axis=1)
    HT = np.ascontiguousarray(H.T)

    def colblocks(W):
        W4 = W.reshape(8, 128, 8, 128)
        return np.ascontiguousarray(W4.transpose(0, 3, 2, 1))

    wkb = colblocks(np.asarray(inputs["L0_attn_b_Wk"], f32)).reshape(8, 128, 1024)
    wkab = colblocks(np.asarray(inputs["L0_attn_b_Wka"], f32)).reshape(8, 128, 1024)
    wvb = colblocks(np.asarray(inputs["L0_attn_b_Wv"], f32)).reshape(8, 128, 1024)

    def ppcols(v):
        return np.ascontiguousarray(np.asarray(v, f32).reshape(8, 128).T)

    battn = np.concatenate([ppcols(inputs["L0_attn_b_bk"]), ppcols(inputs["L0_attn_b_bka"]),
                            ppcols(inputs["L0_attn_b_bv"]), ppcols(inputs["L0_attn_b_gamma"]),
                            ppcols(inputs["L0_attn_b_beta"])], axis=1)
    wel_blk = np.ascontiguousarray(Wel.reshape(8, 128, 24))
    rwb = np.ascontiguousarray(np.asarray(inputs["L0_ffn_router_w"], f32).T.reshape(8, 128, 8))
    b1 = np.asarray(inputs["L0_ffn_b1"], f32)
    b1v = np.ascontiguousarray(b1.reshape(8, 32, 128).transpose(2, 0, 1).reshape(128, 256))
    b2l = np.ascontiguousarray(np.asarray(inputs["L0_ffn_b2"], f32))
    w1 = np.asarray(inputs["L0_ffn_w1"], f32)
    w2 = np.asarray(inputs["L0_ffn_w2"], f32)
    w1b = np.ascontiguousarray(
        w1.reshape(8, 32, 128, 8, 128).transpose(0, 1, 4, 3, 2).reshape(8, 32, 128, 1024))
    w2b = np.ascontiguousarray(
        w2.reshape(8, 8, 128, 32, 128).transpose(0, 3, 4, 1, 2).reshape(8, 32, 128, 1024))
    sel = np.zeros((8, 8 * 128), f32)
    for e in range(8):
        sel[e, e * 128:(e + 1) * 128] = 1.0
    id128 = np.eye(128, dtype=f32)
    lxc = np.array([[2, 4, 8, 16, 32, 1, 1, 1]], dtype=f32)
    xhead = np.ascontiguousarray(x[0:1, :])

    shared = dict(wel=wel_blk, bel=bel, wle=Wle, ble=ble, htm=HT,
                  wkb=wkb, wkab=wkab, wvb=wvb, battn=battn, rwb=rwb,
                  b1v=b1v, b2l=b2l, sel=sel, id128=id128,
                  w1b=w1b, w2b=w2b, lxc=lxc, xhead=xhead)
    in_maps = []
    for j in range(NCORES):
        sh = x[j * TC:(j + 1) * TC, :]
        m = dict(shared)
        m["xT"] = np.ascontiguousarray(sh.T)
        m["mexc"] = np.broadcast_to(
            (np.arange(8) < j).astype(f32), (128, 8)).copy()
        in_maps.append(m)
    return in_maps


def kernel(**inputs):
    phases = inputs.pop("_phases", ALL_PHASES)
    dbg = inputs.pop("_dbg", False)
    in_maps = _prep_inputs(inputs)
    nc = build_nc(phases=phases, dbg=dbg)
    res = run_bass_kernel_spmd(nc, in_maps, core_ids=list(range(NCORES)))
    out = np.empty((T, D), np.float32)
    for j in range(NCORES):
        out[j * TC:(j + 1) * TC, :] = res.results[j]["yT"].T
    out = out.reshape(1, T, D)
    if dbg:
        return out, res
    return out


# version 08
# speedup vs baseline: 1.1595x; 1.0483x over previous
import numpy as np
import concourse.bass as bass
import concourse.mybir as mybir
from concourse.bass_utils import run_bass_kernel_spmd

F32 = mybir.dt.float32
BF16 = mybir.dt.bfloat16
I32 = mybir.dt.int32
AF = mybir.ActivationFunctionType
ALU = mybir.AluOpType
AX = mybir.AxisListType

NCORES = 8
T = 16384
TC = T // NCORES
D = 1024
NDC = 8
PI = float(np.pi)
TWO_PI = float(2 * np.pi)
HALF_PI = float(np.pi / 2)
LN_EPS = 1e-5
BIG_NEG = -1.0e30

ALL_PHASES = ("P1", "P2", "P3", "P4", "P5")
P4_STOP = 99


class Prog:
    QUEUES = ("qs", "qg", "qa")

    def __init__(self):
        self.steps = []
        self.region_w = {}
        self.region_r = {}

    def step(self, eng, fn, reads=(), writes=(), dma=None, cc=False, deps=()):
        dep = set(d for d in deps if d is not None)
        for rg in reads:
            w = self.region_w.get(rg)
            if w is not None:
                dep.add(w)
        for rg in writes:
            w = self.region_w.get(rg)
            if w is not None:
                dep.add(w)
            dep.update(self.region_r.get(rg, ()))
        sid = len(self.steps)
        counter = "cc" if cc else (dma if dma is not None else eng)
        self.steps.append([eng, fn, counter, sorted(dep)])
        for rg in reads:
            self.region_r.setdefault(rg, set()).add(sid)
        for rg in writes:
            self.region_w[rg] = sid
            self.region_r[rg] = set()
        return sid

    def fence(self, regions):
        out = set()
        for rg in regions:
            w = self.region_w.get(rg)
            if w is not None:
                out.add(w)
            out.update(self.region_r.get(rg, ()))
        return sorted(out)

    def emit(self, nc, block, sems):
        steps = self.steps
        needed = set()
        for st in steps:
            needed.update(st[3])
        ordinal = [None] * len(steps)
        counts = {}
        for sid, st in enumerate(steps):
            c = st[2]
            if sid in needed or c == "cc" or c in self.QUEUES:
                ordinal[sid] = counts.get(c, 0)
                counts[c] = ordinal[sid] + 1
        totals = {c: k * (16 if c in self.QUEUES else 1) for c, k in counts.items()}

        def wait_of(sid):
            c = steps[sid][2]
            v = ordinal[sid] + 1
            return c, (v * 16 if c in self.QUEUES else v)

        def emit_engine(eng_name, eng_obj):
            last = {}
            for sid, st in enumerate(steps):
                issuer, fn, counter, deps = st
                if issuer != eng_name:
                    continue
                for d in deps:
                    dc_, dv = wait_of(d)
                    if (eng_name == "tensor" and dc_ == "tensor"
                            and steps[d][2] == "tensor"):
                        continue
                    if last.get(dc_, 0) < dv:
                        eng_obj.wait_ge(sems[dc_], dv)
                        last[dc_] = dv
                inst = fn(eng_obj)
                if ordinal[sid] is not None:
                    inst.then_inc(sems[counter], 16 if counter in self.QUEUES else 1)

        @block.sync
        def _(sync):
            emit_engine("sync", sync)
            for c in sorted(totals):
                if c != "sync" and totals[c]:
                    sync.wait_ge(sems[c], totals[c])

        @block.scalar
        def _(scalar):
            emit_engine("scalar", scalar)

        @block.vector
        def _(vector):
            emit_engine("vector", vector)

        @block.tensor
        def _(tensor):
            emit_engine("tensor", tensor)

        @block.gpsimd
        def _(gpsimd):
            emit_engine("gpsimd", gpsimd)


def build_nc(phases=ALL_PHASES, dbg=False):
    import contextlib
    nc = bass.Bass("TRN2", target_bir_lowering=False)
    dram = {}

    def din(name, shape, dtype=F32):
        dram[name] = nc.dram_tensor(name, shape, dtype, kind="ExternalInput").ap()
        return dram[name]

    xT_h = din("xT", [D, TC])
    xhead_h = din("xhead", [1, D])
    wel_h = din("wel", [NDC, 128, 24])
    bel_h = din("bel", [24, 1])
    wle_h = din("wle", [24, D])
    ble_h = din("ble", [128, 8])
    ht_h = din("htm", [24, 12])
    wkb_h = din("wkb", [NDC, 128, D])
    wkab_h = din("wkab", [NDC, 128, D])
    wvb_h = din("wvb", [NDC, 128, D])
    battn_h = din("battn", [128, 40])
    rwb_h = din("rwb", [NDC, 128, 8])
    b1_h = din("b1v", [128, 8 * 32])
    b2l_h = din("b2l", [8, D])
    sel_h = din("sel", [8, 8 * 128])
    id128_h = din("id128", [128, 128])
    w1_h = din("w1b", [8, 32, 128, D])
    w2_h = din("w2b", [8, 32, 128, D])
    mexc_h = din("mexc", [128, 8])
    lx_h = din("lxc", [1, 8])

    yT_h = nc.dram_tensor("yT", [D, TC], F32, kind="ExternalOutput").ap()

    w1s_h = nc.dram_tensor("w1s", [8, 16, 128, 4096], BF16).ap()
    w2s_h = nc.dram_tensor("w2s", [8, 16, 128, 4096], BF16).ap()
    ew_h = nc.dram_tensor("ew_spill", [NDC, 128, TC], F32,
                          kind=("ExternalOutput" if dbg else "Internal")).ap()
    cc1i_h = nc.dram_tensor("cc1i", [1, 8], F32).ap()
    cc1o_h = nc.dram_tensor("cc1o", [8, 8], F32).ap()
    cc2i_h = nc.dram_tensor("cc2i", [128, 8], F32).ap()
    cc2o_h = nc.dram_tensor("cc2o", [8, 128, 8], F32).ap()
    cc3i_h = nc.dram_tensor("cc3i", [128, 8], F32).ap()
    cc3o_h = nc.dram_tensor("cc3o", [8, 128, 8], F32).ap()

    es = contextlib.ExitStack()

    def sb(name, shape, dtype=F32):
        return es.enter_context(nc.sbuf_tensor(name, shape, dtype))

    x_sb = sb("x_sb", [128, NDC * TC])
    bigA = sb("bigA", [128, 16384])
    bigB = sb("bigB", [128, 13824])
    wk_sb = sb("wk_sb", [128, 2048])
    wel_sb = sb("wel_sb", [128, NDC * 24])
    ht_sb = sb("ht_sb", [24, 12])
    b1_sb = sb("b1_sb", [128, 8 * 32])
    b2l_sb = sb("b2l_sb", [8, D])
    sel_sb = sb("sel_sb", [8, 8 * 128])
    id128_sb = sb("id128_sb", [128, 128])
    rl_sb = sb("rl_sb", [8, 512])
    gates_sb = sb("gates_sb", [8, 512])
    gtm_sb = sb("gtm_sb", [128, 64])
    battn_sb = sb("battn_sb", [128, 48])
    bias_sb = sb("bias_sb", [128, 48])
    rwb_sb = sb("rwb_sb", [128, NDC * 8])
    rws_sb = sb("rws_sb", [128, 64])
    mexc_sb = sb("mexc_sb", [128, 8])
    tot_sb = sb("tot_sb", [128, 40])
    mm_sb = sb("mm_sb", [128, 8])
    ones128 = sb("ones128", [128, 1])
    onesr = sb("onesr", [1, 128])
    ones12 = sb("ones12", [12, 1])
    kid_i = sb("kid_i", [128, 8], I32)
    gath_sb = sb("gath_sb", [128, 64])
    g1_sb = sb("g1_sb", [1, 64])

    def psmk(name, shape):
        return es.enter_context(nc.psum_tensor(name, shape, F32))

    psA = psmk("psA", [128, 512])
    psB = psmk("psB", [128, 512])
    psU0 = psmk("psU0", [128, 512])
    psU1 = psmk("psU1", [128, 512])
    psD0 = psmk("psD0", [128, 512])
    psD1 = psmk("psD1", [128, 512])
    psC = psmk("psC", [128, 1024])

    sems = {}
    for cname in ("sync", "scalar", "vector", "tensor", "gpsimd", "qs", "qg", "qa", "cc"):
        sems[cname] = es.enter_context(nc.semaphore("sem_" + cname))

    block = es.enter_context(nc.Block())
    p = Prog()

    def xc(dc, t0, tn):
        return x_sb[:, dc * TC + t0: dc * TC + t0 + tn]

    def dmas(fn, reads=(), writes=(), deps=()):
        return p.step("sync", fn, reads=reads, writes=writes, dma="qs", deps=deps)

    def dmag(fn, reads=(), writes=(), deps=()):
        return p.step("gpsimd", fn, reads=reads, writes=writes, dma="qg", deps=deps)

    def dmaa(fn, reads=(), writes=(), deps=()):
        return p.step("scalar", fn, reads=reads, writes=writes, dma="qa", deps=deps)

    def V(fn, reads=(), writes=(), deps=()):
        return p.step("vector", fn, reads=reads, writes=writes, deps=deps)

    def S(fn, reads=(), writes=(), deps=()):
        return p.step("scalar", fn, reads=reads, writes=writes, deps=deps)

    def TT(fn, reads=(), writes=(), deps=()):
        return p.step("tensor", fn, reads=reads, writes=writes, deps=deps)

    def G(fn, reads=(), writes=(), deps=()):
        return p.step("gpsimd", fn, reads=reads, writes=writes, deps=deps)

    dmas(lambda s: s.dma_start(
        out=x_sb[:].rearrange("p (c t) -> p c t", c=NDC),
        in_=xT_h.rearrange("(c p) t -> c p t", p=128).transpose([1, 0, 2])),
        writes=[("x", dc) for dc in range(NDC)])
    dmas(lambda s: s.dma_start(
        out=wel_sb[:].rearrange("p (c m) -> p c m", c=NDC),
        in_=wel_h.transpose([1, 0, 2])), writes=["wel"])
    dmas(lambda s: s.dma_start(out=bigB[0:24, 8192:9216], in_=wle_h), writes=["wle"])
    dmas(lambda s: s.dma_start(out=ht_sb[:], in_=ht_h), writes=["htm"])
    dmas(lambda s: s.dma_start(out=battn_sb[:, 0:40], in_=battn_h), writes=["battn"])
    dmas(lambda s: s.dma_start(out=b1_sb[:], in_=b1_h), writes=["b1"])
    dmas(lambda s: s.dma_start(out=b2l_sb[:], in_=b2l_h), writes=["b2l"])
    dmas(lambda s: s.dma_start(out=sel_sb[:], in_=sel_h), writes=["sel"])
    dmas(lambda s: s.dma_start(out=id128_sb[:], in_=id128_h), writes=["id128"])
    dmas(lambda s: s.dma_start(
        out=rwb_sb[:].rearrange("p (c e) -> p c e", c=NDC),
        in_=rwb_h.transpose([1, 0, 2])), writes=["rwb"])
    dmas(lambda s: s.dma_start(out=mexc_sb[:], in_=mexc_h), writes=["mexc"])
    dmas(lambda s: s.dma_start(out=bias_sb[:, 0:8], in_=ble_h), writes=["ble"])
    dmas(lambda s: s.dma_start(out=g1_sb[0:1, 32:40], in_=lx_h), writes=["lxrow"])
    dmas(lambda s: s.dma_start(out=bias_sb[0:24, 32:33], in_=bel_h), writes=["bel"])

    V(lambda v: v.memset(ones128[:], 1.0), writes=["ones128"])
    V(lambda v: v.memset(onesr[:], 1.0), writes=["onesr"])
    V(lambda v: v.memset(ones12[:], 1.0), writes=["ones12"])
    V(lambda v: v.memset(bias_sb[:, 24:25], HALF_PI), writes=["hpi"])
    V(lambda v: v.tensor_scalar(bias_sb[0:24, 33:34], bias_sb[0:24, 32:33], -1.0, None, ALU.mult),
      reads=["bel"], writes=["negbel"])
    V(lambda v: v.tensor_scalar(battn_sb[:, 40:48], battn_sb[:, 8:16], float(1.0 / 32.0), None, ALU.mult),
      reads=["battn"], writes=["bka32"])
    G(lambda g: g.iota(kid_i[:, :], [[128, 8]], channel_multiplier=1), writes=["kid"])
    rws_bf = rws_sb[:].bitcast(BF16)
    V(lambda v: v.tensor_copy(rws_bf[:, 0:64], rwb_sb[:]), reads=["rwb"], writes=["rws"])
    V(lambda v: v.tensor_sub(rws_bf[:, 64:128], rwb_sb[:], rws_bf[:, 0:64]),
      reads=["rwb", "rws"], writes=["rws"])

    def emit_ln(tn, src_ap, dst_ap, sq_ap, rowb, src_regions, dst_regions,
                sq_region, row_region, post_chunk=None):
        for dcx in range(NDC):
            TT(lambda t, dcx=dcx: t.matmul(psA[0:1, 0:tn], ones128[:], src_ap(dcx),
                                           start=(dcx == 0), stop=(dcx == NDC - 1)),
               reads=[src_regions[dcx], "ones128"], writes=["psA"])
        sqv = sq_ap[:, 0:tn]
        for dcx in range(NDC):
            S(lambda s, dcx=dcx: s.activation(sqv, src_ap(dcx), AF.Square),
              reads=[src_regions[dcx]], writes=[sq_region])
            TT(lambda t, dcx=dcx: t.matmul(psB[0:1, 0:tn], ones128[:], sqv,
                                           start=(dcx == 0), stop=(dcx == NDC - 1)),
               reads=[sq_region, "ones128"], writes=["psB"])
        V(lambda v: v.tensor_scalar(rowb(0), psA[0:1, 0:tn], float(1.0 / D), None, ALU.mult),
          reads=["psA"], writes=[row_region])
        V(lambda v: v.tensor_scalar(rowb(1), psB[0:1, 0:tn], float(1.0 / D), None, ALU.mult),
          reads=["psB", row_region], writes=[row_region])
        V(lambda v: v.tensor_mul(rowb(2), rowb(0), rowb(0)),
          reads=[row_region], writes=[row_region])
        V(lambda v: v.tensor_sub(rowb(1), rowb(1), rowb(2)),
          reads=[row_region], writes=[row_region])
        V(lambda v: v.tensor_scalar(rowb(1), rowb(1), LN_EPS, None, ALU.add),
          reads=[row_region], writes=[row_region])
        V(lambda v: v.reciprocal(rowb(2), rowb(1)),
          reads=[row_region], writes=[row_region])
        S(lambda s: s.activation(rowb(1), rowb(2), AF.Sqrt),
          reads=[row_region], writes=[row_region])
        TT(lambda t: t.matmul(psD0[:, 0:tn], onesr[:], rowb(0), start=True, stop=True),
           reads=[row_region, "onesr"], writes=["psD0"])
        TT(lambda t: t.matmul(psD1[:, 0:tn], onesr[:], rowb(1), start=True, stop=True),
           reads=[row_region, "onesr"], writes=["psD1"])
        for dcx in range(NDC):
            V(lambda v, dcx=dcx: v.tensor_sub(dst_ap(dcx), src_ap(dcx), psD0[:, 0:tn]),
              reads=[src_regions[dcx], "psD0"], writes=[dst_regions[dcx]])
            V(lambda v, dcx=dcx: v.tensor_mul(dst_ap(dcx), dst_ap(dcx), psD1[:, 0:tn]),
              reads=[dst_regions[dcx], "psD1"], writes=[dst_regions[dcx]])
            if post_chunk is not None:
                post_chunk(dcx)

    NT512 = TC // 512
    rowslot = lambda i: bigA[0:1, 10240 + i * 512: 10240 + (i + 1) * 512]
    rowk_f = bigA[0:1, 12288:13312]
    rowi_i = bigA[0:1, 13312:14336].bitcast(I32)
    arow = lambda a, b: bigA[0:1, 14336 + a: 14336 + b]
    asc = lambda i: bigA[0:1, 15360 + i: 15360 + i + 1]
    fx1_row = bigA[0:1, 15488:16000]
    ubuf = lambda dcx: bigB[:, dcx * 512:(dcx + 1) * 512]
    wle_sb = bigB[0:24, 8192:9216]
    bits_sb = bigB[0:24, 9216:9728]
    syn_sb = bigB[0:12, 9728:10240]
    sqbuf = bigB[:, 4096:4608]
    dcmp = bigB[0:1, 4608:5632]
    fx2_row = bigB[0:1, 5632:6144]
    onetwob = bigB[0:12, 6144:6656]
    aoutb = bigB[:, 6656:7168]
    dsq = bigB[0:1, 7168:8192]

    if "P1" in phases:
        G(lambda g: g.iota(rowi_i, [[1, 1024]], channel_multiplier=0), writes=["rowi"])
        V(lambda v: v.tensor_copy(rowk_f, rowi_i), reads=["rowi"], writes=["rowk"])
        V(lambda v: v.memset(mm_sb[:, 0:2], BIG_NEG), writes=["mmacc"])
        for it in range(NT512):
            t0 = it * 512
            emit_ln(512, lambda dcx, t0=t0: xc(dcx, t0, 512), ubuf, sqbuf, rowslot,
                    [("x", dcx) for dcx in range(NDC)],
                    [("u", dcx) for dcx in range(NDC)], "sq", "lnrow")
            for dcx in range(NDC):
                V(lambda v, dcx=dcx: v.reduce_max(mm_sb[:, 2:3], ubuf(dcx), axis=AX.X),
                  reads=[("u", dcx)], writes=["mmtmp"])
                V(lambda v: v.tensor_max(mm_sb[:, 0:1], mm_sb[:, 0:1], mm_sb[:, 2:3]),
                  reads=["mmtmp", "mmacc"], writes=["mmacc"])
                V(lambda v, dcx=dcx: v.tensor_reduce(mm_sb[:, 2:3], ubuf(dcx), AX.X, ALU.min, negate=True),
                  reads=[("u", dcx)], writes=["mmtmp"])
                V(lambda v: v.tensor_max(mm_sb[:, 1:2], mm_sb[:, 1:2], mm_sb[:, 2:3]),
                  reads=["mmtmp", "mmacc"], writes=["mmacc"])
        TT(lambda t: t.transpose(psC[0:2, 0:128], mm_sb[:, 0:2], id128_sb[:]),
           reads=["mmacc", "id128"], writes=["psC"])
        V(lambda v: v.reduce_max(mm_sb[0:2, 4:5], psC[0:2, 0:128], axis=AX.X),
          reads=["psC"], writes=["mmred"])
        dmas(lambda s: s.dma_start(out=g1_sb[0:1, 0:2], in_=mm_sb[0:2, 4:5]),
             reads=["mmred"], writes=["e1row"])
        V(lambda v: v.memset(g1_sb[0:1, 2:8], 0.0), writes=["e1pad"])
        dmag(lambda g: g.dma_start(out=cc1i_h, in_=g1_sb[0:1, 0:8]),
             reads=["e1row", "e1pad"], writes=["cc1i"])
        p.step("gpsimd", lambda g: g.collective_compute(
            "AllGather", ALU.bypass, replica_groups=[list(range(NCORES))],
            ins=[cc1i_h.opt()], outs=[cc1o_h.opt()]),
            reads=["cc1i"], writes=["cc1o"], cc=True)
        dmag(lambda g: g.dma_start(out=gath_sb[0:2, 0:8], in_=cc1o_h.transpose([1, 0])[0:2], allow_slow_non_contiguous=True),
             reads=["cc1o"], writes=["e1g"])
        V(lambda v: v.reduce_max(gath_sb[0:2, 8:9], gath_sb[0:2, 0:8], axis=AX.X),
          reads=["e1g"], writes=["e1red"])
        dmas(lambda s: s.dma_start(out=g1_sb[0:1, 8:10], in_=gath_sb[0:2, 8:9]),
             reads=["e1red"], writes=["gmm"])

        dmas(lambda s: s.dma_start(out=arow(0, 1024), in_=xhead_h), writes=["xh"])
        V(lambda v: v.reduce_sum(asc(0), arow(0, 1024), axis=AX.X),
          reads=["xh"], writes=["dsc"])
        V(lambda v: v.tensor_scalar(asc(0), asc(0), float(1.0 / D), None, ALU.mult),
          reads=["dsc"], writes=["dsc"])
        S(lambda s: s.activation(dsq, arow(0, 1024), AF.Square),
          reads=["xh"], writes=["dsqr"])
        V(lambda v: v.reduce_sum(asc(1), dsq, axis=AX.X),
          reads=["dsqr", "dsc"], writes=["dsc"])
        V(lambda v: v.tensor_scalar(asc(1), asc(1), float(1.0 / D), None, ALU.mult),
          reads=["dsc"], writes=["dsc"])
        V(lambda v: v.tensor_mul(asc(2), asc(0), asc(0)), reads=["dsc"], writes=["dsc"])
        V(lambda v: v.tensor_sub(asc(1), asc(1), asc(2)), reads=["dsc"], writes=["dsc"])
        V(lambda v: v.tensor_scalar(asc(1), asc(1), LN_EPS, None, ALU.add),
          reads=["dsc"], writes=["dsc"])
        V(lambda v: v.reciprocal(asc(2), asc(1)), reads=["dsc"], writes=["dsc"])
        S(lambda s: s.activation(asc(3), asc(2), AF.Sqrt), reads=["dsc"], writes=["dsc"])
        V(lambda v: v.tensor_scalar(arow(0, 1024), arow(0, 1024), asc(0), None, ALU.subtract),
          reads=["xh", "dsc"], writes=["xh"])
        V(lambda v: v.tensor_scalar(arow(0, 1024), arow(0, 1024), asc(3), None, ALU.mult),
          reads=["xh", "dsc"], writes=["xh"])
        V(lambda v: v.tensor_add(asc(4), g1_sb[0:1, 8:9], g1_sb[0:1, 9:10]),
          reads=["gmm"], writes=["dsc"])
        V(lambda v: v.tensor_scalar(asc(4), asc(4), 1e-10, None, ALU.add),
          reads=["dsc"], writes=["dsc"])
        V(lambda v: v.reciprocal(asc(5), asc(4)), reads=["dsc"], writes=["dsc"])
        V(lambda v: v.tensor_scalar(arow(0, 1024), arow(0, 1024), g1_sb[0:1, 9:10], None, ALU.add),
          reads=["xh", "gmm"], writes=["xh"])
        V(lambda v: v.tensor_scalar(arow(0, 1024), arow(0, 1024), asc(5), None, ALU.mult),
          reads=["xh", "dsc"], writes=["xh"])
        for i, sc2 in enumerate((4, 16, 64, 256, 1024)):
            V(lambda v, sc2=sc2: v.reduce_sum(asc(6), arow(0, sc2), axis=AX.X),
              reads=["xh"], writes=["dsc"])
            V(lambda v, sc2=sc2: v.tensor_scalar(asc(6), asc(6), float(1.0 / sc2), None, ALU.mult),
              reads=["dsc"], writes=["dsc"])
            V(lambda v, sc2=sc2: v.tensor_scalar(dcmp[0:1, 0:sc2], arow(0, sc2), asc(6), None, ALU.is_gt),
              reads=["xh", "dsc"], writes=["dcmpr"])
            V(lambda v, sc2=sc2, i=i: v.reduce_sum(asc(16 + i), dcmp[0:1, 0:sc2], axis=AX.X),
              reads=["dcmpr"], writes=["dcnt"])
            V(lambda v, i=i: v.tensor_scalar(asc(16 + i), asc(16 + i), 1.0, None, ALU.max),
              reads=["dcnt"], writes=["dcnt"])
        S(lambda s: s.activation(bigA[0:1, 15376:15381], bigA[0:1, 15376:15381], AF.Ln),
          reads=["dcnt"], writes=["dly"])
        S(lambda s: s.activation(bigA[0:1, 15384:15389], g1_sb[0:1, 32:37], AF.Ln),
          reads=["lxrow"], writes=["dlx"])
        ly = bigA[0:1, 15376:15381]
        lx = bigA[0:1, 15384:15389]
        tmp5 = bigA[0:1, 15392:15397]
        V(lambda v: v.tensor_mul(tmp5, lx, ly), reads=["dly", "dlx"], writes=["d5"])
        V(lambda v: v.reduce_sum(asc(40), tmp5, axis=AX.X), reads=["d5"], writes=["dsxy"])
        V(lambda v: v.reduce_sum(asc(41), lx, axis=AX.X), reads=["dlx"], writes=["dsx"])
        V(lambda v: v.reduce_sum(asc(42), ly, axis=AX.X), reads=["dly"], writes=["dsy"])
        V(lambda v: v.tensor_mul(tmp5, lx, lx), reads=["dlx", "d5"], writes=["d5"])
        V(lambda v: v.reduce_sum(asc(43), tmp5, axis=AX.X), reads=["d5"], writes=["dsxx"])
        V(lambda v: v.tensor_scalar(asc(44), asc(40), 5.0, None, ALU.mult),
          reads=["dsxy"], writes=["dnum"])
        V(lambda v: v.tensor_mul(asc(45), asc(41), asc(42)),
          reads=["dsx", "dsy"], writes=["dt"])
        V(lambda v: v.tensor_sub(asc(44), asc(44), asc(45)),
          reads=["dnum", "dt"], writes=["dnum"])
        V(lambda v: v.tensor_scalar(asc(46), asc(43), 5.0, None, ALU.mult),
          reads=["dsxx"], writes=["dden"])
        V(lambda v: v.tensor_mul(asc(45), asc(41), asc(41)),
          reads=["dsx"], writes=["dt"])
        V(lambda v: v.tensor_sub(asc(46), asc(46), asc(45)),
          reads=["dden", "dt"], writes=["dden"])
        V(lambda v: v.reciprocal(asc(45), asc(46)), reads=["dden"], writes=["dt"])
        V(lambda v: v.tensor_mul(asc(47), asc(44), asc(45)),
          reads=["dnum", "dt"], writes=["dq"])
        V(lambda v: v.tensor_scalar(asc(47), asc(47), -1.0, 1.0, ALU.mult, ALU.max),
          reads=["dq"], writes=["dq"])
        V(lambda v: v.tensor_scalar(asc(48), asc(47), 2.0, None, ALU.min),
          reads=["dq"], writes=["dD"])
        V(lambda v: v.tensor_scalar(asc(49), asc(48), 0.2, 1.0, ALU.mult, ALU.add),
          reads=["dD"], writes=["dom"])
        V(lambda v: v.tensor_mul(asc(49), asc(49), asc(48)),
          reads=["dom", "dD"], writes=["dom"])
        V(lambda v: v.tensor_scalar(asc(50), asc(48), 0.5, 1.0, ALU.mult, ALU.add),
          reads=["dD"], writes=["da"])
        V(lambda v: v.tensor_scalar(asc(51), asc(48), 0.1, None, ALU.mult),
          reads=["dD"], writes=["db"])
        TT(lambda t: t.matmul(psB[:, 0:3], onesr[:], bigA[0:1, 15409:15412], start=True, stop=True),
           reads=["dom", "da", "db", "onesr"], writes=["psB"])
        V(lambda v: v.tensor_copy(bias_sb[:, 25:28], psB[:, 0:3]),
          reads=["psB"], writes=["dbc"])

        V(lambda v: v.tensor_copy(gtm_sb[:, 0:8], kid_i[:, :]), reads=["kid"], writes=["gt0"])
        V(lambda v: v.tensor_scalar(gtm_sb[:, 8:16], gtm_sb[:, 0:8], -1.0, 1024.0, ALU.mult, ALU.add),
          reads=["gt0"], writes=["gt1"])
        V(lambda v: v.tensor_tensor(gtm_sb[:, 0:8], gtm_sb[:, 0:8], gtm_sb[:, 8:16], ALU.min),
          reads=["gt0", "gt1"], writes=["gt0"])
        V(lambda v: v.tensor_scalar(gtm_sb[:, 0:8], gtm_sb[:, 0:8], float(1.0 / 1024.0), 1e-8, ALU.mult, ALU.add),
          reads=["gt0"], writes=["gt0"])
        V(lambda v: v.tensor_scalar(gtm_sb[:, 8:16], gtm_sb[:, 0:8], bias_sb[:, 26:27], None, ALU.mult),
          reads=["gt0", "dbc"], writes=["gt1"])
        V(lambda v: v.tensor_scalar(gtm_sb[:, 8:16], gtm_sb[:, 8:16], bias_sb[:, 25:26], None, ALU.add),
          reads=["gt1", "dbc"], writes=["gt1"])
        V(lambda v: v.tensor_scalar(gtm_sb[:, 16:24], gtm_sb[:, 8:16], PI, None, ALU.is_gt),
          reads=["gt1"], writes=["gt2"])
        V(lambda v: v.scalar_tensor_tensor(gtm_sb[:, 8:16], gtm_sb[:, 16:24], -TWO_PI,
                                           gtm_sb[:, 8:16], ALU.mult, ALU.add),
          reads=["gt2", "gt1"], writes=["gt1"])
        S(lambda s: s.activation(gtm_sb[:, 16:24], gtm_sb[:, 8:16], AF.Sin),
          reads=["gt1"], writes=["gt2"])
        S(lambda s: s.activation(gtm_sb[:, 24:32], gtm_sb[:, 0:8], AF.Square),
          reads=["gt0"], writes=["gt3"])
        V(lambda v: v.tensor_scalar(gtm_sb[:, 24:32], gtm_sb[:, 24:32], bias_sb[:, 27:28], None, ALU.mult),
          reads=["gt3", "dbc"], writes=["gt3"])
        V(lambda v: v.tensor_scalar(gtm_sb[:, 8:16], gtm_sb[:, 0:8], -1.0, None, ALU.mult),
          reads=["gt0"], writes=["gt1"])
        V(lambda v: v.tensor_scalar(gtm_sb[:, 8:16], gtm_sb[:, 8:16], bias_sb[:, 25:26], None, ALU.add),
          reads=["gt1", "dbc"], writes=["gt1"])
        V(lambda v: v.tensor_add(gtm_sb[:, 8:16], gtm_sb[:, 8:16], gtm_sb[:, 24:32]),
          reads=["gt1", "gt3"], writes=["gt1"])
        S(lambda s: s.activation(gtm_sb[:, 24:32], gtm_sb[:, 8:16], AF.Sin),
          reads=["gt1"], writes=["gt3"])
        V(lambda v: v.tensor_mul(bias_sb[:, 16:24], gtm_sb[:, 16:24], gtm_sb[:, 24:32]),
          reads=["gt2", "gt3"], writes=["fim"])
        S(lambda s: s.activation(gtm_sb[:, 24:32], gtm_sb[:, 8:16], AF.Sin,
                                 bias=bias_sb[:, 24:25], scale=-1.0),
          reads=["gt1", "hpi"], writes=["gt3"])
        V(lambda v: v.tensor_mul(bias_sb[:, 8:16], gtm_sb[:, 16:24], gtm_sb[:, 24:32]),
          reads=["gt2", "gt3"], writes=["fre"])
        V(lambda v: v.tensor_scalar(bias_sb[:, 8:16], bias_sb[:, 8:16], float(1.0 / 1024.0), None, ALU.mult),
          reads=["fre"], writes=["fre"])
        V(lambda v: v.tensor_scalar(bias_sb[:, 16:24], bias_sb[:, 16:24], float(1.0 / 1024.0), None, ALU.mult),
          reads=["fim"], writes=["fim"])
        V(lambda v: v.tensor_scalar(bias_sb[:, 36:44], bias_sb[:, 16:24], -1.0, None, ALU.mult),
          reads=["fim"], writes=["fimn"])

        CT = bigA[:, 0:1024]
        STb = bigA[:, 1024:2048]
        T2b = bigA[:, 2048:3072]
        T3b = bigA[:, 3072:4096]
        CTm = bigA[:, 4096:4224]
        STm = bigA[:, 4224:4352]
        for kc in range(NDC):
            krow = rowk_f[0:1, 128 * kc: 128 * kc + 128]
            for h in range(2):
                pu = psU0 if h == 0 else psU1
                pur = "psU%d" % h
                TT(lambda t, h=h, krow=krow, pu=pu: t.matmul(
                    pu[:, :], krow,
                    rowk_f[0:1, h * 512:(h + 1) * 512], start=True, stop=True),
                   reads=["rowk"], writes=[pur])
                V(lambda v, h=h, pu=pu: v.tensor_copy(
                    T3b[:, h * 512:(h + 1) * 512].bitcast(I32), pu[:, :]),
                  reads=[pur], writes=["t3"])
                V(lambda v, h=h: v.tensor_scalar(
                    T2b[:, h * 512:(h + 1) * 512].bitcast(I32),
                    T3b[:, h * 512:(h + 1) * 512].bitcast(I32), 1023, None, ALU.bitwise_and),
                  reads=["t3"], writes=["t2"])
                V(lambda v, h=h: v.tensor_copy(
                    CT[:, h * 512:(h + 1) * 512], T2b[:, h * 512:(h + 1) * 512].bitcast(I32)),
                  reads=["t2"], writes=["ct"])
            V(lambda v: v.tensor_scalar(T2b, CT, 512.0, None, ALU.is_ge),
              reads=["ct"], writes=["t2"])
            V(lambda v: v.tensor_scalar(T3b, CT, float(TWO_PI / 1024.0), None, ALU.mult),
              reads=["ct"], writes=["t3"])
            V(lambda v: v.scalar_tensor_tensor(STb, T2b, -TWO_PI, T3b, ALU.mult, ALU.add),
              reads=["t2", "t3"], writes=["st"])
            S(lambda s: s.activation(T2b, STb, AF.Abs), reads=["st"], writes=["t2"])
            S(lambda s: s.activation(CT, T2b, AF.Sin, bias=bias_sb[:, 24:25], scale=-1.0),
              reads=["t2", "hpi"], writes=["ct"])
            S(lambda s: s.activation(STb, STb, AF.Sin), reads=["st"], writes=["st"])
            V(lambda v: v.tensor_copy(CTm, CT[:, 0:128]), reads=["ct"], writes=["ctm"])
            V(lambda v: v.tensor_copy(STm, STb[:, 0:128]), reads=["st"], writes=["stm"])
            fre_c = bias_sb[:, 8 + kc: 9 + kc]
            fim_c = bias_sb[:, 16 + kc: 17 + kc]
            fimn_c = bias_sb[:, 36 + kc: 37 + kc]
            V(lambda v, a=fre_c: v.tensor_scalar(T3b, STb, a, None, ALU.mult),
              reads=["st", "fre"], writes=["t3"])
            V(lambda v, a=fim_c: v.scalar_tensor_tensor(T2b, CT, a, T3b, ALU.mult, ALU.add),
              reads=["ct", "t3", "fim"], writes=["t2"])
            V(lambda v, a=fre_c: v.tensor_scalar(T3b, CT, a, None, ALU.mult),
              reads=["ct", "fre"], writes=["t3"])
            V(lambda v, a=fimn_c: v.scalar_tensor_tensor(CT, STb, a, T3b, ALU.mult, ALU.add),
              reads=["st", "t3", "fimn"], writes=["ct"])
            for h in range(2):
                TT(lambda t, h=h, kc=kc: t.matmul(
                    psC[:, h * 512:(h + 1) * 512], CTm, CT[:, h * 512:(h + 1) * 512],
                    start=(kc == 0), stop=False),
                   reads=["ctm", "ct"], writes=["psC"])
                TT(lambda t, h=h, kc=kc: t.matmul(
                    psC[:, h * 512:(h + 1) * 512], STm, T2b[:, h * 512:(h + 1) * 512],
                    start=False, stop=(kc == NDC - 1)),
                   reads=["stm", "t2"], writes=["psC"])
        for h in range(2):
            V(lambda v, h=h: v.tensor_copy(bigA[:, 5120 + h * 512: 5120 + (h + 1) * 512],
                                           psC[:, h * 512:(h + 1) * 512]),
              reads=["psC"], writes=["strip0"])

    if "P2" in phases:
        strip = lambda j: bigA[:, 5120 + j * 128: 5120 + (j + 1) * 128]
        X1 = lambda ncc: bigA[:, 6144 + ncc * 512: 6144 + (ncc + 1) * 512]
        for it in range(NT512):
            t0 = it * 512
            emit_ln(512, lambda dcx, t0=t0: xc(dcx, t0, 512), ubuf, sqbuf, rowslot,
                    [("x", dcx) for dcx in range(NDC)],
                    [("u", dcx) for dcx in range(NDC)], "sq", "lnrow")
            for ncc in range(NDC):
                pu = psU0 if ncc % 2 == 0 else psU1
                pur = "psU%d" % (ncc % 2)
                for mc in range(NDC):
                    TT(lambda t, ncc=ncc, mc=mc, pu=pu: t.matmul(
                        pu[:, :], strip((ncc - mc) % 8), ubuf(mc),
                        start=(mc == 0), stop=(mc == NDC - 1)),
                       reads=["strip0", ("u", mc)], writes=[pur])
                V(lambda v, ncc=ncc, pu=pu: v.tensor_copy(X1(ncc), pu[:, :]),
                  reads=[pur], writes=[("x1", ncc)])
            for dcx in range(NDC):
                TT(lambda t, dcx=dcx: t.matmul(
                    psA[0:24, :], wel_sb[:, dcx * 24:(dcx + 1) * 24], X1(dcx),
                    start=(dcx == 0), stop=(dcx == NDC - 1)),
                   reads=["wel", ("x1", dcx)], writes=["psA"])
            S(lambda s: s.activation(bits_sb, psA[0:24, :], AF.Identity,
                                     bias=bias_sb[0:24, 32:33], scale=1.0),
              reads=["psA", "bel"], writes=["bits"])
            V(lambda v: v.tensor_scalar(bits_sb, bits_sb, 0.0, None, ALU.is_gt),
              reads=["bits"], writes=["bits"])
            TT(lambda t: t.matmul(psB[0:12, :], ht_sb[:, :], bits_sb,
                                  start=True, stop=True),
               reads=["htm", "bits"], writes=["psB"])
            V(lambda v: v.tensor_scalar(syn_sb, psB[0:12, :], 0.5, None, ALU.is_gt),
              reads=["psB"], writes=["syn"])
            TT(lambda t: t.matmul(psA[0:1, :], ones12[:], syn_sb, start=True, stop=True),
               reads=["ones12", "syn"], writes=["psA"])
            V(lambda v: v.tensor_scalar(fx1_row, psA[0:1, :], 0.5, None, ALU.is_gt),
              reads=["psA"], writes=["fx1"])
            V(lambda v: v.tensor_scalar(fx2_row, psA[0:1, :], 3.5, None, ALU.is_le),
              reads=["psA"], writes=["fx2"])
            V(lambda v: v.tensor_mul(fx1_row, fx1_row, fx2_row),
              reads=["fx1", "fx2"], writes=["fx1"])
            TT(lambda t: t.matmul(psB[0:12, :], onesr[0:1, 0:12], fx1_row,
                                  start=True, stop=True),
               reads=["onesr", "fx1"], writes=["psB"])
            V(lambda v: v.tensor_tensor(syn_sb, syn_sb, psB[0:12, :], ALU.mult),
              reads=["syn", "psB"], writes=["syn"])
            V(lambda v: v.tensor_scalar(onetwob, bits_sb[0:12, 0:512], -2.0, 1.0, ALU.mult, ALU.add),
              reads=["bits"], writes=["otb"])
            V(lambda v: v.tensor_mul(syn_sb, syn_sb, onetwob),
              reads=["syn", "otb"], writes=["syn"])
            V(lambda v: v.tensor_add(bits_sb[0:12, 0:512], bits_sb[0:12, 0:512], syn_sb),
              reads=["bits", "syn"], writes=["bits"])
            for dcx in range(NDC):
                TT(lambda t, dcx=dcx: t.matmul(
                    psB[:, :], wle_sb[:, dcx * 128:(dcx + 1) * 128], bits_sb,
                    start=True, stop=True),
                   reads=["wle", "bits"], writes=["psB"])
                S(lambda s, dcx=dcx: s.activation(aoutb, psB[:, :], AF.Identity,
                                                  bias=bias_sb[:, dcx:dcx + 1], scale=1.0),
                  reads=["psB", "ble"], writes=["aout"])
                V(lambda v, dcx=dcx: v.tensor_add(aoutb, aoutb, X1(dcx)),
                  reads=["aout", ("x1", dcx)], writes=["aout"])
                V(lambda v, dcx=dcx, t0=t0: v.tensor_add(xc(dcx, t0, 512), xc(dcx, t0, 512), aoutb),
                  reads=["aout", ("x", dcx)], writes=[("x", dcx)])

    def emit_ffn(call, fence=(), do_split=False):
        bigAb_ = bigA[:].bitcast(BF16)

        SPB2 = bigB[:, 11776:13824]

        def emit_split_for(gi2, fdeps=()):
            if not do_split or gi2 >= 128:
                return
            e2, g2 = gi2 // 16, gi2 % 16
            s2 = gi2 % 4
            for wi, (wsrc, wdst, wtag, cbase, ldq, buf) in enumerate((
                    (w1_h, w1s_h, "w1s", s2 * 8192, dmas, wk_sb[:, 0:2048]),
                    (w2_h, w2s_h, "w2s", s2 * 8192 + 4096, dmag, SPB2))):
                wr = "wst%d" % (wi + 1)
                hi = bigAb_[:, cbase: cbase + 2048]
                lo = bigAb_[:, cbase + 2048: cbase + 4096]
                ldeps = list(fdeps)
                if wi == 1 and gi2 == 0:
                    ldeps += p.fence(["utmp", "frow", "fsq"])
                ldq(lambda s, wsrc=wsrc, e2=e2, g2=g2, buf=buf: s.dma_start(
                    out=buf, in_=wsrc[e2, 2 * g2: 2 * g2 + 2].transpose([1, 0, 2])),
                    writes=[("spin", wi)], deps=ldeps)
                S(lambda s, hi=hi, buf=buf: s.activation(hi, buf, AF.Identity),
                  reads=[("spin", wi)], writes=[(wr, s2)], deps=list(fdeps))
                V(lambda v, hi=hi, lo=lo, buf=buf: v.tensor_sub(lo, buf, hi),
                  reads=[("spin", wi), (wr, s2)], writes=[(wr, s2)])
                dmaa(lambda s, wdst=wdst, e2=e2, g2=g2, cbase=cbase: s.dma_start(
                    out=wdst[e2, g2], in_=bigAb_[:, cbase: cbase + 4096]),
                    reads=[(wr, s2)], writes=[(wtag, e2, g2)])
        UHb = lambda dcx: bigB[:, 0:2048].bitcast(BF16)[:, dcx * 512:(dcx + 1) * 512]
        ULb = lambda dcx: bigB[:, 2048:4096].bitcast(BF16)[:, dcx * 512:(dcx + 1) * 512]
        ACC = lambda dcx: bigB[:, 4096 + dcx * 512: 4096 + (dcx + 1) * 512]
        GGF = bigB[:, 8192:8704]
        GGH = lambda i: bigB[:, 8704 + i * 256: 8704 + (i + 1) * 256].bitcast(BF16)
        GGL = lambda i: bigB[:, 9728 + i * 256: 9728 + (i + 1) * 256].bitcast(BF16)
        GBC = bigB[:, 10752:11264]
        SQF = bigB[:, 11264:11776]
        UTMP = bigB[:, 11776:12288]
        frow = lambda i: bigB[0:1, 12288 + i * 512: 12288 + (i + 1) * 512]
        bigAb = bigA[:].bitcast(BF16)

        def W1t(s_, i, hilo, dcx):
            base = s_ * 8192 + hilo * 2048 + i * 1024
            return bigAb[:, base + dcx * 128: base + (dcx + 1) * 128]

        def W2t(s_, i, hilo, dcx):
            base = s_ * 8192 + 4096 + hilo * 2048 + i * 1024
            return bigAb[:, base + dcx * 128: base + (dcx + 1) * 128]

        groups = [(e, g) for e in range(8) for g in range(16)]
        for tq in range(4):
            t0 = tq * 512

            def usplit(dcx):
                V(lambda v, dcx=dcx: v.tensor_copy(UHb(dcx), UTMP),
                  reads=["utmp"], writes=[("uh", dcx)])
                V(lambda v, dcx=dcx: v.tensor_sub(ULb(dcx), UTMP, UHb(dcx)),
                  reads=["utmp", ("uh", dcx)], writes=[("ul", dcx)])

            emit_ln(512, lambda dcx, t0=t0: xc(dcx, t0, 512), lambda dcx: UTMP,
                    SQF, frow,
                    [("x", dcx) for dcx in range(NDC)],
                    ["utmp"] * NDC, "fsq", "frow", post_chunk=usplit)
            for dcx in range(NDC):
                TT(lambda t, dcx=dcx: t.matmul(psA[0:8, :], rws_bf[:, dcx * 8:(dcx + 1) * 8],
                                               UHb(dcx), start=(dcx == 0), stop=False),
                   reads=["rws", ("uh", dcx)], writes=["psA"])
                TT(lambda t, dcx=dcx: t.matmul(psA[0:8, :], rws_bf[:, dcx * 8:(dcx + 1) * 8],
                                               ULb(dcx), start=False, stop=False),
                   reads=["rws", ("ul", dcx)], writes=["psA"])
                TT(lambda t, dcx=dcx: t.matmul(psA[0:8, :], rws_bf[:, 64 + dcx * 8: 64 + (dcx + 1) * 8],
                                               UHb(dcx), start=False, stop=(dcx == NDC - 1)),
                   reads=["rws", ("uh", dcx)], writes=["psA"])
            V(lambda v: v.tensor_copy(rl_sb[:, :], psA[0:8, :]), reads=["psA"], writes=["rl"])
            for b in range(4):
                TM = gtm_sb[:, b * 16: b * 16 + 8]
                m1c = gtm_sb[:, b * 16 + 8: b * 16 + 9]
                n1c = gtm_sb[:, b * 16 + 9: b * 16 + 10]
                m2c = gtm_sb[:, b * 16 + 10: b * 16 + 11]
                n2c = gtm_sb[:, b * 16 + 11: b * 16 + 12]
                pc = gtm_sb[:, b * 16 + 12: b * 16 + 13]
                ac = gtm_sb[:, b * 16 + 13: b * 16 + 14]
                g2c = gtm_sb[:, b * 16 + 14: b * 16 + 15]
                C1 = bigB[:, 11264 + b * 32 + 0: 11264 + b * 32 + 8]
                C2 = bigB[:, 11264 + b * 32 + 8: 11264 + b * 32 + 16]
                TMX = bigB[:, 11264 + b * 32 + 16: 11264 + b * 32 + 24]
                rg = "g%d" % b
                TT(lambda t, b=b: t.transpose(psB[:, 0:8], rl_sb[:, b * 128:(b + 1) * 128],
                                              id128_sb[0:8, 0:8]),
                   reads=["rl", "id128"], writes=["psB"])
                V(lambda v, TM=TM: v.tensor_copy(TM, psB[:, 0:8]), reads=["psB"], writes=[rg])
                V(lambda v, TM=TM, m1c=m1c: v.reduce_max(m1c, TM, axis=AX.X), reads=[rg], writes=[rg])
                V(lambda v, TM=TM, m1c=m1c, C1=C1: v.tensor_scalar(C1, TM, m1c, None, ALU.is_equal),
                  reads=[rg, "fsq"], writes=[rg, "fsq"])
                V(lambda v, C1=C1, n1c=n1c: v.reduce_sum(n1c, C1, axis=AX.X), reads=[rg, "fsq"], writes=[rg])
                V(lambda v, C1=C1, TM=TM, TMX=TMX: v.scalar_tensor_tensor(TMX, C1, BIG_NEG, TM, ALU.mult, ALU.add),
                  reads=[rg, "fsq"], writes=[rg, "fsq"])
                V(lambda v, TMX=TMX, m2c=m2c: v.reduce_max(m2c, TMX, axis=AX.X), reads=[rg, "fsq"], writes=[rg])
                V(lambda v, TMX=TMX, m2c=m2c, C2=C2: v.tensor_scalar(C2, TMX, m2c, None, ALU.is_equal),
                  reads=[rg, "fsq"], writes=[rg, "fsq"])
                V(lambda v, C2=C2, n2c=n2c: v.reduce_sum(n2c, C2, axis=AX.X), reads=[rg, "fsq"], writes=[rg])
                V(lambda v, n1c=n1c, pc=pc: v.tensor_scalar(pc, n1c, -1.0, 1.0, ALU.add, ALU.min),
                  reads=[rg], writes=[rg])
                V(lambda v, m1c=m1c, m2c=m2c, n1c=n1c: v.tensor_sub(n1c, m1c, m2c),
                  reads=[rg], writes=[rg])
                V(lambda v, n1c=n1c, pc=pc: v.tensor_scalar(n1c, n1c, pc, None, ALU.mult),
                  reads=[rg], writes=[rg])
                V(lambda v, n1c=n1c, m2c=m2c: v.tensor_add(n1c, n1c, m2c),
                  reads=[rg], writes=[rg])
                V(lambda v, m1c=m1c, n1c=n1c: v.tensor_sub(n1c, m1c, n1c),
                  reads=[rg], writes=[rg])
                S(lambda s, n1c=n1c, ac=ac: s.activation(ac, n1c, AF.Sigmoid),
                  reads=[rg], writes=[rg + "a"])
                V(lambda v, ac=ac, m1c=m1c: v.tensor_scalar(m1c, ac, -1.0, 1.0, ALU.mult, ALU.add),
                  reads=[rg + "a"], writes=[rg])
                V(lambda v, pc=pc: v.tensor_scalar(pc, pc, -1.0, 1.0, ALU.mult, ALU.add),
                  reads=[rg], writes=[rg])
                V(lambda v, m1c=m1c, pc=pc: v.tensor_mul(m1c, m1c, pc),
                  reads=[rg], writes=[rg])
                V(lambda v, n2c=n2c, m2c=m2c: v.reciprocal(m2c, n2c),
                  reads=[rg], writes=[rg])
                V(lambda v, m1c=m1c, m2c=m2c, g2c=g2c: v.tensor_mul(g2c, m1c, m2c),
                  reads=[rg], writes=[rg])
                V(lambda v, C2=C2, g2c=g2c: v.tensor_scalar(C2, C2, g2c, None, ALU.mult),
                  reads=[rg, "fsq"], writes=[rg, "fsq"])
                V(lambda v, C1=C1, ac=ac, C2=C2, TM=TM: v.scalar_tensor_tensor(TM, C1, ac, C2, ALU.mult, ALU.add),
                  reads=[rg, rg + "a", "fsq"], writes=[rg])
                TT(lambda t, TM=TM: t.transpose(psB[0:8, 0:128], TM, id128_sb[:, :]),
                   reads=[rg, "id128"], writes=["psB"])
                V(lambda v, b=b: v.tensor_copy(gates_sb[:, b * 128:(b + 1) * 128], psB[0:8, 0:128]),
                  reads=["psB"], writes=["gates"])
            for dcx in range(NDC):
                TT(lambda t, dcx=dcx: t.matmul(psB[:, :], b2l_sb[:, dcx * 128:(dcx + 1) * 128],
                                               gates_sb[:, :], start=True, stop=True),
                   reads=["b2l", "gates"], writes=["psB"])
                V(lambda v, dcx=dcx: v.tensor_copy(ACC(dcx), psB[:, :]),
                  reads=["psB"], writes=[("acc", dcx)])
            pending_down = []
            if tq == 0 and do_split:
                emit_split_for(0, fdeps=fence)
                emit_split_for(1, fdeps=fence)
            for gi, (e, g) in enumerate(groups):
                slot = gi % 4
                fdeps = list(fence) if (tq == 0 and gi < 4 and not do_split) else []
                if not (tq == 0 and do_split):
                    dmas(lambda s, e=e, g=g, slot=slot: s.dma_start(
                        out=bigAb[:, slot * 8192: slot * 8192 + 4096],
                        in_=w1s_h[e, g]),
                        writes=[("wst1", slot)], deps=fdeps,
                        reads=[("w1s", e, g)])
                    dmas(lambda s, e=e, g=g, slot=slot: s.dma_start(
                        out=bigAb[:, slot * 8192 + 4096: slot * 8192 + 8192],
                        in_=w2s_h[e, g]),
                        writes=[("wst2", slot)],
                        deps=fdeps,
                        reads=[("w2s", e, g)])
                if g == 0:
                    TT(lambda t, e=e: t.matmul(psB[:, :], sel_sb[:, e * 128:(e + 1) * 128],
                                               gates_sb[:, :], start=True, stop=True),
                       reads=["sel", "gates"], writes=["psB"])
                    V(lambda v: v.tensor_copy(GBC, psB[:, :]), reads=["psB"], writes=["gbc"])
                for i in range(2):
                    fc = 2 * g + i
                    gsl = 2 * (gi % 2) + i
                    pu = psU0 if (i % 2 == 0) else psU1
                    pur = "psU%d" % (i % 2)
                    for dcx in range(NDC):
                        TT(lambda t, i=i, dcx=dcx, slot=slot, pu=pu: t.matmul(
                            pu[:, :], W1t(slot, i, 0, dcx), UHb(dcx),
                            start=(dcx == 0), stop=False),
                           reads=[("wst1", slot), ("uh", dcx)], writes=[pur])
                        TT(lambda t, i=i, dcx=dcx, slot=slot, pu=pu: t.matmul(
                            pu[:, :], W1t(slot, i, 0, dcx), ULb(dcx),
                            start=False, stop=False),
                           reads=[("wst1", slot), ("ul", dcx)], writes=[pur])
                        TT(lambda t, i=i, dcx=dcx, slot=slot, pu=pu: t.matmul(
                            pu[:, :], W1t(slot, i, 1, dcx), UHb(dcx),
                            start=False, stop=(dcx == NDC - 1)),
                           reads=[("wst1", slot), ("uh", dcx)], writes=[pur])
                    S(lambda s, gsl=gsl, e=e, fc=fc, pu=pu: s.activation(
                        GGF, pu[:, :], AF.Gelu,
                        bias=b1_sb[:, e * 32 + fc: e * 32 + fc + 1], scale=1.0),
                      reads=[pur, "b1"], writes=["ggf"])
                    V(lambda v: v.tensor_mul(GGF, GGF, GBC),
                      reads=["ggf", "gbc"], writes=["ggf"])
                    S(lambda s, gsl=gsl: s.activation(GGH(gsl), GGF, AF.Identity),
                      reads=["ggf"], writes=[("ggh", gsl)])
                    V(lambda v, gsl=gsl: v.tensor_sub(GGL(gsl), GGF, GGH(gsl)),
                      reads=["ggf", ("ggh", gsl)], writes=[("ggl", gsl)])
                def emit_down(slot=slot, gp=gi % 2):
                    for dcx in range(NDC):
                        pd = (psD0, psD1, psA)[dcx % 3]
                        pdr = ("psD0", "psD1", "psA")[dcx % 3]
                        for i in range(2):
                            gsl2 = 2 * gp + i
                            TT(lambda t, i=i, gsl2=gsl2, dcx=dcx, slot=slot, pd=pd: t.matmul(
                                pd[:, :], W2t(slot, i, 0, dcx), GGH(gsl2),
                                start=(i == 0), stop=False),
                               reads=[("wst2", slot), ("ggh", gsl2)], writes=[pdr])
                            TT(lambda t, i=i, gsl2=gsl2, dcx=dcx, slot=slot, pd=pd: t.matmul(
                                pd[:, :], W2t(slot, i, 0, dcx), GGL(gsl2),
                                start=False, stop=False),
                               reads=[("wst2", slot), ("ggl", gsl2)], writes=[pdr])
                            TT(lambda t, i=i, gsl2=gsl2, dcx=dcx, slot=slot, pd=pd: t.matmul(
                                pd[:, :], W2t(slot, i, 1, dcx), GGH(gsl2),
                                start=False, stop=(i == 1)),
                               reads=[("wst2", slot), ("ggh", gsl2)], writes=[pdr])
                        V(lambda v, dcx=dcx, pd=pd: v.tensor_add(ACC(dcx), ACC(dcx), pd[:, :]),
                          reads=[pdr, ("acc", dcx)], writes=[("acc", dcx)])
                pending_down.append(emit_down)
                if len(pending_down) > 1:
                    pending_down.pop(0)()
                if tq == 0 and do_split:
                    emit_split_for(gi + 2, fdeps=fence if gi < 2 else ())
            while pending_down:
                pending_down.pop(0)()
            for dcx in range(NDC):
                V(lambda v, dcx=dcx, t0=t0: v.tensor_add(xc(dcx, t0, 512), xc(dcx, t0, 512), ACC(dcx)),
                  reads=[("acc", dcx), ("x", dcx)], writes=[("x", dcx)])

    if "P3" in phases:
        emit_ffn(0, do_split=True, fence=p.fence(["ct", "st", "t2", "t3", "ctm", "stm", "rowk", "rowi", "strip0",
                "lnrow", "xh", "dsc", "dly", "dlx", "d5", "dsxy", "dsx", "dsy",
                "dsxx", "dnum", "dden", "dt", "dq", "dD", "dom", "da", "db",
                "dcnt", "fx1"] + [("x1", i) for i in range(NDC)]))

    if "P4" in phases:
        CHB = bigB[:, 0:2048]
        HT2 = lambda dcx: bigB[:, 2048 + dcx * 256: 2048 + (dcx + 1) * 256]
        RQ = bigB[:, 8192:9216]
        T1Q = bigB[:, 9216:10240]
        SQ4 = bigB[:, 10240:10752]
        VT4 = bigB[:, 6144:6656]
        hrow1 = lambda i: bigB[0:1, 11264 + i * 256: 11264 + (i + 1) * 256]
        hrow2 = lambda i: bigB[0:1, 12288 + i * 256: 12288 + (i + 1) * 256]
        ONES2K = bigB[:, 4096:6144]
        V(lambda v: v.memset(ONES2K, 1.0), writes=["ones2k"])
        NT256 = TC // 256

        def emit_h_tile(it):
            t0 = it * 256
            emit_ln(256, lambda dcx, t0=t0: xc(dcx, t0, 256), HT2, SQ4, hrow1,
                    [("x", dcx) for dcx in range(NDC)],
                    [("h", dcx) for dcx in range(NDC)], "p4sq", "hrow1")
            emit_ln(256, HT2, HT2, SQ4, hrow2,
                    [("h", dcx) for dcx in range(NDC)],
                    [("h", dcx) for dcx in range(NDC)], "p4sq", "hrow2")
            for dcx in range(NDC):
                V(lambda v, dcx=dcx: v.tensor_scalar(
                    HT2(dcx), HT2(dcx), battn_sb[:, 24 + dcx:25 + dcx],
                    battn_sb[:, 32 + dcx:33 + dcx], ALU.mult, ALU.add),
                  reads=[("h", dcx), "battn"], writes=[("h", dcx)])

        def make_stream(src_h, tag, pos):
            state = {"k": 0}

            def prefetch():
                k = state["k"]
                if k < len(pos):
                    _, dco2 = pos[k]
                    slot = k % 2
                    dmas(lambda s, dco2=dco2, slot=slot: s.dma_start(
                        out=wk_sb[:, slot * 1024:(slot + 1) * 1024], in_=src_h[dco2]),
                        writes=[(tag, slot)])
                    state["k"] += 1
            return prefetch

        pos = [(it, dco) for it in range(NT256) for dco in range(NDC)]
        pf1 = make_stream(wkb_h, "wkS", pos)
        pf1(); pf1()
        for it in range(NT256):
            t0 = it * 256
            emit_h_tile(it)
            if dbg and it == 6 and not dram.get("dbgH"):
                dram["dbgH"] = nc.dram_tensor("dbgH", [128, 2048], F32, kind="ExternalOutput").ap()
                dmas(lambda s: s.dma_start(out=dram["dbgH"], in_=bigB[:, 2048:4096]),
                     reads=[("h", i) for i in range(NDC)], writes=["dbgH"])
            for dco in range(NDC):
                slot = (it * NDC + dco) % 2
                pu = psU0 if dco % 2 == 0 else psU1
                pur = "psU%d" % (dco % 2)
                for dcx in range(NDC):
                    TT(lambda t, dcx=dcx, slot=slot, pu=pu: t.matmul(
                        pu[:, 0:256], wk_sb[:, slot * 1024 + dcx * 128: slot * 1024 + (dcx + 1) * 128],
                        HT2(dcx), start=(dcx == 0), stop=(dcx == NDC - 1)),
                       reads=[("wkS", slot), ("h", dcx)], writes=[pur])
                S(lambda s, dco=dco, t0=t0, pu=pu: s.activation(
                    bigA[:, dco * TC + t0: dco * TC + t0 + 256], pu[:, 0:256],
                    AF.Identity, bias=battn_sb[:, dco:dco + 1], scale=1.0),
                  reads=[pur, "battn"], writes=[("keys", dco)])
                pf1()
        if dbg:
            dbgK_h = nc.dram_tensor("dbgK", [128, 16384], F32, kind="ExternalOutput").ap()
            dmas(lambda s: s.dma_start(out=dbgK_h, in_=bigA[:]),
                 reads=[("keys", i) for i in range(NDC)], writes=["dbgK"])
        pf2 = make_stream(wkab_h, "wkS", pos)
        pf2(); pf2()
        for it in range(NT256):
            t0 = it * 256
            for dco in range(NDC):
                slot = (it * NDC + dco) % 2
                pd = psD0 if dco % 2 == 0 else psD1
                pdr = "psD%d" % (dco % 2)
                for dcx in range(NDC):
                    TT(lambda t, dcx=dcx, t0=t0, slot=slot, pd=pd: t.matmul(
                        pd[:, 0:256], wk_sb[:, slot * 1024 + dcx * 128: slot * 1024 + (dcx + 1) * 128],
                        bigA[:, dcx * TC + t0: dcx * TC + t0 + 256],
                        start=(dcx == 0), stop=(dcx == NDC - 1)),
                       reads=[("wkS", slot), ("keys", dcx)], writes=[pdr])
                S(lambda s, dco=dco, pd=pd: s.activation(
                    CHB[:, dco * 256:(dco + 1) * 256], pd[:, 0:256], AF.Exp,
                    bias=battn_sb[:, 40 + dco:41 + dco], scale=float(1.0 / 32.0)),
                  reads=[pdr, "bka32"], writes=["chb"])
                pf2()
            dmas(lambda s, it=it: s.dma_start(
                out=ew_h[:, :, it * 256:(it + 1) * 256].transpose([1, 0, 2]),
                in_=CHB.rearrange("p (c t) -> p c t", c=NDC)),
                reads=["chb"], writes=["ewh%d" % it])
        CHB2 = bigB[:, 8192:10240]
        for dcx in range(NDC):
            buf = CHB if dcx % 2 == 0 else CHB2
            bt = ("chb", dcx % 2)
            ldq = dmas if dcx % 2 == 0 else dmag
            ldq(lambda s, dcx=dcx, buf=buf: s.dma_start(out=buf, in_=ew_h[dcx]),
                reads=["ewh%d" % it2 for it2 in range(NT256)], writes=[bt])
            V(lambda v, buf=buf: v.tensor_tensor_scan(buf, ONES2K, buf, 0.0, ALU.mult, ALU.add),
              reads=[bt, "ones2k"], writes=[bt])
            V(lambda v, dcx=dcx, buf=buf: v.tensor_copy(tot_sb[:, dcx:dcx + 1], buf[:, 2047:2048]),
              reads=[bt], writes=["tot1"])
            dmaa(lambda s, dcx=dcx, buf=buf: s.dma_start(out=ew_h[dcx], in_=buf),
                 reads=[bt], writes=["csh%d" % dcx])
        dmaa(lambda g: g.dma_start(out=cc2i_h, in_=tot_sb[:, 0:8]),
             reads=["tot1"], writes=["cc2i"])
        p.step("gpsimd", lambda g: g.collective_compute(
            "AllGather", ALU.bypass, replica_groups=[list(range(NCORES))],
            ins=[cc2i_h.opt()], outs=[cc2o_h.opt()]),
            reads=["cc2i"], writes=["cc2o"], cc=True)
        dmaa(lambda g: g.dma_start(out=gath_sb[:].rearrange("p (r c) -> p r c", r=8),
                                   in_=cc2o_h.transpose([1, 0, 2])),
             reads=["cc2o"], writes=["gath"])
        V(lambda v: v.memset(tot_sb[:, 8:16], 0.0), writes=["carry1"])
        for r8 in range(8):
            V(lambda v, r8=r8: v.scalar_tensor_tensor(
                tot_sb[:, 32:40], gath_sb[:, r8 * 8:(r8 + 1) * 8], mexc_sb[:, r8:r8 + 1],
                tot_sb[:, 8:16], ALU.mult, ALU.add),
              reads=["gath", "mexc", "carry1"], writes=["ctmp"])
            V(lambda v: v.tensor_copy(tot_sb[:, 8:16], tot_sb[:, 32:40]),
              reads=["ctmp"], writes=["carry1"])
        RQ2 = bigB[:, 2048:3072]
        T1Q2 = bigB[:, 3072:4096]
        for dcx in range(NDC):
            buf = CHB if dcx % 2 == 0 else CHB2
            bt = ("chb", dcx % 2)
            ldq = dmas if dcx % 2 == 0 else dmag
            ldq(lambda s, dcx=dcx, buf=buf: s.dma_start(out=buf, in_=ew_h[dcx]),
                reads=["csh%d" % dcx], writes=[bt])
            V(lambda v, dcx=dcx, buf=buf: v.tensor_scalar(buf, buf, tot_sb[:, 8 + dcx:9 + dcx], None, ALU.add),
              reads=[bt, "carry1"], writes=[bt])
            for q in range(2):
                q0 = q * 1024
                V(lambda v, q0=q0, buf=buf: v.reciprocal(RQ2, buf[:, q0:q0 + 1024]),
                  reads=[bt], writes=["rq"])
                if q == 0:
                    V(lambda v, dcx=dcx: v.tensor_scalar(
                        T1Q2[:, 0:1], RQ2[:, 0:1], tot_sb[:, 8 + dcx:9 + dcx], None, ALU.mult),
                      reads=["rq", "carry1"], writes=["t1q"])
                    V(lambda v, buf=buf: v.tensor_mul(T1Q2[:, 1:1024], buf[:, 0:1023], RQ2[:, 1:1024]),
                      reads=[bt, "rq"], writes=["t1q"])
                else:
                    V(lambda v, q0=q0, buf=buf: v.tensor_mul(T1Q2, buf[:, q0 - 1:q0 + 1023], RQ2),
                      reads=[bt, "rq"], writes=["t1q"])
                V(lambda v: v.tensor_scalar(T1Q2, T1Q2, -1.0, 1.0, ALU.mult, ALU.add),
                  reads=["t1q"], writes=["t1q"])
                V(lambda v, dcx=dcx, q0=q0: v.tensor_mul(
                    bigA[:, dcx * TC + q0: dcx * TC + q0 + 1024],
                    bigA[:, dcx * TC + q0: dcx * TC + q0 + 1024], T1Q2),
                  reads=["t1q", ("keys", dcx)], writes=[("keys", dcx)])
        for dcx in range(NDC):
            V(lambda v, dcx=dcx: v.tensor_tensor_scan(
                bigA[:, dcx * TC: dcx * TC + TC], ONES2K, bigA[:, dcx * TC: dcx * TC + TC],
                0.0, ALU.mult, ALU.add),
              reads=[("keys", dcx), "ones2k"], writes=[("keys", dcx)])
            V(lambda v, dcx=dcx: v.tensor_copy(tot_sb[:, 16 + dcx:17 + dcx],
                                               bigA[:, dcx * TC + TC - 1: dcx * TC + TC]),
              reads=[("keys", dcx)], writes=["tot2"])
        dmaa(lambda g: g.dma_start(out=cc3i_h, in_=tot_sb[:, 16:24]),
             reads=["tot2"], writes=["cc3i"])
        p.step("gpsimd", lambda g: g.collective_compute(
            "AllGather", ALU.bypass, replica_groups=[list(range(NCORES))],
            ins=[cc3i_h.opt()], outs=[cc3o_h.opt()]),
            reads=["cc3i"], writes=["cc3o"], cc=True)
        dmaa(lambda g: g.dma_start(out=gath_sb[:].rearrange("p (r c) -> p r c", r=8),
                                   in_=cc3o_h.transpose([1, 0, 2])),
             reads=["cc3o"], writes=["gath"])
        V(lambda v: v.memset(tot_sb[:, 24:32], 0.0), writes=["carry2"])
        for r8 in range(8):
            V(lambda v, r8=r8: v.scalar_tensor_tensor(
                tot_sb[:, 32:40], gath_sb[:, r8 * 8:(r8 + 1) * 8], mexc_sb[:, r8:r8 + 1],
                tot_sb[:, 24:32], ALU.mult, ALU.add),
              reads=["gath", "mexc", "carry2"], writes=["ctmp"])
            V(lambda v: v.tensor_copy(tot_sb[:, 24:32], tot_sb[:, 32:40]),
              reads=["ctmp"], writes=["carry2"])
        for dcx in range(NDC):
            V(lambda v, dcx=dcx: v.tensor_scalar(
                bigA[:, dcx * TC: dcx * TC + TC], bigA[:, dcx * TC: dcx * TC + TC],
                tot_sb[:, 24 + dcx:25 + dcx], None, ALU.add),
              reads=[("keys", dcx), "carry2"], writes=[("keys", dcx)])
        pf3 = make_stream(wvb_h, "wkS", pos)
        pf3(); pf3()
        for it in range(NT256):
            t0 = it * 256
            emit_h_tile(it)
            if dbg and it == 6 and not dram.get("dbgH"):
                dram["dbgH"] = nc.dram_tensor("dbgH", [128, 2048], F32, kind="ExternalOutput").ap()
                dmas(lambda s: s.dma_start(out=dram["dbgH"], in_=bigB[:, 2048:4096]),
                     reads=[("h", i) for i in range(NDC)], writes=["dbgH"])
            for dco in range(NDC):
                slot = (it * NDC + dco) % 2
                pu = psU0 if dco % 2 == 0 else psU1
                pur = "psU%d" % (dco % 2)
                for dcx in range(NDC):
                    TT(lambda t, dcx=dcx, slot=slot, pu=pu: t.matmul(
                        pu[:, 0:256], wk_sb[:, slot * 1024 + dcx * 128: slot * 1024 + (dcx + 1) * 128],
                        HT2(dcx), start=(dcx == 0), stop=(dcx == NDC - 1)),
                       reads=[("wkS", slot), ("h", dcx)], writes=[pur])
                S(lambda s, dco=dco, pu=pu: s.activation(
                    VT4[:, 0:256], pu[:, 0:256], AF.Identity,
                    bias=battn_sb[:, 16 + dco:17 + dco], scale=1.0),
                  reads=[pur, "battn"], writes=["vtmp"])
                V(lambda v, dco=dco, t0=t0: v.tensor_mul(
                    VT4[:, 0:256], VT4[:, 0:256],
                    bigA[:, dco * TC + t0: dco * TC + t0 + 256]),
                  reads=["vtmp", ("keys", dco)], writes=["vtmp"])
                V(lambda v, dco=dco, t0=t0: v.tensor_add(
                    xc(dco, t0, 256), xc(dco, t0, 256), VT4[:, 0:256]),
                  reads=["vtmp", ("x", dco)], writes=[("x", dco)])
                pf3()

    if "P5" in phases:
        emit_ffn(1, do_split=("P3" not in phases), fence=p.fence(["ct", "st", "t2", "t3", "ctm", "stm", "rowk", "rowi", "strip0",
                "lnrow", "xh", "dsc", "dly", "dlx", "d5", "dsxy", "dsx", "dsy",
                "dsxx", "dnum", "dden", "dt", "dq", "dD", "dom", "da", "db",
                "dcnt", "fx1"] + [("x1", i) for i in range(NDC)]
            + [("keys", i) for i in range(NDC)]))

    dmas(lambda s: s.dma_start(
        out=yT_h.rearrange("(c p) t -> c p t", p=128).transpose([1, 0, 2]),
        in_=x_sb[:].rearrange("p (c t) -> p c t", c=NDC)),
        reads=[("x", dcx) for dcx in range(NDC)], writes=["yout"])
    if dbg:
        dbgA_h = nc.dram_tensor("dbgA", [128, 16384], F32, kind="ExternalOutput").ap()
        dbgB_h = nc.dram_tensor("dbgB", [128, 13312], F32, kind="ExternalOutput").ap()
        dbgr_h = nc.dram_tensor("dbgr", [1, 64], F32, kind="ExternalOutput").ap()
        dbgc_h = nc.dram_tensor("dbgc", [128, 48], F32, kind="ExternalOutput").ap()
        allreg = list(p.region_w.keys())
        dmas(lambda s: s.dma_start(out=dbgA_h, in_=bigA[:]), reads=allreg, writes=["dbg1"])
        dmas(lambda s: s.dma_start(out=dbgB_h, in_=bigB[:]), reads=[], writes=["dbg2"])
        dmas(lambda s: s.dma_start(out=dbgr_h, in_=g1_sb[:]), reads=[], writes=["dbg3"])
        dmas(lambda s: s.dma_start(out=dbgc_h, in_=bias_sb[:]), reads=[], writes=["dbg4"])
        dbgt_h = nc.dram_tensor("dbgt", [128, 40], F32, kind="ExternalOutput").ap()
        dmas(lambda s: s.dma_start(out=dbgt_h, in_=tot_sb[:]), reads=[], writes=["dbg5"])

    p.emit(nc, block, sems)
    es.close()
    return nc


def _prep_inputs(inputs):
    f32 = np.float32
    x = np.asarray(inputs["L0_attn_a_x"], f32).reshape(T, D)
    Wel = np.asarray(inputs["L0_attn_a_W_el"], f32)
    bel = np.asarray(inputs["L0_attn_a_b_el"], f32).reshape(24, 1)
    Wle = np.asarray(inputs["L0_attn_a_W_le"], f32)
    ble = np.ascontiguousarray(np.asarray(inputs["L0_attn_a_b_le"], f32).reshape(8, 128).T)
    A = np.array([[1, 1, 1, 0, 1, 1, 1, 0, 0, 0, 1, 1], [1, 1, 0, 1, 1, 1, 0, 0, 0, 1, 1, 0],
                  [1, 0, 1, 1, 1, 0, 0, 0, 1, 1, 0, 1], [0, 1, 1, 1, 0, 0, 0, 1, 1, 0, 1, 1],
                  [0, 0, 0, 0, 1, 1, 1, 1, 1, 1, 1, 1], [1, 1, 1, 1, 0, 0, 0, 0, 0, 0, 0, 0],
                  [0, 0, 0, 0, 1, 1, 1, 1, 0, 0, 0, 0], [1, 1, 0, 0, 1, 1, 0, 0, 1, 1, 0, 0],
                  [1, 0, 1, 0, 1, 0, 1, 0, 1, 0, 1, 0], [0, 1, 0, 1, 0, 1, 0, 1, 0, 1, 0, 1],
                  [1, 0, 0, 1, 1, 0, 0, 1, 1, 0, 0, 1], [0, 1, 1, 0, 0, 1, 1, 0, 0, 1, 1, 0]],
                 dtype=f32)
    H = np.concatenate([A.T, np.eye(12, dtype=f32)], axis=1)
    HT = np.ascontiguousarray(H.T)

    def colblocks(W):
        W4 = W.reshape(8, 128, 8, 128)
        return np.ascontiguousarray(W4.transpose(0, 3, 2, 1))

    wkb = colblocks(np.asarray(inputs["L0_attn_b_Wk"], f32)).reshape(8, 128, 1024)
    wkab = colblocks(np.asarray(inputs["L0_attn_b_Wka"], f32)).reshape(8, 128, 1024)
    wvb = colblocks(np.asarray(inputs["L0_attn_b_Wv"], f32)).reshape(8, 128, 1024)

    def ppcols(v):
        return np.ascontiguousarray(np.asarray(v, f32).reshape(8, 128).T)

    battn = np.concatenate([ppcols(inputs["L0_attn_b_bk"]), ppcols(inputs["L0_attn_b_bka"]),
                            ppcols(inputs["L0_attn_b_bv"]), ppcols(inputs["L0_attn_b_gamma"]),
                            ppcols(inputs["L0_attn_b_beta"])], axis=1)
    wel_blk = np.ascontiguousarray(Wel.reshape(8, 128, 24))
    rwb = np.ascontiguousarray(np.asarray(inputs["L0_ffn_router_w"], f32).T.reshape(8, 128, 8))
    b1 = np.asarray(inputs["L0_ffn_b1"], f32)
    b1v = np.ascontiguousarray(b1.reshape(8, 32, 128).transpose(2, 0, 1).reshape(128, 256))
    b2l = np.ascontiguousarray(np.asarray(inputs["L0_ffn_b2"], f32))
    w1 = np.asarray(inputs["L0_ffn_w1"], f32)
    w2 = np.asarray(inputs["L0_ffn_w2"], f32)
    w1b = np.ascontiguousarray(
        w1.reshape(8, 32, 128, 8, 128).transpose(0, 1, 4, 3, 2).reshape(8, 32, 128, 1024))
    w2b = np.ascontiguousarray(
        w2.reshape(8, 8, 128, 32, 128).transpose(0, 3, 4, 1, 2).reshape(8, 32, 128, 1024))
    sel = np.zeros((8, 8 * 128), f32)
    for e in range(8):
        sel[e, e * 128:(e + 1) * 128] = 1.0
    id128 = np.eye(128, dtype=f32)
    lxc = np.array([[2, 4, 8, 16, 32, 1, 1, 1]], dtype=f32)
    xhead = np.ascontiguousarray(x[0:1, :])

    shared = dict(wel=wel_blk, bel=bel, wle=Wle, ble=ble, htm=HT,
                  wkb=wkb, wkab=wkab, wvb=wvb, battn=battn, rwb=rwb,
                  b1v=b1v, b2l=b2l, sel=sel, id128=id128,
                  w1b=w1b, w2b=w2b, lxc=lxc, xhead=xhead)
    in_maps = []
    for j in range(NCORES):
        sh = x[j * TC:(j + 1) * TC, :]
        m = dict(shared)
        m["xT"] = np.ascontiguousarray(sh.T)
        m["mexc"] = np.broadcast_to(
            (np.arange(8) < j).astype(f32), (128, 8)).copy()
        in_maps.append(m)
    return in_maps


def kernel(**inputs):
    phases = inputs.pop("_phases", ALL_PHASES)
    dbg = inputs.pop("_dbg", False)
    in_maps = _prep_inputs(inputs)
    nc = build_nc(phases=phases, dbg=dbg)
    res = run_bass_kernel_spmd(nc, in_maps, core_ids=list(range(NCORES)))
    out = np.empty((T, D), np.float32)
    for j in range(NCORES):
        out[j * TC:(j + 1) * TC, :] = res.results[j]["yT"].T
    out = out.reshape(1, T, D)
    if dbg:
        return out, res
    return out
